# version 01
# baseline (speedup 1.0000x reference)
import math
from contextlib import ExitStack

import numpy as np
import concourse.bass as bass
import concourse.mybir as mybir
from concourse.bass_utils import run_bass_kernel_spmd

F32 = mybir.dt.float32
BF16 = mybir.dt.bfloat16
AF = mybir.ActivationFunctionType
ALU = mybir.AluOpType
AX = mybir.AxisListType

NCORE = 8
T = 1024
DM = 2048
KC = DM // 128
LN_EPS = 1e-5


def _tile_w(w, tile_cols=128):
    K, M = w.shape
    Kc, Mc = K // 128, M // tile_cols
    return np.ascontiguousarray(w.reshape(Kc, 128, Mc, tile_cols).transpose(2, 1, 0, 3))


def _feat_major(x):
    return np.ascontiguousarray(x.T)


class Sem:
    def __init__(self, ctx, nc, name):
        self.h = ctx.enter_context(nc.semaphore(name))
        self.n = 0

    def inc(self, k=1):
        self.n += k
        return self.n



def emit_ln(nc, st, xT, xcast, nch, tok, post_bf16=True):
    ones_b = st["ones128_bf16"]
    ones_f = st["ones1_f32"]
    rows = st["lnrows"]
    sq = st["lnsq"]
    ab = st["lnab"]
    pool = st["psum_pool"]
    ps = pool[0:1, 0:2 * tok]
    pab = pool[:, 2 * tok:4 * tok].rearrange("p (a b) -> p a b", a=2)
    s_sq, s_fr = st["s_ln_sq"], st["s_ln_fr"]
    nsl = tok // 512
    inv_n = 1.0 / (nch * 128)

    with nc.Block() as blk:
        @blk.vector
        def _(vector):
            for k in range(nch):
                vector.tensor_copy(xcast[:, k, :], xT[:, k, :])

    with nc.Block() as blk:
        @blk.scalar
        def _(scalar):
            for k in range(nch):
                if k >= 2:
                    scalar.wait_ge(s_fr.h, s_fr.n + (k - 1))
                scalar.activation(sq[:, k % 2, :], xcast[:, k, :], AF.Square)
                _flush(scalar, st, "scalar").then_inc(s_sq.h, 1)

        @blk.tensor
        def _(tensor):
            for k in range(nch):
                tensor.wait_ge(s_sq.h, s_sq.n + (k + 1))
                for n in range(nsl):
                    tensor.matmul(ps[0:1, n * 512:(n + 1) * 512],
                                  ones_b[:, 0:1], xcast[:, k, n * 512:(n + 1) * 512],
                                  start=(k == 0), stop=(k == nch - 1))
                for n in range(nsl):
                    mmi = tensor.matmul(ps[0:1, tok + n * 512:tok + (n + 1) * 512],
                                        ones_b[:, 0:1], sq[:, k % 2, n * 512:(n + 1) * 512],
                                        start=(k == 0), stop=(k == nch - 1))
                mmi.then_inc(s_fr.h, 1)
        s_sq.inc(nch)
        s_fr.inc(nch)

    m_v = rows[0:1, 0:tok]
    r_v = rows[0:1, tok:2 * tok]
    t_v = rows[0:1, 2 * tok:3 * tok]
    with nc.Block() as blk:
        @blk.vector
        def _(vector):
            vector.tensor_scalar_mul(m_v, ps[0:1, 0:tok], inv_n)
            vector.tensor_scalar_mul(r_v, ps[0:1, tok:2 * tok], inv_n)
            vector.tensor_mul(t_v, m_v, m_v)
            vector.tensor_sub(r_v, r_v, t_v)
            vector.tensor_scalar_add(r_v, r_v, LN_EPS)
    with nc.Block() as blk:
        @blk.scalar
        def _(scalar):
            scalar.activation(t_v, r_v, AF.Sqrt)
    with nc.Block() as blk:
        @blk.vector
        def _(vector):
            vector.reciprocal(r_v, t_v)
            vector.tensor_mul(t_v, m_v, r_v)

    with nc.Block() as blk:
        @blk.tensor
        def _(tensor):
            for n in range(nsl):
                tensor.matmul(pab[:, 0, n * 512:(n + 1) * 512], ones_f[0:1, :],
                              r_v[0:1, n * 512:(n + 1) * 512], start=True, stop=True)
            for n in range(nsl):
                tensor.matmul(pab[:, 1, n * 512:(n + 1) * 512], ones_f[0:1, :],
                              t_v[0:1, n * 512:(n + 1) * 512], start=True, stop=True)


    with nc.Block() as blk:
        @blk.scalar
        def _(scalar):
            scalar.activation(ab[:, 0, :], pab[:, 0, :], AF.Copy)
            scalar.activation(ab[:, 1, :], pab[:, 1, :], AF.Copy)

    with nc.Block() as blk:
        half = nch // 2

        @blk.vector
        def _(vector):
            for k in range(half):
                vector.tensor_mul(xT[:, k, :], xT[:, k, :], ab[:, 0, :])
                vector.tensor_sub(xT[:, k, :], xT[:, k, :], ab[:, 1, :])
                if post_bf16:
                    vector.tensor_copy(xcast[:, k, :], xT[:, k, :])

        @blk.gpsimd
        def _(gp):
            for k in range(half, nch):
                gp.tensor_mul(xT[:, k, :], xT[:, k, :], ab[:, 0, :])
                gp.tensor_sub(xT[:, k, :], xT[:, k, :], ab[:, 1, :])
                if post_bf16:
                    gp.tensor_copy(xcast[:, k, :], xT[:, k, :])


def alloc_ln_state(nc, ctx, st, tok):
    e = ctx.enter_context
    st["lnrows"] = e(nc.sbuf_tensor("lnrows", [1, 3 * tok], F32))
    st["lnsq"] = e(nc.sbuf_tensor("lnsq", [128, 2, tok], BF16))
    st["lnab"] = e(nc.sbuf_tensor("lnab", [128, 2, tok], F32))
    st["s_ln_sq"] = Sem(ctx, nc, "s_ln_sq")
    st["s_ln_fr"] = Sem(ctx, nc, "s_ln_fr")


def alloc_consts(nc, ctx, st):
    e = ctx.enter_context
    st["ones128_bf16"] = e(nc.sbuf_tensor("ones128b", [128, 1], BF16))
    st["ones1_f32"] = e(nc.sbuf_tensor("ones1f", [1, 128], F32))
    st["junk"] = e(nc.sbuf_tensor("junk", [1, 8], F32))


_FLUSH_SLOT = {"scalar": 0, "vector": 1, "gpsimd": 2}


def _flush(E, st, eng):
    sl = _FLUSH_SLOT[eng]
    src = st["ones1_f32"][0:1, 0:1]
    dst = st["junk"][0:1, sl:sl + 1]
    if eng == "scalar":
        return E.copy(dst, src)
    return E.tensor_copy(dst, src)


def emit_consts_init(nc, st):
    with nc.Block() as blk:
        @blk.vector
        def _(vector):
            vector.memset(st["ones128_bf16"][:, :], 1.0)
            vector.memset(st["ones1_f32"][:, :], 1.0)



def build_tail():
    nc = bass.Bass("TRN2", target_bir_lowering=False)
    x3_d = nc.dram_tensor("x3t", [DM, T], F32, kind="ExternalInput")
    fac_d = nc.dram_tensor("facs", [KC, 128 * KC * 8 * 128], F32, kind="ExternalInput")
    bias_d = nc.dram_tensor("biasc", [128, KC], F32, kind="ExternalInput")
    out_d = nc.dram_tensor("out", [DM, T], F32, kind="ExternalOutput")
    isq8 = 1.0 / math.sqrt(8.0)

    with ExitStack() as ctx:
        e = ctx.enter_context
        st = {}
        xT = e(nc.sbuf_tensor("xT", [128, KC, T], F32))
        xb = e(nc.sbuf_tensor("xb", [128, KC, T], BF16))
        fst = e(nc.sbuf_tensor("fst", [128, KC, 8, 128], F32))
        wsum = e(nc.sbuf_tensor("wsum", [128, KC, 128], F32))
        wb16 = e(nc.sbuf_tensor("wb16", [128, KC, 128], BF16))
        bias_s = e(nc.sbuf_tensor("bias_s", [128, KC], F32))
        ev = e(nc.sbuf_tensor("ev", [128, 2, 512], F32))
        psum_pool = e(nc.psum_tensor("psum_pool", [128, 4096], F32))
        st["psum_pool"] = psum_pool
        pm = psum_pool[:, 0:1024].rearrange("p (a b) -> p a b", a=2)
        alloc_consts(nc, ctx, st)
        alloc_ln_state(nc, ctx, st, T)
        s_ld = Sem(ctx, nc, "s_ld")
        s_fs = Sem(ctx, nc, "s_fs")
        s_w = Sem(ctx, nc, "s_w")
        s_mm = Sem(ctx, nc, "s_mm")
        s_ev = Sem(ctx, nc, "s_ev")
        s_res = Sem(ctx, nc, "s_res")
        s_st = Sem(ctx, nc, "s_st")

        emit_consts_init(nc, st)

        with nc.Block() as blk:
            @blk.sync
            def _(sync):
                sync.dma_start(xT[:, :, :],
                               x3_d[:, :].rearrange("(k p) t -> p k t", p=128)
                               ).then_inc(s_ld.h, 16)
                sync.dma_start(bias_s[:, :], bias_d[:, :]).then_inc(s_ld.h, 16)
                sync.wait_ge(s_ld.h, s_ld.inc(32))

        with nc.Block() as blk:
            @blk.vector
            def _(vector):
                for k in range(KC):
                    vector.tensor_copy(xb[:, k, :], xT[:, k, :])

        with nc.Block() as blk:
            @blk.sync
            def _(sync):
                for m in range(KC):
                    if m >= 1:
                        sync.wait_ge(s_fs.h, m)
                    sync.dma_start(fst[:, :, :, :],
                                   fac_d[m, :].rearrange("(p k r c) -> p k r c",
                                                         p=128, k=KC, r=8)
                                   ).then_inc(s_ld.h, 16)
            s_ld.inc(16 * KC)

            @blk.vector
            def _(vector):
                for m in range(KC):
                    vector.wait_ge(s_ld.h, 32 + 16 * (m + 1))
                    if m >= 1:
                        vector.wait_ge(s_mm.h, 2 * m)
                    vector.tensor_add(wsum[:, :, :], fst[:, :, 0, :], fst[:, :, 1, :])
                    for r in range(2, 8):
                        vector.tensor_add(wsum[:, :, :], wsum[:, :, :], fst[:, :, r, :])
                    vector.tensor_copy(wb16[:, :, :], wsum[:, :, :])
                    _flush(vector, st, "vector").then_inc(s_fs.h, 1)
                    _flush(vector, st, "vector").then_inc(s_w.h, 1)
            s_fs.inc(KC)
            s_w.inc(KC)

            @blk.tensor
            def _(tensor):
                for m in range(KC):
                    tensor.wait_ge(s_w.h, m + 1)
                    if m >= 1:
                        tensor.wait_ge(s_ev.h, 2 * m)
                    for n in range(2):
                        for k in range(KC):
                            mmi = tensor.matmul(pm[:, n, :], wb16[:, k, :],
                                                xb[:, k, n * 512:(n + 1) * 512],
                                                start=(k == 0), stop=(k == KC - 1))
                        mmi.then_inc(s_mm.h, 1)
            s_mm.inc(2 * KC)

            @blk.scalar
            def _(scalar):
                for m in range(KC):
                    if m >= 1:
                        scalar.wait_ge(s_res.h, m)
                    for n in range(2):
                        scalar.wait_ge(s_mm.h, 2 * m + n + 1)
                        scalar.activation(ev[:, n, :], pm[:, n, :], AF.Identity,
                                          bias=bias_s[:, m:m + 1], scale=isq8)
                    _flush(scalar, st, "scalar").then_inc(s_ev.h, 2)
            s_ev.inc(2 * KC)

            @blk.gpsimd
            def _(gp):
                for m in range(KC):
                    gp.wait_ge(s_ev.h, 2 * m + 2)
                    gp.tensor_add(xT[:, m, :], xT[:, m, :],
                                  ev[:, :, :].rearrange("p a b -> p (a b)"))
                    _flush(gp, st, "gpsimd").then_inc(s_res.h, 1)
            s_res.inc(KC)

        emit_ln(nc, st, xT, xb, KC, T, post_bf16=False)

        with nc.Block() as blk:
            @blk.sync
            def _(sync):
                sync.dma_start(out_d[:, :].rearrange("(k p) t -> p k t", p=128),
                               xT[:, :, :]).then_inc(s_st.h, 16)
                sync.wait_ge(s_st.h, s_st.inc(16))

    return nc


def host_tail_inputs(x3_full, factors, bias):
    fac = np.ascontiguousarray(
        factors.reshape(8, KC, 128, KC, 128).transpose(3, 2, 1, 0, 4)
    ).reshape(KC, -1)
    biasc = np.ascontiguousarray(bias.reshape(KC, 128).T)
    maps = []
    for c in range(NCORE):
        b, h = c // 2, c % 2
        x3t = _feat_major(x3_full[b, h * T:(h + 1) * T, :])
        maps.append(dict(x3t=x3t, facs=fac, biasc=biasc))
    return maps


def gather_tail(results):
    out = np.empty((4, 2048, DM), np.float32)
    for c in range(NCORE):
        b, h = c // 2, c % 2
        out[b, h * T:(h + 1) * T, :] = results[c]["out"].T
    return out



class Plan:
    ENGINES = ("sync", "tensor", "vector", "scalar", "gpsimd")

    def __init__(self):
        self.q = {e: [] for e in self.ENGINES}

    def add(self, eng, fn):
        self.q[eng].append(fn)

    def emit(self, nc):
        with nc.Block() as blk:
            for eng in self.ENGINES:
                fns = self.q[eng]
                if fns:
                    getattr(blk, eng)(lambda E, fns=fns: [f(E) for f in fns])



MOE_F = 1408
MOE_FC = MOE_F // 128
NEXP = 10


def build_moe(nexp=NEXP, do_ln=True, dbg=False):
    nc = bass.Bass("TRN2", target_bir_lowering=False)
    x1_d = nc.dram_tensor("x1t", [DM, T], F32, kind="ExternalInput")
    gwt_d = nc.dram_tensor("gwt", [128, KC, 8], F32, kind="ExternalInput")
    gates_d = nc.dram_tensor("gates", [NEXP * MOE_FC, 128 * KC * 128], F32, kind="ExternalInput")
    ups_d = nc.dram_tensor("ups", [NEXP * MOE_FC, 128 * KC * 128], F32, kind="ExternalInput")
    downs_d = nc.dram_tensor("downs", [NEXP * KC, 128 * MOE_FC * 128], F32, kind="ExternalInput")
    idn_d = nc.dram_tensor("idn", [128, 128], F32, kind="ExternalInput")
    out_d = nc.dram_tensor("out", [DM, T], F32, kind="ExternalOutput")
    scr_d = nc.dram_tensor("scr", [NEXP, T], F32, kind="Internal")

    with ExitStack() as ctx:
        e = ctx.enter_context
        st = {}
        xT = e(nc.sbuf_tensor("xT", [128, KC, T], F32))
        xb = e(nc.sbuf_tensor("xb", [128, KC, T], BF16))
        wstg = e(nc.sbuf_tensor("wstg", [128, 2, KC * 128], F32))
        wb = e(nc.sbuf_tensor("wb", [128, 2, KC * 128], BF16))
        sg = e(nc.sbuf_tensor("sg", [128, MOE_FC, T], BF16))
        uc = e(nc.sbuf_tensor("uc", [128, 2, T], BF16))
        cbc = e(nc.sbuf_tensor("cbc", [128, NEXP, T], BF16))
        idn = e(nc.sbuf_tensor("idn_s", [128, 128], F32))
        gw_s = e(nc.sbuf_tensor("gw_s", [128, KC, 8], F32))
        gw_b = e(nc.sbuf_tensor("gw_b", [128, KC, 8], BF16))
        srow = e(nc.sbuf_tensor("srow", [8, T], F32))
        combT = e(nc.sbuf_tensor("combT", [8, T], F32))
        crow1 = e(nc.sbuf_tensor("crow1", [1, T], F32))
        stok = e(nc.sbuf_tensor("stok", [128, 8, 8], F32))
        ctok = e(nc.sbuf_tensor("ctok", [128, 8, 8], F32))
        rtree = e(nc.sbuf_tensor("rtree", [128, 64], F32))
        psum_pool = e(nc.psum_tensor("psum_pool", [128, 4096], F32))
        st["psum_pool"] = psum_pool
        alloc_consts(nc, ctx, st)
        alloc_ln_state(nc, ctx, st, T)

        s_ld = Sem(ctx, nc, "s_ld")
        s_cast = Sem(ctx, nc, "s_cast")
        s_mmg = Sem(ctx, nc, "s_mmg")
        s_mmu = Sem(ctx, nc, "s_mmu")
        s_mmd = Sem(ctx, nc, "s_mmd")
        s_evg = Sem(ctx, nc, "s_evg")
        s_evu = Sem(ctx, nc, "s_evu")
        s_h = Sem(ctx, nc, "s_h")
        s_acc = Sem(ctx, nc, "s_acc")
        s_r = Sem(ctx, nc, "s_r")
        s_st = Sem(ctx, nc, "s_st")

        pg = psum_pool[:, 0:1024].rearrange("p (a b) -> p a b", a=2)
        pu = psum_pool[:, 1024:2048].rearrange("p (a b) -> p a b", a=2)
        pd = psum_pool[:, 2048:3072].rearrange("p (a b) -> p a b", a=2)

        emit_consts_init(nc, st)

        with nc.Block() as blk:
            @blk.sync
            def _(sync):
                sync.dma_start(xT[:, :, :],
                               x1_d[:, :].rearrange("(k p) t -> p k t", p=128)
                               ).then_inc(s_ld.h, 16)
                sync.dma_start(gw_s[:, :, :], gwt_d[:, :, :]).then_inc(s_ld.h, 16)
                sync.dma_start(idn[:, :], idn_d[:, :]).then_inc(s_ld.h, 16)
                sync.wait_ge(s_ld.h, s_ld.inc(48))

        with nc.Block() as blk:
            @blk.vector
            def _(vector):
                for k in range(KC):
                    vector.tensor_copy(xb[:, k, :], xT[:, k, :])
                vector.tensor_copy(gw_b[:, :, :], gw_s[:, :, :])

        psc = psum_pool[0:8, 1024:2048]
        with nc.Block() as blk:
            @blk.tensor
            def _(tensor):
                for n in range(2):
                    for k in range(KC):
                        mmi = tensor.matmul(psc[:, n * 512:(n + 1) * 512],
                                            gw_b[:, k, :], xb[:, k, n * 512:(n + 1) * 512],
                                            start=(k == 0), stop=(k == KC - 1))
                mmi.then_inc(s_r.h, 1)
            s_r.inc(1)

            @blk.scalar
            def _(scalar):
                scalar.wait_ge(s_r.h, 1)
                scalar.activation(srow[:, :], psc[:, :], AF.Identity)

        ptok = psum_pool[:, 0:64].rearrange("p (a b) -> p a b", a=8)
        with nc.Block() as blk:
            @blk.tensor
            def _(tensor):
                for j in range(8):
                    tensor.transpose(ptok[:, j, :], srow[:, j * 128:(j + 1) * 128],
                                     idn[0:8, 0:8])
            @blk.scalar
            def _(scalar):
                pass

        with nc.Block() as blk:
            @blk.scalar
            def _(scalar):
                for j in range(8):
                    scalar.activation(stok[:, j, :], ptok[:, j, :], AF.Identity)

        def vstep(fn):
            with nc.Block() as blk:
                blk.vector(fn)

        with nc.Block() as blk:
            @blk.scalar
            def _(scalar):
                for j in range(8):
                    scalar.activation(ctok[:, j, :], stok[:, j, :], AF.Exp)

        t1 = rtree[:, 0:32].rearrange("p (a b) -> p a b", a=8)
        t2 = rtree[:, 32:48].rearrange("p (a b) -> p a b", a=8)
        sums = rtree[:, 48:56]
        rs = rtree[:, 56:64]
        cs = ctok[:, :, :]
        ss = stok[:, :, :]
        vstep(lambda v: v.tensor_add(t1[:, :, :], cs[:, :, 0:4], cs[:, :, 4:8]))
        vstep(lambda v: v.tensor_add(t2[:, :, :], t1[:, :, 0:2], t1[:, :, 2:4]))
        vstep(lambda v: v.tensor_add(sums, t2[:, :, 0], t2[:, :, 1]))
        vstep(lambda v: v.reciprocal(rs, sums))
        vstep(lambda v: [v.tensor_scalar(cs[:, j, :], cs[:, j, :], rs[:, j:j + 1],
                                         None, ALU.mult) for j in range(8)])
        vstep(lambda v: v.tensor_max(t1[:, :, :], cs[:, :, 0:4], cs[:, :, 4:8]))
        vstep(lambda v: v.tensor_max(t2[:, :, :], t1[:, :, 0:2], t1[:, :, 2:4]))
        vstep(lambda v: v.tensor_max(sums, t2[:, :, 0], t2[:, :, 1]))
        vstep(lambda v: [v.tensor_scalar(ss[:, j, :], cs[:, j, :], sums[:, j:j + 1],
                                         None, ALU.is_lt) for j in range(8)])
        vstep(lambda v: v.tensor_mul(ss, ss, cs))
        vstep(lambda v: v.tensor_max(t1[:, :, :], ss[:, :, 0:4], ss[:, :, 4:8]))
        vstep(lambda v: v.tensor_max(t2[:, :, :], t1[:, :, 0:2], t1[:, :, 2:4]))
        vstep(lambda v: v.tensor_max(sums, t2[:, :, 0], t2[:, :, 1]))
        vstep(lambda v: [v.tensor_scalar(ss[:, j, :], cs[:, j, :], sums[:, j:j + 1],
                                         None, ALU.is_ge) for j in range(8)])
        vstep(lambda v: v.tensor_mul(cs, cs, ss))

        pct = psum_pool[0:8, 2048:3072].rearrange("p (a b) -> p a b", a=8)
        with nc.Block() as blk:
            @blk.tensor
            def _(tensor):
                for j in range(8):
                    tensor.transpose(pct[:, j, :], ctok[:, j, :], idn[:, :])
        with nc.Block() as blk:
            @blk.scalar
            def _(scalar):
                for j in range(8):
                    scalar.activation(combT[:, j * 128:(j + 1) * 128], pct[:, j, :],
                                      AF.Identity)
        with nc.Block() as blk:
            @blk.sync
            def _(sync):
                sync.dma_start(scr_d[0:8, :], combT[:, :]).then_inc(s_ld.h, 16)
                sync.wait_ge(s_ld.h, s_ld.inc(16))
            @blk.vector
            def _(vector):
                vector.memset(crow1[:, :], 1.0)
        with nc.Block() as blk:
            @blk.sync
            def _(sync):
                sync.dma_start(scr_d[8:9, :], crow1[:, :]).then_inc(s_ld.h, 16)
                sync.dma_start(scr_d[9:10, :], crow1[:, :]).then_inc(s_ld.h, 16)
                sync.wait_ge(s_ld.h, s_ld.inc(32))

        pcb = psum_pool[:, 0:1024]
        s_cbm = Sem(ctx, nc, "s_cbm")
        s_cbe = Sem(ctx, nc, "s_cbe")
        with nc.Block() as blk:
            @blk.sync
            def _(sync):
                for ex in range(NEXP):
                    if ex >= 1:
                        sync.wait_ge(s_cbm.h, 2 * ex)
                    sync.dma_start(crow1[:, :], scr_d[ex:ex + 1, :]).then_inc(s_ld.h, 16)
            base_ld = s_ld.n
            s_ld.inc(16 * NEXP)

            @blk.tensor
            def _(tensor):
                for ex in range(NEXP):
                    tensor.wait_ge(s_ld.h, base_ld + 16 * (ex + 1))
                    if ex >= 1:
                        tensor.wait_ge(s_cbe.h, ex)
                    for n in range(2):
                        tensor.matmul(pcb[:, n * 512:(n + 1) * 512], st["ones1_f32"][0:1, :],
                                      crow1[0:1, n * 512:(n + 1) * 512],
                                      start=True, stop=True).then_inc(s_cbm.h, 1)
            s_cbm.inc(2 * NEXP)

            @blk.scalar
            def _(scalar):
                for ex in range(NEXP):
                    scalar.wait_ge(s_cbm.h, 2 * (ex + 1))
                    scalar.activation(cbc[:, ex, :], pcb[:, :], AF.Identity)
                    _flush(scalar, st, "scalar").then_inc(s_cbe.h, 1)
            s_cbe.inc(NEXP)

        plan = Plan()
        stamp = {}
        sctr = [0]
        gctr = [0]
        uctr = [0]
        dctr = [0]
        ld_base = s_ld.n

        def emit_strip(dram, row, ncols_k, fam_sem, fam_ctr, psum_view, mm_rhs_fn,
                       evac_fn, evac_eng, evac_sem, pre_mm_waits=(), pre_evac_waits=()):
            s_i = sctr[0]; sctr[0] += 1
            fs = fam_ctr[0]; fam_ctr[0] += 1
            slot = s_i % 2
            ncol = ncols_k * 128

            def dma(sync, s_i=s_i, slot=slot, row=row, ncol=ncol, dram=dram,
                    ncols_k=ncols_k):
                if s_i >= 2:
                    sync.wait_ge(s_cast.h, s_i - 1)
                sync.dma_start(wstg[:, slot, 0:ncol],
                               dram[row, :].rearrange("(p k c) -> p (k c)", p=128,
                                                      k=ncols_k)
                               ).then_inc(s_ld.h, 16)
            plan.add("sync", dma)
            s_ld.inc(16)

            def cast(gp, s_i=s_i, slot=slot, ncol=ncol):
                gp.wait_ge(s_ld.h, ld_base + 16 * (s_i + 1))
                if s_i >= 2:
                    sem, cnt = stamp[s_i - 2]
                    gp.wait_ge(sem.h, cnt)
                gp.tensor_copy(wb[:, slot, 0:ncol], wstg[:, slot, 0:ncol])
                _flush(gp, st, "gpsimd").then_inc(s_cast.h, 1)
            plan.add("gpsimd", cast)
            s_cast.inc(1)

            def mms(tensor, s_i=s_i, slot=slot, ncols_k=ncols_k,
                    waits=tuple(pre_mm_waits)):
                tensor.wait_ge(s_cast.h, s_i + 1)
                for sem, cnt in waits:
                    if cnt > 0:
                        tensor.wait_ge(sem.h, cnt)
                wv = wb[:, slot, 0:ncols_k * 128].rearrange("p (k c) -> p k c",
                                                            k=ncols_k)
                for n in range(2):
                    for k in range(ncols_k):
                        mmi = tensor.matmul(psum_view[:, n, :], wv[:, k, :],
                                            mm_rhs_fn(k, n),
                                            start=(k == 0), stop=(k == ncols_k - 1))
                    mmi.then_inc(fam_sem.h, 1)
            plan.add("tensor", mms)
            fam_sem.inc(2)
            stamp[s_i] = (fam_sem, fam_sem.n)

            def evac(E, fs=fs, waits=tuple(pre_evac_waits)):
                for n in range(2):
                    E.wait_ge(fam_sem.h, 2 * fs + n + 1)
                    for sem, cnt in waits:
                        if cnt > 0:
                            E.wait_ge(sem.h, cnt)
                    evac_fn(E, n)
                if evac_sem is not None:
                    _flush(E, st, evac_eng).then_inc(evac_sem.h, 2)
            plan.add(evac_eng, evac)
            if evac_sem is not None:
                evac_sem.inc(2)

        for ex in range(nexp):
            for m in range(MOE_FC):
                def g_evac_f(m=m):
                    def f(E, n, m=m):
                        E.activation(sg[:, m, n * 512:(n + 1) * 512],
                                     pg[:, n, :], AF.Silu)
                    return f
                emit_strip(gates_d, ex * MOE_FC + m, KC, s_mmg, gctr, pg,
                           lambda k, n: xb[:, k, n * 512:(n + 1) * 512],
                           g_evac_f(), "scalar", s_evg,
                           pre_mm_waits=[(s_evg, 2 * gctr[0])],
                           pre_evac_waits=[(s_mmd, 32 * ex)] if ex >= 1 else [])

                def u_evac_f(ex=ex, m=m):
                    def f(E, n, ex=ex, m=m):
                        E.tensor_mul(uc[:, m % 2, n * 512:(n + 1) * 512],
                                     pu[:, n, :],
                                     cbc[:, ex, n * 512:(n + 1) * 512])
                    return f
                emit_strip(ups_d, ex * MOE_FC + m, KC, s_mmu, uctr, pu,
                           lambda k, n: xb[:, k, n * 512:(n + 1) * 512],
                           u_evac_f(), "vector", None,
                           pre_mm_waits=[(s_h, 11 * ex + m)])

                def hmul(vector, ex=ex, m=m, tg=s_evg.n):
                    vector.wait_ge(s_evg.h, tg)
                    vector.tensor_mul(sg[:, m, :], sg[:, m, :], uc[:, m % 2, :])
                    _flush(vector, st, "vector").then_inc(s_h.h, 1)
                plan.add("vector", hmul)
                s_h.inc(1)

            for m in range(KC):
                def d_evac_f(m=m):
                    def f(E, n, m=m):
                        E.scalar_tensor_tensor(xT[:, m, n * 512:(n + 1) * 512],
                                               pd[:, n, :], 1.0,
                                               xT[:, m, n * 512:(n + 1) * 512],
                                               ALU.mult, ALU.add)
                    return f
                pre_mm = [(s_acc, 2 * dctr[0])]
                if m == 0:
                    pre_mm.append((s_h, 11 * (ex + 1)))
                emit_strip(downs_d, ex * KC + m, MOE_FC, s_mmd, dctr, pd,
                           lambda k, n: sg[:, k, n * 512:(n + 1) * 512],
                           d_evac_f(), "vector", s_acc,
                           pre_mm_waits=pre_mm)

        plan.emit(nc)

        if do_ln:
            emit_ln(nc, st, xT, xb, KC, T, post_bf16=False)
        if dbg:
            dbg_sc = nc.dram_tensor("dbg_sc", [8, T], F32, kind="ExternalOutput")
            dbg_cb = nc.dram_tensor("dbg_cb", [8, T], F32, kind="ExternalOutput")
            dbg_cbc = nc.dram_tensor("dbg_cbc", [128, NEXP, T], BF16, kind="ExternalOutput")
            dbg_sg = nc.dram_tensor("dbg_sg", [128, MOE_FC, T], BF16, kind="ExternalOutput")
            with nc.Block() as blk:
                @blk.sync
                def _(sync):
                    sync.dma_start(dbg_sc[:, :], srow[:, :]).then_inc(s_st.h, 16)
                    sync.dma_start(dbg_cb[:, :], combT[:, :]).then_inc(s_st.h, 16)
                    sync.dma_start(dbg_cbc[:, :, :], cbc[:, :, :]).then_inc(s_st.h, 16)
                    sync.dma_start(dbg_sg[:, :, :], sg[:, :, :]).then_inc(s_st.h, 16)
                    sync.wait_ge(s_st.h, s_st.inc(64))
        with nc.Block() as blk:
            @blk.sync
            def _(sync):
                sync.dma_start(out_d[:, :].rearrange("(k p) t -> p k t", p=128),
                               xT[:, :, :]).then_inc(s_st.h, 16)
                sync.wait_ge(s_st.h, s_st.inc(16))

    return nc


def host_moe_inputs(x1_full, gate_weight, gate_proj, up_proj, down_proj,
                    shared_gate, shared_up, shared_down):
    gs = [_tile_w(gate_proj[e]) for e in range(8)]
    gs += [_tile_w(shared_gate[:, :MOE_F]), _tile_w(shared_gate[:, MOE_F:])]
    us = [_tile_w(up_proj[e]) for e in range(8)]
    us += [_tile_w(shared_up[:, :MOE_F]), _tile_w(shared_up[:, MOE_F:])]
    ds = [_tile_w(down_proj[e]) for e in range(8)]
    ds += [_tile_w(shared_down[:MOE_F, :]), _tile_w(shared_down[MOE_F:, :])]
    gates = np.concatenate(gs, 0).reshape(NEXP * MOE_FC, -1)
    ups = np.concatenate(us, 0).reshape(NEXP * MOE_FC, -1)
    downs = np.concatenate(ds, 0).reshape(NEXP * KC, -1)
    gwt = np.ascontiguousarray(
        gate_weight.T.reshape(KC, 128, 1, 8).transpose(1, 0, 3, 2))[:, :, :, 0]
    idn = np.eye(128, dtype=np.float32)
    maps = []
    for c in range(NCORE):
        b, h = c // 2, c % 2
        x1t = _feat_major(x1_full[b, h * T:(h + 1) * T, :])
        maps.append(dict(x1t=x1t, gwt=gwt, gates=gates, ups=ups, downs=downs,
                         idn=idn))
    return maps


def gather_std(results, key="out"):
    out = np.empty((4, 2048, DM), np.float32)
    for c in range(NCORE):
        b, h = c // 2, c % 2
        out[b, h * T:(h + 1) * T, :] = results[c][key].T
    return out



HKV = 4
NH = 16
SEQ = 2048
SCALE = 1.0 / math.sqrt(128.0)


def build_attn():
    nc = bass.Bass("TRN2", target_bir_lowering=False)
    xseq_d = nc.dram_tensor("xseq", [DM, SEQ], F32, kind="ExternalInput")
    xown_d = nc.dram_tensor("xown", [DM, T], F32, kind="ExternalInput")
    wq_d = nc.dram_tensor("wq", [KC, 128 * KC * 128], F32, kind="ExternalInput")
    wkt_d = nc.dram_tensor("wkt", [128, HKV * KC * 128], F32, kind="ExternalInput")
    wvt_d = nc.dram_tensor("wvt", [128, KC * 512], F32, kind="ExternalInput")
    wo_d = nc.dram_tensor("wo", [KC, 128 * KC * 128], F32, kind="ExternalInput")
    mask_d = nc.dram_tensor("mask", [9 * 128, 128], BF16, kind="ExternalInput")
    out_d = nc.dram_tensor("out", [DM, T], F32, kind="ExternalOutput")

    with ExitStack() as ctx:
        e = ctx.enter_context
        st = {}
        alloc_consts(nc, ctx, st)
        alloc_ln_state(nc, ctx, st, T)
        ones1_b = e(nc.sbuf_tensor("ones1b", [1, 128], BF16))
        xa = e(nc.sbuf_tensor("xa", [128, KC, T], BF16))
        psum_pool = e(nc.psum_tensor("psum_pool", [128, 4096], F32))
        st["psum_pool"] = psum_pool

        s_ld = Sem(ctx, nc, "s_ld")
        s_cast = Sem(ctx, nc, "s_cast")
        s_mm = Sem(ctx, nc, "s_mm")
        s_ev = Sem(ctx, nc, "s_ev")
        s_pe = Sem(ctx, nc, "s_pe")
        s_pm = Sem(ctx, nc, "s_pm")
        s_pv = Sem(ctx, nc, "s_pv")
        s_tl = Sem(ctx, nc, "s_tl")
        s_ao = Sem(ctx, nc, "s_ao")
        s_acc = Sem(ctx, nc, "s_acc")
        s_st = Sem(ctx, nc, "s_st")

        emit_consts_init(nc, st)
        with nc.Block() as blk:
            @blk.vector
            def _(vector):
                vector.memset(ones1_b[:, :], 1.0)

        def run_strips(wstg, wb, jobs):
            plan = Plan()
            ld_base = s_ld.n
            cast_base = s_cast.n
            mm_stamp = {}
            for s_i, jb in enumerate(jobs):
                slot = s_i % 2

                def dma(sync, s_i=s_i, slot=slot, jb=jb):
                    if s_i >= 2:
                        sync.wait_ge(s_cast.h, cast_base + s_i - 1)
                    sync.dma_start(wstg[:, slot, :],
                                   jb["dram"][jb["row"], :]
                                   .rearrange("(p x) -> p x", p=128)).then_inc(s_ld.h, 16)
                plan.add("sync", dma)
                s_ld.inc(16)

                def cast(gp, s_i=s_i, slot=slot):
                    gp.wait_ge(s_ld.h, ld_base + 16 * (s_i + 1))
                    if s_i >= 2:
                        gp.wait_ge(s_mm.h, mm_stamp[s_i - 2])
                    gp.tensor_copy(wb[:, slot, :], wstg[:, slot, :])
                    _flush(gp, st, "gpsimd").then_inc(s_cast.h, 1)
                plan.add("gpsimd", cast)
                s_cast.inc(1)

                ev_base = s_ev.n

                def mms(tensor, s_i=s_i, slot=slot, jb=jb, ev_base=ev_base):
                    tensor.wait_ge(s_cast.h, cast_base + s_i + 1)
                    if s_i >= 1:
                        tensor.wait_ge(s_ev.h, ev_base)
                    for sem, cnt in jb.get("pre_mm", ()):
                        if cnt > 0:
                            tensor.wait_ge(sem.h, cnt)
                    wv_ = wb[:, slot, :].rearrange("p (k c) -> p k c", k=KC)
                    for n in range(2):
                        for k in range(KC):
                            mmi = tensor.matmul(jb["psv"][:, n, :], wv_[:, k, :],
                                                jb["rhs_fn"](k, n),
                                                start=(k == 0), stop=(k == KC - 1))
                        mmi.then_inc(s_mm.h, 1)
                plan.add("tensor", mms)
                s_mm.inc(2)
                mm_stamp[s_i] = s_mm.n

                def evac(E, jb=jb, mm_base=s_mm.n - 2):
                    for n in range(2):
                        E.wait_ge(s_mm.h, mm_base + n + 1)
                        jb["evac_fn"](E, n)
                    _flush(E, st, jb.get("evac_eng", "scalar")).then_inc(s_ev.h, 1)
                plan.add(jb.get("evac_eng", "scalar"), evac)
                s_ev.inc(1)
            plan.emit(nc)

        pq = psum_pool[:, 0:1024].rearrange("p (a b) -> p a b", a=2)

        with ExitStack() as actx:
            ea = actx.enter_context
            qT = ea(nc.sbuf_tensor("qT", [128, NH, T], BF16, side="right"))
            kT = ea(nc.sbuf_tensor("kT", [128, HKV, SEQ], BF16, side="right"))
            vtok = ea(nc.sbuf_tensor("vtok", [128, 16, 512], BF16, side="right"))

            with ExitStack() as qctx:
                eq = qctx.enter_context
                wstg = eq(nc.sbuf_tensor("wstg", [128, 2, KC * 128], F32, side="right"))
                wb = eq(nc.sbuf_tensor("wb", [128, 2, KC * 128], BF16, side="right"))

                for g in range(8):
                    with nc.Block() as blk:
                        @blk.sync
                        def _(sync, g=g):
                            sync.dma_start(
                                wstg[:, :, 0:T].rearrange("p s t -> p s t"),
                                xown_d[:, :].rearrange("(k p) t -> p k t", p=128)
                                [:, 2 * g:2 * g + 2, :]).then_inc(s_ld.h, 16)
                            sync.wait_ge(s_ld.h, s_ld.inc(16))
                    with nc.Block() as blk:
                        @blk.vector
                        def _(vector, g=g):
                            vector.tensor_copy(xa[:, 2 * g:2 * g + 2, :],
                                               wstg[:, :, 0:T])

                jobs = []
                for m in range(NH):
                    def q_ev(m=m):
                        def f(E, n, m=m):
                            E.activation(qT[:, m, n * 512:(n + 1) * 512],
                                         pq[:, n, :], AF.Identity)
                        return f
                    jobs.append(dict(dram=wq_d, row=m, psv=pq,
                                     rhs_fn=lambda k, n: xa[:, k, n * 512:(n + 1) * 512],
                                     evac_fn=q_ev()))
                run_strips(wstg, wb, jobs)

            pk = psum_pool[:, 1024:1536].unsqueeze(1)
            pv = psum_pool[:, 1536:2048].unsqueeze(1)
            with ExitStack() as kctx:
                ek = kctx.enter_context
                sstg = ek(nc.sbuf_tensor("sstg", [128, KC, 256], F32, side="right"))
                xsq = ek(nc.sbuf_tensor("xsq", [128, KC, 256], BF16, side="right"))
                wkb = ek(nc.sbuf_tensor("wkb", [128, HKV, KC, 128], BF16, side="right"))
                wvb = ek(nc.sbuf_tensor("wvb", [128, KC, 512], BF16, side="right"))

                for half in range(2):
                    with nc.Block() as blk:
                        @blk.sync
                        def _(sync, half=half):
                            sync.dma_start(
                                sstg[:, :, :].rearrange("p k c -> p (k c)"),
                                wkt_d[:, half * 4096:(half + 1) * 4096]
                                ).then_inc(s_ld.h, 16)
                            sync.wait_ge(s_ld.h, s_ld.inc(16))
                    with nc.Block() as blk:
                        @blk.vector
                        def _(vector, half=half):
                            vector.tensor_copy(
                                wkb[:, 2 * half:2 * half + 2, :, :],
                                sstg[:, :, :].rearrange("p k c -> p (k c)")
                                .rearrange("p (h k c) -> p h k c", h=2, k=KC))
                for half in range(2):
                    with nc.Block() as blk:
                        @blk.sync
                        def _(sync, half=half):
                            sync.dma_start(
                                sstg[:, :, :].rearrange("p k c -> p (k c)"),
                                wvt_d[:, half * 4096:(half + 1) * 4096]
                                ).then_inc(s_ld.h, 16)
                            sync.wait_ge(s_ld.h, s_ld.inc(16))
                    with nc.Block() as blk:
                        @blk.vector
                        def _(vector, half=half):
                            vector.tensor_copy(
                                wvb[:, 8 * half:8 * half + 8, :],
                                sstg[:, :, :].rearrange("p k c -> p (k c)")
                                .rearrange("p (k c) -> p k c", k=8))

                for eth in range(8):
                    with nc.Block() as blk:
                        @blk.sync
                        def _(sync, eth=eth):
                            sync.dma_start(
                                sstg[:, :, :],
                                xseq_d[:, :].rearrange("(k p) t -> p k t", p=128)
                                [:, :, eth * 256:(eth + 1) * 256]).then_inc(s_ld.h, 16)
                            sync.wait_ge(s_ld.h, s_ld.inc(16))
                    with nc.Block() as blk:
                        @blk.vector
                        def _(vector):
                            vector.tensor_copy(xsq[:, :, :], sstg[:, :, :])
                    mm_base = s_mm.n
                    ev_base = s_ev.n
                    with nc.Block() as blk:
                        @blk.tensor
                        def _(tensor, eth=eth, mm_base=mm_base, ev_base=ev_base):
                            for m in range(HKV):
                                if m >= 1:
                                    tensor.wait_ge(s_ev.h, ev_base + m)
                                for k in range(KC):
                                    mmi = tensor.matmul(pk[:, 0, 0:256],
                                                        wkb[:, m, k, :],
                                                        xsq[:, k, :],
                                                        start=(k == 0), stop=(k == KC - 1))
                                mmi.then_inc(s_mm.h, 1)
                            for tt in range(2):
                                if tt >= 1:
                                    tensor.wait_ge(s_ev.h, ev_base + HKV + tt)
                                for k in range(KC):
                                    mmi = tensor.matmul(
                                        pv[:, 0, :],
                                        xsq[:, k, tt * 128:(tt + 1) * 128],
                                        wvb[:, k, :], start=(k == 0), stop=(k == KC - 1))
                                mmi.then_inc(s_mm.h, 1)

                        @blk.scalar
                        def _(scalar, eth=eth, mm_base=mm_base):
                            for m in range(HKV):
                                scalar.wait_ge(s_mm.h, mm_base + m + 1)
                                scalar.activation(
                                    kT[:, m, eth * 256:(eth + 1) * 256],
                                    pk[:, 0, 0:256], AF.Identity)
                                _flush(scalar, st, "scalar").then_inc(s_ev.h, 1)
                            for tt in range(2):
                                scalar.wait_ge(s_mm.h, mm_base + HKV + tt + 1)
                                scalar.activation(vtok[:, eth * 2 + tt, :],
                                                  pv[:, 0, :], AF.Identity)
                                _flush(scalar, st, "scalar").then_inc(s_ev.h, 1)
                    s_mm.inc(HKV + 2)
                    s_ev.inc(HKV + 2)

            negC = ea(nc.sbuf_tensor("negC", [1, 4, T], BF16, side="right"))
            sq2 = ea(nc.sbuf_tensor("sq2", [128, 2048], BF16, side="right"))
            qrow = ea(nc.sbuf_tensor("qrow", [1, 4, T], BF16, side="right"))
            masks = ea(nc.sbuf_tensor("masks", [128, 9, 128], BF16, side="right"))
            pbuf = ea(nc.sbuf_tensor("pbuf", [128, 2, 512], BF16, side="right"))
            rlb = ea(nc.sbuf_tensor("rlb", [128, 512], F32, side="right"))
            rl = ea(nc.sbuf_tensor("rl", [1, 512], F32, side="right"))
            kmaxs = ea(nc.sbuf_tensor("kmaxs", [1, 8], F32, side="right"))
            with nc.Block() as blk:
                @blk.sync
                def _(sync):
                    sync.dma_start(masks[:, :, :],
                                   mask_d[:, :].rearrange("(r p) c -> p r c", p=128)
                                   ).then_inc(s_ld.h, 16)
                    sync.wait_ge(s_ld.h, s_ld.inc(16))
            def emit_cprime(kvh):
                with nc.Block() as blk:
                    @blk.scalar
                    def _(scalar, kvh=kvh):
                        scalar.activation(sq2[:, :], kT[:, kvh, :], AF.Square)
                with nc.Block() as blk:
                    @blk.tensor
                    def _(tensor):
                        for n in range(4):
                            tensor.matmul(psum_pool[0:1, n * 512:(n + 1) * 512],
                                          st["ones128_bf16"][:, 0:1],
                                          sq2[:, n * 512:(n + 1) * 512],
                                          start=True, stop=True)
                with nc.Block() as blk:
                    @blk.vector
                    def _(vector):
                        vector.tensor_reduce(kmaxs[0:1, 0:1], psum_pool[0:1, 0:2048],
                                             AX.X, ALU.max)
                with nc.Block() as blk:
                    @blk.scalar
                    def _(scalar):
                        scalar.activation(kmaxs[0:1, 1:2], kmaxs[0:1, 0:1], AF.Sqrt)
                with nc.Block() as blk:
                    @blk.scalar
                    def _(scalar, kvh=kvh):
                        for i in range(4):
                            scalar.activation(sq2[:, i * 512:(i + 1) * 512],
                                              qT[:, 4 * kvh + i, 0:512], AF.Square)
                with nc.Block() as blk:
                    @blk.tensor
                    def _(tensor):
                        for i in range(4):
                            tensor.matmul(psum_pool[0:1, i * 512:(i + 1) * 512],
                                          st["ones128_bf16"][:, 0:1],
                                          sq2[:, i * 512:(i + 1) * 512],
                                          start=True, stop=True)
                with nc.Block() as blk:
                    @blk.scalar
                    def _(scalar, kvh=kvh):
                        for i in range(4):
                            scalar.activation(sq2[:, i * 512:(i + 1) * 512],
                                              qT[:, 4 * kvh + i, 512:1024], AF.Square)
                with nc.Block() as blk:
                    @blk.tensor
                    def _(tensor):
                        for i in range(4):
                            tensor.matmul(
                                psum_pool[0:1, 2048 + i * 512:2048 + (i + 1) * 512],
                                st["ones128_bf16"][:, 0:1],
                                sq2[:, i * 512:(i + 1) * 512], start=True, stop=True)
                with nc.Block() as blk:
                    @blk.scalar
                    def _(scalar):
                        for i in range(4):
                            scalar.activation(qrow[0:1, i, 0:512],
                                              psum_pool[0:1, i * 512:(i + 1) * 512],
                                              AF.Sqrt)
                            scalar.activation(
                                qrow[0:1, i, 512:1024],
                                psum_pool[0:1, 2048 + i * 512:2048 + (i + 1) * 512],
                                AF.Sqrt)
                with nc.Block() as blk:
                    @blk.vector
                    def _(vector):
                        for i in range(4):
                            vector.tensor_scalar(negC[0:1, i, :], qrow[0:1, i, :],
                                                 kmaxs[0:1, 1:2], -1.0,
                                                 ALU.mult, ALU.mult)

            psS = psum_pool[:, 0:1024].rearrange("p (a b) -> p a b", a=2)
            psL = psum_pool[0:1, 1024:1536]
            psO = psum_pool[:, 1536:2048]
            psRB = psum_pool[:, 2048:2560]
            for kvh in range(HKV):
                emit_cprime(kvh)
                with nc.Block() as blk:
                    tjobs, sjobs, vjobs = [], [], []
                    for j in range(8):
                        nch = j + 9
                        qrhs = qT[:, 4 * kvh:4 * kvh + 4, j * 128:(j + 1) * 128]
                        crhs = negC[0:1, 0:4, j * 128:(j + 1) * 128]
                        mm_base = s_mm.n
                        pe_base = s_pe.n
                        pm_base = s_pm.n
                        pv_base = s_pv.n
                        ao_base = s_ao.n
                        tl_base = s_tl.n

                        def tjob(tensor, kvh=kvh, j=j, nch=nch, qrhs=qrhs, crhs=crhs,
                                 mm_base=mm_base, pe_base=pe_base, pm_base=pm_base,
                                 ao_base=ao_base, tl_base=tl_base):
                            def lpv(c):
                                if c >= j:
                                    tensor.wait_ge(s_pm.h, pm_base + (c - j) + 1)
                                else:
                                    tensor.wait_ge(s_pe.h, pe_base + c + 1)
                                if c == 0 and ao_base > 0:
                                    tensor.wait_ge(s_ao.h, ao_base)
                                tensor.matmul(psL[0:1, :], st["ones128_bf16"][:, 0:1],
                                              pbuf[:, c % 2, :],
                                              start=(c == 0), stop=(c == nch - 1))
                                tensor.matmul(psO[:, :],
                                              vtok[:, c, kvh * 128:(kvh + 1) * 128],
                                              pbuf[:, c % 2, :],
                                              start=(c == 0), stop=(c == nch - 1)
                                              ).then_inc(s_pv.h, 1)
                            for c in range(nch):
                                if c >= 2:
                                    tensor.wait_ge(s_pe.h, pe_base + c - 1)
                                elif c == 0 and pe_base > 0:
                                    tensor.wait_ge(s_pe.h, pe_base)
                                tensor.matmul(psS[:, c % 2, :],
                                              kT[:, kvh, c * 128:(c + 1) * 128],
                                              qrhs, start=True, stop=False)
                                tensor.matmul(psS[:, c % 2, :], ones1_b[0:1, :],
                                              crhs, start=False, stop=True
                                              ).then_inc(s_mm.h, 1)
                                if c >= 1:
                                    lpv(c - 1)
                            lpv(nch - 1)
                            tensor.wait_ge(s_tl.h, tl_base + 1)
                            tensor.matmul(psRB[:, :], st["ones1_f32"][0:1, :],
                                          rl[0:1, :], start=True, stop=True
                                          ).then_inc(s_tl.h, 1)
                        tjobs.append(tjob)
                        s_mm.inc(nch)
                        s_pv.inc(nch)

                        def sjob(scalar, nch=nch, mm_base=mm_base, pv_base=pv_base,
                                 tl_base=tl_base):
                            for c in range(nch):
                                scalar.wait_ge(s_mm.h, mm_base + c + 1)
                                if c >= 2:
                                    scalar.wait_ge(s_pv.h, pv_base + c - 1)
                                scalar.activation(pbuf[:, c % 2, :], psS[:, c % 2, :],
                                                  AF.Exp, scale=SCALE)
                                _flush(scalar, st, "scalar").then_inc(s_pe.h, 1)
                            scalar.wait_ge(s_tl.h, tl_base + 2)
                            scalar.activation(rlb[:, :], psRB[:, :], AF.Identity)
                            _flush(scalar, st, "scalar").then_inc(s_tl.h, 1)
                        sjobs.append(sjob)
                        s_pe.inc(nch)

                        def vjob(vector, kvh=kvh, j=j, nch=nch, pe_base=pe_base,
                                 pv_base=pv_base, tl_base=tl_base):
                            for c in range(j, nch):
                                vector.wait_ge(s_pe.h, pe_base + c + 1)
                                r = c - j
                                vector.tensor_mul(
                                    pbuf[:, c % 2, :].rearrange("p (a b) -> p a b", a=4),
                                    pbuf[:, c % 2, :].rearrange("p (a b) -> p a b", a=4),
                                    masks[:, r, :].unsqueeze(1).broadcast_to([128, 4, 128]))
                                _flush(vector, st, "vector").then_inc(s_pm.h, 1)
                            vector.wait_ge(s_pv.h, pv_base + nch)
                            vector.reciprocal(rl[0:1, :], psL[0:1, :])
                            _flush(vector, st, "vector").then_inc(s_tl.h, 1)
                            vector.wait_ge(s_tl.h, tl_base + 3)
                            vector.tensor_mul(
                                xa[:, 4 * kvh:4 * kvh + 4, j * 128:(j + 1) * 128],
                                psO[:, :].rearrange("p (a b) -> p a b", a=4),
                                rlb[:, :].rearrange("p (a b) -> p a b", a=4))
                            _flush(vector, st, "vector").then_inc(s_ao.h, 1)
                        vjobs.append(vjob)
                        s_pm.inc(nch - j)
                        s_tl.inc(3)
                        s_ao.inc(1)

                    blk.tensor(lambda E, fns=tjobs: [f(E) for f in fns])
                    blk.scalar(lambda E, fns=sjobs: [f(E) for f in fns])
                    blk.vector(lambda E, fns=vjobs: [f(E) for f in fns])

        with ExitStack() as wctx:
            ew = wctx.enter_context
            xT = ew(nc.sbuf_tensor("xT", [128, KC, T], F32, side="right"))
            wstg2 = ew(nc.sbuf_tensor("wstg2", [128, 2, KC * 128], F32, side="right"))
            wb2 = ew(nc.sbuf_tensor("wb2", [128, 2, KC * 128], BF16, side="right"))
            st["xT"] = xT

            with nc.Block() as blk:
                @blk.sync
                def _(sync):
                    sync.dma_start(xT[:, :, :],
                                   xown_d[:, :].rearrange("(k p) t -> p k t", p=128)
                                   ).then_inc(s_ld.h, 16)
                    sync.wait_ge(s_ld.h, s_ld.inc(16))

            jobs = []
            for m in range(KC):
                def wo_ev(m=m):
                    def f(E, n, m=m):
                        E.scalar_tensor_tensor(xT[:, m, n * 512:(n + 1) * 512],
                                               pq[:, n, :], 1.0,
                                               xT[:, m, n * 512:(n + 1) * 512],
                                               ALU.mult, ALU.add)
                    return f
                jobs.append(dict(dram=wo_d, row=m, psv=pq,
                                 rhs_fn=lambda k, n: xa[:, k, n * 512:(n + 1) * 512],
                                 evac_fn=wo_ev(), evac_eng="vector"))
            run_strips(wstg2, wb2, jobs)

            emit_ln(nc, st, xT, xa, KC, T, post_bf16=False)

            with nc.Block() as blk:
                @blk.sync
                def _(sync):
                    sync.dma_start(out_d[:, :].rearrange("(k p) t -> p k t", p=128),
                                   xT[:, :, :]).then_inc(s_st.h, 16)
                    sync.wait_ge(s_st.h, s_st.inc(16))

    return nc


def host_attn_inputs(x_full, wq, wk, wv, wo):
    wq_t = _tile_w(wq).reshape(KC, -1)
    wo_t = _tile_w(wo).reshape(KC, -1)
    wkt = np.ascontiguousarray(
        wk.reshape(KC, 128, HKV, 128).transpose(1, 2, 0, 3)).reshape(128, -1)
    wvt = np.ascontiguousarray(
        wv.reshape(KC, 128, 512).transpose(1, 0, 2)).reshape(128, -1)
    maps = []
    for c in range(NCORE):
        b, h = c // 2, c % 2
        xseq = _feat_major(x_full[b])
        xown = np.ascontiguousarray(xseq[:, h * T:(h + 1) * T])
        import ml_dtypes
        mask = np.zeros((9, 128, 128), np.float32)
        for r in range(9):
            if r < 8 * h:
                mask[r, :, :] = 1.0
            elif r == 8 * h:
                mask[r] = np.triu(np.ones((128, 128), np.float32))
        mask = mask.astype(ml_dtypes.bfloat16)
        maps.append(dict(xseq=xseq, xown=xown, wq=wq_t, wkt=wkt, wvt=wvt, wo=wo_t,
                         mask=mask.reshape(9 * 128, 128)))
    return maps



DIN = 4096
NHD = 64
HD = 64
DST = 128
CONVD = 4352
XBC_CH = 34
ZB = 8
NCH = 8


def build_ssd(dbg=False):
    nc = bass.Bass("TRN2", target_bir_lowering=False)
    x2_d = nc.dram_tensor("x2t", [DM, T], F32, kind="ExternalInput")
    win_d = nc.dram_tensor("win", [67, 128 * KC * 128], F32, kind="ExternalInput")
    wout_d = nc.dram_tensor("wout", [KC, 128 * 32 * 128], F32, kind="ExternalInput")
    cw_d = nc.dram_tensor("cw", [128, XBC_CH * 4], F32, kind="ExternalInput")
    cb_d = nc.dram_tensor("cb", [128, XBC_CH], F32, kind="ExternalInput")
    dtb_d = nc.dram_tensor("dtb", [1, NHD], F32, kind="ExternalInput")
    alog_d = nc.dram_tensor("alog", [1, NHD], F32, kind="ExternalInput")
    dp_d = nc.dram_tensor("dp", [1, DIN], F32, kind="ExternalInput")
    nw_d = nc.dram_tensor("nw", [128, 32], F32, kind="ExternalInput")
    idn_d = nc.dram_tensor("idnf", [128, 128], F32, kind="ExternalInput")
    idnb_d = nc.dram_tensor("idnb", [128, 128], BF16, kind="ExternalInput")
    tri_d = nc.dram_tensor("tri", [128, 128], BF16, kind="ExternalInput")
    flags_d = nc.dram_tensor("flags", [128, 2], F32, kind="ExternalInput")
    out_d = nc.dram_tensor("out", [DM, T], F32, kind="ExternalOutput")
    zscr = nc.dram_tensor("zscr", [T, DIN], BF16, kind="Internal")
    xscr = nc.dram_tensor("xscr", [DIN, T], BF16, kind="Internal")
    yscr = nc.dram_tensor("yscr", [T, DIN], BF16, kind="Internal")
    acscr = nc.dram_tensor("acscr", [NCH, 2, NHD, 128], F32, kind="Internal")
    prescr = nc.dram_tensor("prescr", [NHD, NCH], F32, kind="Internal")
    sumscr = nc.dram_tensor("sumscr", [NHD, NCH], F32, kind="Internal")
    cc_in = nc.dram_tensor("cc_in", [128, DIN + 2], F32, kind="Internal")
    cc_out = nc.dram_tensor("cc_out", [256, DIN + 2], F32, kind="Internal")
    edscr = nc.dram_tensor("edscr", [128, XBC_CH * 3], F32, kind="Internal")
    edscr2 = nc.dram_tensor("edscr2", [256, XBC_CH * 3], F32, kind="Internal")

    with ExitStack() as ctx:
        e = ctx.enter_context
        st = {}
        alloc_consts(nc, ctx, st)
        ones1_b = e(nc.sbuf_tensor("ones1b", [1, 128], BF16))
        xb = e(nc.sbuf_tensor("xb", [128, KC, T], BF16))
        psum_pool = e(nc.psum_tensor("psum_pool", [128, 4096], F32))
        st["psum_pool"] = psum_pool
        idn = e(nc.sbuf_tensor("idn_s", [128, 128], F32))
        idnb = e(nc.sbuf_tensor("idnb_s", [128, 128], BF16))
        tri = e(nc.sbuf_tensor("tri_s", [128, 128], BF16))
        bT = e(nc.sbuf_tensor("bT", [128, T], BF16))
        cT = e(nc.sbuf_tensor("cT", [128, T], BF16))
        dt_tok = e(nc.sbuf_tensor("dt_tok", [128, NCH, NHD], F32))
        acum_tok = e(nc.sbuf_tensor("acum_tok", [128, NCH, NHD], F32))
        rows = e(nc.sbuf_tensor("rows_c", [1, 4 * NHD], F32))
        nw_cols = e(nc.sbuf_tensor("nw_cols", [128, 32], F32))
        flags = e(nc.sbuf_tensor("flags_s", [128, 2], F32))

        s_ld = Sem(ctx, nc, "s_ld")
        s_cast = Sem(ctx, nc, "s_cast")
        s_mm = Sem(ctx, nc, "s_mm")
        s_ev = Sem(ctx, nc, "s_ev")
        s_cc = Sem(ctx, nc, "s_cc")
        s_pm_c = Sem(ctx, nc, "s_pm_c")
        s_st = Sem(ctx, nc, "s_st")

        emit_consts_init(nc, st)

        with nc.Block() as blk:
            @blk.sync
            def _(sync):
                for dst, src in [(idn[:, :], idn_d[:, :]), (idnb[:, :], idnb_d[:, :]),
                                 (tri[:, :], tri_d[:, :]), (nw_cols[:, :], nw_d[:, :]),
                                 (flags[:, :], flags_d[:, :]),
                                 (rows[0:1, 0:NHD], dtb_d[:, :]),
                                 (rows[0:1, NHD:2 * NHD], alog_d[:, :])]:
                    sync.dma_start(dst, src).then_inc(s_ld.h, 16)
                sync.wait_ge(s_ld.h, s_ld.inc(16 * 7))
        with ExitStack() as xctx:
            xstg = xctx.enter_context(
                nc.sbuf_tensor("xstg", [128, 2, T], F32, side="right"))
            for g in range(8):
                with nc.Block() as blk:
                    @blk.sync
                    def _(sync, g=g):
                        sync.dma_start(xstg[:, :, :],
                                       x2_d[:, :].rearrange("(k p) t -> p k t", p=128)
                                       [:, 2 * g:2 * g + 2, :]).then_inc(s_ld.h, 16)
                        sync.wait_ge(s_ld.h, s_ld.inc(16))
                with nc.Block() as blk:
                    @blk.vector
                    def _(vector, g=g):
                        vector.tensor_copy(xb[:, 2 * g:2 * g + 2, :], xstg[:, :, :])
        with nc.Block() as blk:
            @blk.vector
            def _(vector):
                vector.memset(ones1_b[:, :], 1.0)
        with ExitStack() as zctx:
            ez = zctx.enter_context
            wstg = ez(nc.sbuf_tensor("wstg", [128, 2, 4, KC, 128], F32, side="right"))
            wbz = ez(nc.sbuf_tensor("wbz", [128, 2, 4, KC, 128], BF16, side="right"))
            szb = ez(nc.sbuf_tensor("szb", [128, 2, 512], BF16, side="right"))
            pz = psum_pool[:, 0:1024].rearrange("p (a b) -> p a b", a=2)
            plan = Plan()
            ld_base = s_ld.n
            cast_base = s_cast.n
            mm_base0 = s_mm.n
            ev_base0 = s_ev.n
            st_base = s_st.n
            mm_stamp = {}
            for zg in range(ZB):
                def dma(sync, zg=zg):
                    if zg >= 2:
                        sync.wait_ge(s_cast.h, cast_base + zg - 1)
                    for i in range(4):
                        sync.dma_start(wstg[:, zg % 2, i, :, :],
                                       win_d[4 * zg + i, :]
                                       .rearrange("(p x) -> p x", p=128)
                                       .rearrange("p (k c) -> p k c", k=KC)
                                       ).then_inc(s_ld.h, 16)
                plan.add("sync", dma)
                s_ld.inc(64)

                def cast(gp, zg=zg):
                    gp.wait_ge(s_ld.h, ld_base + 64 * (zg + 1))
                    if zg >= 2:
                        gp.wait_ge(s_mm.h, mm_stamp[zg - 2])
                    gp.tensor_copy(wbz[:, zg % 2, :, :, :], wstg[:, zg % 2, :, :, :])
                    _flush(gp, st, "gpsimd").then_inc(s_cast.h, 1)
                plan.add("gpsimd", cast)
                s_cast.inc(1)

                def mms(tensor, zg=zg, ev_base=s_ev.n):
                    tensor.wait_ge(s_cast.h, cast_base + zg + 1)
                    for tt in range(8):
                        if not (zg == 0 and tt == 0):
                            tensor.wait_ge(s_ev.h, ev_base0 + (zg * 8 + tt) - 1)
                        for k in range(KC):
                            mmi = tensor.matmul(
                                pz[:, tt % 2, :],
                                xb[:, k, tt * 128:(tt + 1) * 128],
                                wbz[:, zg % 2, :, k, :],
                                start=(k == 0), stop=(k == KC - 1))
                        mmi.then_inc(s_mm.h, 1)
                plan.add("tensor", mms)
                s_mm.inc(8)
                mm_stamp[zg] = s_mm.n

                def evz(scalar, zg=zg):
                    for tt in range(8):
                        scalar.wait_ge(s_mm.h, mm_base0 + zg * 8 + tt + 1)
                        if zg * 8 + tt >= 2:
                            scalar.wait_ge(s_st.h, st_base + 16 * (zg * 8 + tt - 1))
                        scalar.activation(szb[:, tt % 2, :], pz[:, tt % 2, :], AF.Silu)
                        _flush(scalar, st, "scalar").then_inc(s_ev.h, 1)
                plan.add("scalar", evz)
                s_ev.inc(8)

                def stz(sync, zg=zg):
                    for tt in range(8):
                        sync.wait_ge(s_ev.h, ev_base0 + zg * 8 + tt + 1)
                        sync.dma_start(
                            zscr[tt * 128:(tt + 1) * 128, zg * 512:(zg + 1) * 512],
                            szb[:, tt % 2, :]).then_inc(s_st.h, 16)
                plan.add("sync", stz)
                s_st.inc(16 * 8)
            plan.emit(nc)

        sumrow = e(nc.sbuf_tensor("sumrow", [1, 512], F32))
        prerow = e(nc.sbuf_tensor("prerow", [1, 512], F32))
        xfix_tok = e(nc.sbuf_tensor("xfix_tok", [3, 32 * 128], BF16))
        with ExitStack() as pctx:
            ep = pctx.enter_context
            afm = ep(nc.sbuf_tensor("afm", [NHD, NCH, 128], F32, side="right"))
            acum_fm = ep(nc.sbuf_tensor("acum_fm", [NHD, NCH, 128], F32, side="right"))
            nacum_fm = ep(nc.sbuf_tensor("nacum_fm", [NHD, NCH, 128], F32, side="right"))
            scol = ep(nc.sbuf_tensor("scol", [NHD, 2, NCH], F32, side="right"))
            edgeb = ep(nc.sbuf_tensor("edgeb", [128, XBC_CH * 3], F32, side="right"))
            ownb = ep(nc.sbuf_tensor("ownb", [128, XBC_CH * 3], F32, side="right"))
            negAb = ep(nc.sbuf_tensor("negAb", [128, NHD], F32, side="right"))
            atok = ep(nc.sbuf_tensor("atok", [128, NCH, NHD], F32, side="right"))
            wstg2 = ep(nc.sbuf_tensor("wstg2", [128, 2, KC, 128], F32, side="right"))
            wb2 = ep(nc.sbuf_tensor("wb2", [128, 2, KC, 128], BF16, side="right"))
            xbct = ep(nc.sbuf_tensor("xbct", [128, 2, 1030], BF16, side="right"))
            dvec = ep(nc.sbuf_tensor("dvec", [128, 4, 128], BF16, side="right"))
            cwS = ep(nc.sbuf_tensor("cwS", [128, XBC_CH * 4], F32, side="right"))
            cbS = ep(nc.sbuf_tensor("cbS", [128, XBC_CH], F32, side="right"))
            xsil = ep(nc.sbuf_tensor("xsil", [128, 2, 1024], BF16, side="right"))
            sfp = ep(nc.sbuf_tensor("sfp", [128, 4, NCH, NHD], F32, side="right"))

            pxbc = psum_pool[:, 0:1024].rearrange("p (a b) -> p a b", a=2)
            pcv = psum_pool[:, 1024:2048].rearrange("p (a b) -> p a b", a=2)
            pdt = psum_pool[:, 2048:2560].rearrange("p (a b) -> p a b", a=NCH)
            ptr = psum_pool[:, 2560:3072]

            with nc.Block() as blk:
                @blk.sync
                def _(sync):
                    sync.dma_start(cwS[:, :], cw_d[:, :]).then_inc(s_ld.h, 16)
                    sync.dma_start(cbS[:, :], cb_d[:, :]).then_inc(s_ld.h, 16)
                    sync.wait_ge(s_ld.h, s_ld.inc(32))
                @blk.vector
                def _(vector):
                    vector.memset(xbct[:, 0, 0:3], 0.0)
                    vector.memset(xbct[:, 1, 0:3], 0.0)

            plan = Plan()
            ld_base = s_ld.n
            cast_base = s_cast.n
            mm_base = s_mm.n
            ev_base = s_ev.n
            st_base = s_st.n
            s_cv = Sem(ctx, nc, "s_cv")
            s_cve = Sem(ctx, nc, "s_cve")
            s_dg = Sem(ctx, nc, "s_dg")
            s_ed = Sem(ctx, nc, "s_ed")
            mm_stamp = {}
            for m in range(XBC_CH):
                slot = m % 2

                def dma(sync, m=m, slot=slot):
                    if m >= 2:
                        sync.wait_ge(s_cast.h, cast_base + m - 1)
                    sync.dma_start(wstg2[:, slot, :, :],
                                   win_d[32 + m, :].rearrange("(p k c) -> p k c",
                                                              p=128, k=KC)
                                   ).then_inc(s_ld.h, 16)
                plan.add("sync", dma)
                s_ld.inc(16)

                def cast(gp, m=m, slot=slot):
                    gp.wait_ge(s_ld.h, ld_base + 16 * (m + 1))
                    if m >= 2:
                        gp.wait_ge(s_mm.h, mm_stamp[m - 2])
                    gp.tensor_copy(wb2[:, slot, :, :], wstg2[:, slot, :, :])
                    _flush(gp, st, "gpsimd").then_inc(s_cast.h, 1)
                plan.add("gpsimd", cast)
                s_cast.inc(1)

                def mms(tensor, m=m, slot=slot):
                    tensor.wait_ge(s_cast.h, cast_base + m + 1)
                    if m >= 1:
                        tensor.wait_ge(s_ev.h, ev_base + 2 * m)
                    for n in range(2):
                        for k in range(KC):
                            mmi = tensor.matmul(pxbc[:, n, :], wb2[:, slot, k, :],
                                                xb[:, k, n * 512:(n + 1) * 512],
                                                start=(k == 0), stop=(k == KC - 1))
                        mmi.then_inc(s_mm.h, 1)
                plan.add("tensor", mms)
                s_mm.inc(2)
                mm_stamp[m] = s_mm.n

                def evx(scalar, m=m, slot=slot):
                    for n in range(2):
                        scalar.wait_ge(s_mm.h, mm_base + 2 * m + n + 1)
                        if m >= 2:
                            scalar.wait_ge(s_cv.h, 2 * (m - 1))
                            scalar.wait_ge(s_ed.h, m - 1)
                        scalar.activation(xbct[:, slot, 3 + n * 512:3 + (n + 1) * 512],
                                          pxbc[:, n, :], AF.Identity)
                    _flush(scalar, st, "scalar").then_inc(s_ev.h, 2)
                plan.add("scalar", evx)
                s_ev.inc(2)

                def edge(vector, m=m, slot=slot, evt=s_ev.n):
                    vector.wait_ge(s_ev.h, evt)
                    if m >= 1:
                        vector.wait_ge(s_cv.h, 2 * m)
                    vector.tensor_copy(edgeb[:, 3 * m:3 * m + 3],
                                       xbct[:, slot, 1024:1027])
                    vector.tensor_copy(ownb[:, 3 * m:3 * m + 3],
                                       xbct[:, slot, 3:6])
                    for w in range(4):
                        vector.tensor_scalar(dvec[:, w, :], idnb[:, :],
                                             cwS[:, 4 * m + w:4 * m + w + 1],
                                             None, ALU.mult)
                    _flush(vector, st, "vector").then_inc(s_ed.h, 1)
                    _flush(vector, st, "vector").then_inc(s_dg.h, 1)
                plan.add("vector", edge)
                s_ed.inc(1)
                s_dg.inc(1)

                def cmms(tensor, m=m, slot=slot, dgt=s_dg.n, evt=s_ev.n):
                    tensor.wait_ge(s_dg.h, dgt)
                    tensor.wait_ge(s_ev.h, evt)
                    if m >= 1:
                        tensor.wait_ge(s_cve.h, m)
                    for n in range(2):
                        for w in range(4):
                            mmi = tensor.matmul(
                                pcv[:, n, :], dvec[:, w, :],
                                xbct[:, slot, n * 512 + w:n * 512 + w + 512],
                                start=(w == 0), stop=(w == 3))
                        mmi.then_inc(s_cv.h, 1)
                plan.add("tensor", cmms)
                s_cv.inc(2)

                def evc(scalar, m=m, cvt=s_cv.n, stt=s_st.n):
                    for n in range(2):
                        scalar.wait_ge(s_cv.h, cvt - 2 + n + 1)
                        if m < 32 and m >= 2:
                            scalar.wait_ge(s_st.h, st_base + 16 * (m - 1))
                        if m < 32:
                            scalar.activation(xsil[:, m % 2, n * 512:(n + 1) * 512],
                                              pcv[:, n, :], AF.Silu,
                                              bias=cbS[:, m:m + 1])
                        elif m == 32:
                            scalar.activation(bT[:, n * 512:(n + 1) * 512],
                                              pcv[:, n, :], AF.Silu,
                                              bias=cbS[:, m:m + 1])
                        else:
                            scalar.activation(cT[:, n * 512:(n + 1) * 512],
                                              pcv[:, n, :], AF.Silu,
                                              bias=cbS[:, m:m + 1])
                    _flush(scalar, st, "scalar").then_inc(s_cve.h, 1)
                plan.add("scalar", evc)
                s_cve.inc(1)

                if m < 32:
                    def stx(sync, m=m, cvet=s_cve.n):
                        sync.wait_ge(s_cve.h, cvet)
                        sync.dma_start(xscr[m * 128:(m + 1) * 128, :],
                                       xsil[:, m % 2, :]).then_inc(s_st.h, 16)
                    plan.add("sync", stx)
                    s_st.inc(16)
            plan.emit(nc)

            with nc.Block() as blk:
                @blk.sync
                def _(sync):
                    sync.dma_start(wstg2[:, 0, :, :],
                                   win_d[66, :].rearrange("(p k c) -> p k c",
                                                          p=128, k=KC)
                                   ).then_inc(s_ld.h, 16)
                    sync.wait_ge(s_ld.h, s_ld.inc(16))
            with nc.Block() as blk:
                @blk.vector
                def _(vector):
                    vector.tensor_copy(wb2[:, 0, :, :], wstg2[:, 0, :, :])
            with nc.Block() as blk:
                @blk.tensor
                def _(tensor):
                    for tt in range(NCH):
                        for k in range(KC):
                            tensor.matmul(pdt[:, tt, :],
                                          xb[:, k, tt * 128:(tt + 1) * 128],
                                          wb2[:, 0, k, 0:64],
                                          start=(k == 0), stop=False)
                        tensor.matmul(pdt[:, tt, :], st["ones1_f32"][0:1, :],
                                      rows[0:1, 0:NHD],
                                      start=False, stop=True)
            with nc.Block() as blk:
                @blk.scalar
                def _(scalar):
                    for tt in range(NCH):
                        scalar.activation(sfp[:, 0, tt, :], pdt[:, tt, :], AF.Relu)
                        scalar.activation(sfp[:, 1, tt, :], pdt[:, tt, :], AF.Abs)
            with nc.Block() as blk:
                @blk.scalar
                def _(scalar):
                    for tt in range(NCH):
                        scalar.activation(sfp[:, 2, tt, :], sfp[:, 1, tt, :],
                                          AF.Exp, scale=-1.0)
            with nc.Block() as blk:
                @blk.scalar
                def _(scalar):
                    for tt in range(NCH):
                        scalar.activation(sfp[:, 3, tt, :], sfp[:, 2, tt, :],
                                          AF.Ln, bias=1.0)
            with nc.Block() as blk:
                @blk.vector
                def _(vector):
                    for tt in range(NCH):
                        vector.tensor_add(dt_tok[:, tt, :], sfp[:, 0, tt, :],
                                          sfp[:, 3, tt, :])
            with nc.Block() as blk:
                @blk.scalar
                def _(scalar):
                    scalar.activation(rows[0:1, 2 * NHD:3 * NHD],
                                      rows[0:1, NHD:2 * NHD], AF.Exp)
            with nc.Block() as blk:
                @blk.vector
                def _(vector):
                    vector.tensor_scalar_mul(rows[0:1, 3 * NHD:4 * NHD],
                                             rows[0:1, 2 * NHD:3 * NHD], -1.0)
            with nc.Block() as blk:
                @blk.tensor
                def _(tensor):
                    tensor.matmul(ptr[:, 0:64], st["ones1_f32"][0:1, :],
                                  rows[0:1, 3 * NHD:4 * NHD], start=True, stop=True)
            with nc.Block() as blk:
                @blk.scalar
                def _(scalar):
                    scalar.activation(negAb[:, :], ptr[:, 0:64], AF.Identity)
            with nc.Block() as blk:
                @blk.vector
                def _(vector):
                    vector.tensor_mul(atok[:, :, :], dt_tok[:, :, :],
                                      negAb[:, :].unsqueeze(1)
                                      .broadcast_to([128, NCH, NHD]))
            for rnd in range(2):
                with nc.Block() as blk:
                    @blk.tensor
                    def _(tensor, rnd=rnd):
                        for i in range(4):
                            c = rnd * 4 + i
                            tensor.transpose(
                                ptr[0:64, i * 128:(i + 1) * 128]
                                if False else psum_pool[0:64, 2560 + i * 128:2560 + (i + 1) * 128],
                                atok[:, c, :], idn[:, :])
                with nc.Block() as blk:
                    @blk.scalar
                    def _(scalar, rnd=rnd):
                        for i in range(4):
                            c = rnd * 4 + i
                            scalar.activation(afm[:, c, :],
                                              psum_pool[0:64, 2560 + i * 128:2560 + (i + 1) * 128],
                                              AF.Identity)
            with nc.Block() as blk:
                @blk.vector
                def _(vector):
                    for c in range(NCH):
                        vector.tensor_tensor_scan(acum_fm[:, c, :], afm[:, c, :],
                                                  afm[:, c, :], 0.0, ALU.add, ALU.bypass)
            with nc.Block() as blk:
                @blk.vector
                def _(vector):
                    vector.tensor_scalar_mul(
                        nacum_fm[:, :, :].rearrange("p a b -> p (a b)"),
                        acum_fm[:, :, :].rearrange("p a b -> p (a b)"), -1.0)
                    for c in range(NCH):
                        vector.tensor_copy(scol[:, 0, c:c + 1], acum_fm[:, c, 127:128])
            with nc.Block() as blk:
                @blk.sync
                def _(sync):
                    sync.dma_start(sumscr[:, :], scol[:, 0, :]).then_inc(s_ld.h, 16)
                    sync.wait_ge(s_ld.h, s_ld.inc(16))
            with nc.Block() as blk:
                @blk.vector
                def _(vector):
                    vector.tensor_tensor_scan(scol[:, 1, :], scol[:, 0, :],
                                              scol[:, 0, :], 0.0, ALU.add, ALU.bypass)
            with nc.Block() as blk:
                @blk.vector
                def _(vector):
                    vector.memset(scol[:, 0, 0:1], 0.0)
                    vector.tensor_copy(scol[:, 0, 1:NCH], scol[:, 1, 0:NCH - 1])
            for rnd in range(2):
                with nc.Block() as blk:
                    @blk.tensor
                    def _(tensor, rnd=rnd):
                        for i in range(4):
                            c = rnd * 4 + i
                            tensor.transpose(
                                psum_pool[:, 2560 + i * 64:2560 + (i + 1) * 64],
                                acum_fm[:, c, :], idn[0:64, 0:64])
                with nc.Block() as blk:
                    @blk.scalar
                    def _(scalar, rnd=rnd):
                        for i in range(4):
                            c = rnd * 4 + i
                            scalar.activation(acum_tok[:, c, :],
                                              psum_pool[:, 2560 + i * 64:2560 + (i + 1) * 64],
                                              AF.Identity)
            with nc.Block() as blk:
                @blk.sync
                def _(sync):
                    for c in range(NCH):
                        sync.dma_start(acscr[c, 0, :, :], acum_fm[:, c, :]
                                       ).then_inc(s_ld.h, 16)
                        sync.dma_start(acscr[c, 1, :, :], nacum_fm[:, c, :]
                                       ).then_inc(s_ld.h, 16)
                    sync.dma_start(prescr[:, :], scol[:, 0, :]).then_inc(s_ld.h, 16)
                    sync.wait_ge(s_ld.h, s_ld.inc(16 * 17))


            edr2 = ep(nc.sbuf_tensor("edr2", [128, 2, XBC_CH * 3], F32, side="right"))
            edge_eff = ep(nc.sbuf_tensor("edge_eff", [128, XBC_CH * 3], F32, side="right"))
            fxin = ep(nc.sbuf_tensor("fxin", [128, XBC_CH, 6], F32, side="right"))
            fxm = ep(nc.sbuf_tensor("fxm", [128, 12, XBC_CH], F32, side="right"))
            fxs = ep(nc.sbuf_tensor("fxs", [128, 6, XBC_CH], F32, side="right"))
            fxout = ep(nc.sbuf_tensor("fxout", [128, 3, XBC_CH], F32, side="right"))
            fxsil = ep(nc.sbuf_tensor("fxsil", [128, 3, XBC_CH], BF16, side="right"))
            cwV = cwS[:, :].rearrange("p (m w) -> p m w", m=XBC_CH)

            with nc.Block() as blk:
                @blk.sync
                def _(sync):
                    sync.dma_start(edscr[:, :], edgeb[:, :]).then_inc(s_ld.h, 16)
                    sync.wait_ge(s_ld.h, s_ld.inc(16))
            with nc.Block() as blk:
                @blk.gpsimd
                def _(gp):
                    gp.collective_compute(
                        "AllGather", ALU.bypass,
                        replica_groups=[[0, 1], [2, 3], [4, 5], [6, 7]],
                        ins=[edscr.ap().opt()], outs=[edscr2.ap().opt()],
                    ).then_inc(s_cc.h, 1)
                    gp.wait_ge(s_cc.h, s_cc.inc(1))
            with nc.Block() as blk:
                @blk.sync
                def _(sync):
                    sync.dma_start(edr2[:, 0, :], edscr2[0:128, :]).then_inc(s_ld.h, 16)
                    sync.dma_start(edr2[:, 1, :], edscr2[128:256, :]).then_inc(s_ld.h, 16)
                    sync.wait_ge(s_ld.h, s_ld.inc(32))
            with nc.Block() as blk:
                @blk.vector
                def _(vector):
                    vector.tensor_scalar(edge_eff[:, :], edr2[:, 0, :],
                                         flags[:, 0:1], None, ALU.mult)
                    vector.tensor_scalar(edr2[:, 1, :], edr2[:, 1, :],
                                         flags[:, 1:2], None, ALU.mult)
            with nc.Block() as blk:
                @blk.vector
                def _(vector):
                    vector.tensor_add(edge_eff[:, :], edge_eff[:, :], edr2[:, 1, :])
            with nc.Block() as blk:
                @blk.vector
                def _(vector):
                    vector.tensor_copy(fxin[:, :, 0:3],
                                       edge_eff[:, :].rearrange("p (m w) -> p m w",
                                                                m=XBC_CH))
                    vector.tensor_copy(fxin[:, :, 3:6],
                                       ownb[:, :].rearrange("p (m w) -> p m w",
                                                            m=XBC_CH))
            with nc.Block() as blk:
                @blk.vector
                def _(vector):
                    for t in range(3):
                        for w in range(4):
                            vector.tensor_mul(fxm[:, t * 4 + w, :],
                                              fxin[:, :, t + w], cwV[:, :, w])
            with nc.Block() as blk:
                @blk.vector
                def _(vector):
                    for t in range(3):
                        vector.tensor_add(fxs[:, 2 * t, :], fxm[:, t * 4, :],
                                          fxm[:, t * 4 + 1, :])
                        vector.tensor_add(fxs[:, 2 * t + 1, :], fxm[:, t * 4 + 2, :],
                                          fxm[:, t * 4 + 3, :])
            with nc.Block() as blk:
                @blk.vector
                def _(vector):
                    for t in range(3):
                        vector.tensor_add(fxout[:, t, :], fxs[:, 2 * t, :],
                                          fxs[:, 2 * t + 1, :])
            with nc.Block() as blk:
                @blk.vector
                def _(vector):
                    vector.tensor_add(fxout[:, :, :], fxout[:, :, :],
                                      cbS[:, :].unsqueeze(1)
                                      .broadcast_to([128, 3, XBC_CH]))
            with nc.Block() as blk:
                @blk.scalar
                def _(scalar):
                    scalar.activation(fxsil[:, :, :], fxout[:, :, :], AF.Silu)
            with nc.Block() as blk:
                @blk.vector
                def _(vector):
                    vector.tensor_copy(bT[:, 0:3], fxsil[:, :, 32])
                    vector.tensor_copy(cT[:, 0:3], fxsil[:, :, 33])
            for rnd in range(8):
                with nc.Block() as blk:
                    @blk.tensor
                    def _(tensor, rnd=rnd):
                        for i in range(4):
                            m = rnd * 4 + i
                            tensor.transpose(
                                psum_pool[0:3, 2560 + i * 128:2560 + (i + 1) * 128]
                                .bitcast(BF16)[0:3, 0:128],
                                fxsil[:, :, m], idnb[:, :])
                with nc.Block() as blk:
                    @blk.scalar
                    def _(scalar, rnd=rnd):
                        for i in range(4):
                            m = rnd * 4 + i
                            scalar.activation(
                                xfix_tok[0:3, m * 128:(m + 1) * 128],
                                psum_pool[0:3, 2560 + i * 128:2560 + (i + 1) * 128]
                                .bitcast(BF16)[0:3, 0:128], AF.Identity)

        with ExitStack() as s3:
            e3 = s3.enter_context
            S = e3(nc.sbuf_tensor("S", [128, DIN], F32, side="right"))
            Sb = e3(nc.sbuf_tensor("Sb", [128, DIN], BF16, side="right"))
            xtok = e3(nc.sbuf_tensor("xtok", [128, DIN], BF16, side="right"))
            xd = e3(nc.sbuf_tensor("xd", [128, DIN], BF16, side="right"))
            Yb = e3(nc.sbuf_tensor("Yb", [128, 2, 512], BF16, side="right"))
            mtall = e3(nc.sbuf_tensor("mtall", [128, 8, 8, 128], BF16, side="right"))
            cbm = e3(nc.sbuf_tensor("cbm", [128, 128], BF16, side="right"))
            btok = e3(nc.sbuf_tensor("btok", [128, 128], BF16, side="right"))
            expDs = e3(nc.sbuf_tensor("expDs", [128, 2, 1024], F32, side="right"))
            expDb = e3(nc.sbuf_tensor("expDb", [128, 2, 1024], BF16, side="right"))
            grows = e3(nc.sbuf_tensor("grows", [1, 2, 2, 1024], F32, side="right"))
            odde = e3(nc.sbuf_tensor("odde", [128, 4, NHD], F32, side="right"))
            lrow = e3(nc.sbuf_tensor("lrow", [1, NHD], F32, side="right"))
            lrep = e3(nc.sbuf_tensor("lrep", [1, DIN], BF16, side="right"))
            ttmp = e3(nc.sbuf_tensor("ttmp", [128, 2, 512], BF16, side="right"))
            dpB = e3(nc.sbuf_tensor("dpB", [128, DIN], BF16, side="right"))
            with ExitStack() as dctx:
                dpRow = dctx.enter_context(nc.sbuf_tensor("dpRow", [1, DIN], F32))
                with nc.Block() as blk:
                    @blk.sync
                    def _(sync):
                        sync.dma_start(dpRow[:, :], dp_d[:, :]).then_inc(s_ld.h, 16)
                        sync.wait_ge(s_ld.h, s_ld.inc(16))
                mmb = s_mm.n
                evb = s_ev.n
                with nc.Block() as blk:
                    @blk.tensor
                    def _(tensor, mmb=mmb, evb=evb):
                        for n in range(8):
                            if n >= 1:
                                tensor.wait_ge(s_ev.h, evb + n)
                            tensor.matmul(psum_pool[:, 0:512], st["ones1_f32"][0:1, :],
                                          dpRow[0:1, n * 512:(n + 1) * 512],
                                          start=True, stop=True).then_inc(s_mm.h, 1)
                    @blk.scalar
                    def _(scalar, mmb=mmb, evb=evb):
                        for n in range(8):
                            scalar.wait_ge(s_mm.h, mmb + n + 1)
                            scalar.activation(dpB[:, n * 512:(n + 1) * 512],
                                              psum_pool[:, 0:512], AF.Identity)
                            _flush(scalar, st, "scalar").then_inc(s_ev.h, 1)
                s_mm.inc(8)
                s_ev.inc(8)


            with nc.Block() as blk:
                @blk.sync
                def _(sync):
                    sync.dma_start(sumrow[0:1, :],
                                   sumscr[:, :].rearrange("a b -> (a b)").unsqueeze(0)
                                   ).then_inc(s_ld.h, 16)
                    sync.wait_ge(s_ld.h, s_ld.inc(16))
                @blk.vector
                def _(vector):
                    vector.memset(S[:, :], 0.0)
                    vector.memset(Sb[:, :], 0.0)

            pD = psum_pool[:, 0:2048].rearrange("p (a b) -> p a b", a=2)
            pYd = psum_pool[:, 2048:3072].rearrange("p (a b) -> p a b", a=2)
            pYo = psum_pool[:, 3072:4096].rearrange("p (a b) -> p a b", a=2)
            pCB = psum_pool[:, 0:128]
            pod = psum_pool[:, 128:192]
            pst = psum_pool[:, 0:1024].rearrange("p (a b) -> p a b", a=2)
            plm = psum_pool[:, 1024:2048].rearrange("p (a b) -> p a b", a=2)

            for c in range(NCH):
                with nc.Block() as blk:
                    @blk.sync
                    def _(sync, c=c):
                        sync.dma_start(xtok[:, :], xscr[:, c * 128:(c + 1) * 128],
                                       transpose=True).then_inc(s_ld.h, 16)
                        sync.wait_ge(s_ld.h, s_ld.inc(16))
                if c == 0:
                    with nc.Block() as blk:
                        @blk.vector
                        def _(vector):
                            vector.tensor_copy(xtok[0:3, :], xfix_tok[0:3, :])
                with nc.Block() as blk:
                    @blk.vector
                    def _(vector, c=c):
                        vector.tensor_mul(
                            xd[:, :].rearrange("p (h d) -> p h d", h=NHD),
                            xtok[:, :].rearrange("p (h d) -> p h d", h=NHD),
                            dt_tok[:, c, :].unsqueeze(2).broadcast_to([128, NHD, HD]))
                    @blk.tensor
                    def _(tensor, c=c):
                        tensor.matmul(pod[:, :], st["ones1_f32"][0:1, :],
                                      sumrow[0:1, :].rearrange("a (h c2) -> a h c2",
                                                               c2=NCH)[:, :, c],
                                      start=True, stop=True)
                with nc.Block() as blk:
                    @blk.scalar
                    def _(scalar, c=c):
                        scalar.activation(odde[:, 0, :], acum_tok[:, c, :], AF.Exp)
                        scalar.activation(lrow[0:1, :],
                                          sumrow[0:1, :].rearrange("a (h c2) -> a h c2",
                                                                   c2=NCH)[:, :, c],
                                          AF.Exp)
                    @blk.vector
                    def _(vector, c=c):
                        vector.tensor_sub(odde[:, 1, :], pod[:, :], acum_tok[:, c, :])
                with nc.Block() as blk:
                    @blk.scalar
                    def _(scalar):
                        scalar.activation(odde[:, 2, :], odde[:, 1, :], AF.Exp)
                    @blk.vector
                    def _(vector):
                        vector.tensor_copy(lrep[0:1, :],
                                           lrow[0:1, :].unsqueeze(2)
                                           .broadcast_to([1, NHD, HD]))
                with nc.Block() as blk:
                    @blk.tensor
                    def _(tensor, c=c):
                        tensor.matmul(pCB[:, :], bT[:, c * 128:(c + 1) * 128],
                                      cT[:, c * 128:(c + 1) * 128],
                                      start=True, stop=True)
                with nc.Block() as blk:
                    @blk.vector
                    def _(vector):
                        vector.tensor_mul(cbm[:, :], pCB[:, :], tri[:, :])
                with nc.Block() as blk:
                    @blk.tensor
                    def _(tensor, c=c):
                        tensor.transpose(
                            psum_pool[:, 192:256].bitcast(BF16)[:, 0:128],
                            bT[:, c * 128:(c + 1) * 128], idnb[:, :])
                with nc.Block() as blk:
                    @blk.scalar
                    def _(scalar):
                        scalar.activation(btok[:, :],
                                          psum_pool[:, 192:256].bitcast(BF16)[:, 0:128],
                                          AF.Identity)
                ldb = s_ld.n
                mmb = s_mm.n
                evb = s_ev.n
                with nc.Block() as blk:
                    @blk.sync
                    def _(sync, c=c, ldb=ldb, mmb=mmb):
                        for g in range(8):
                            if g >= 2:
                                sync.wait_ge(s_mm.h, mmb + g - 1)
                            sync.dma_start(
                                grows[0:1, g % 2, 0, :],
                                acscr[c, 0, :, :].rearrange("a b -> (a b)")
                                [8 * g * 128:(8 * g + 8) * 128].unsqueeze(0)
                                ).then_inc(s_ld.h, 16)
                            sync.dma_start(
                                grows[0:1, g % 2, 1, :],
                                acscr[c, 1, :, :].rearrange("a b -> (a b)")
                                [8 * g * 128:(8 * g + 8) * 128].unsqueeze(0)
                                ).then_inc(s_ld.h, 16)
                    s_ld.inc(32 * 8)

                    @blk.tensor
                    def _(tensor, c=c, ldb=ldb, mmb=mmb, evb=evb):
                        for g in range(8):
                            tensor.wait_ge(s_ld.h, ldb + 32 * (g + 1))
                            if g >= 2:
                                tensor.wait_ge(s_ev.h, evb + g - 1)
                            for n in range(2):
                                tensor.matmul(pD[:, g % 2, n * 512:(n + 1) * 512],
                                              st["ones1_f32"][0:1, :],
                                              grows[0:1, g % 2, 0, n * 512:(n + 1) * 512],
                                              start=True, stop=False)
                            for i in range(8):
                                mmi = tensor.matmul(
                                    pD[:, g % 2, i * 128:(i + 1) * 128],
                                    grows[0:1, g % 2, 1, i * 128:(i + 1) * 128],
                                    st["ones1_f32"][0:1, :],
                                    start=False, stop=(i == 3 or i == 7))
                            mmi.then_inc(s_mm.h, 1)
                    s_mm.inc(8)

                    @blk.scalar
                    def _(scalar, mmb=mmb, evb=evb):
                        for g in range(8):
                            scalar.wait_ge(s_mm.h, mmb + g + 1)
                            scalar.activation(expDs[:, g % 2, :], pD[:, g % 2, :],
                                              AF.Relu, scale=-1.0)
                            scalar.activation(expDb[:, g % 2, :], expDs[:, g % 2, :],
                                              AF.Exp, scale=-1.0)
                            _flush(scalar, st, "scalar").then_inc(s_ev.h, 1)
                    s_ev.inc(8)

                    @blk.vector
                    def _(vector, evb=evb):
                        for g in range(8):
                            vector.wait_ge(s_ev.h, evb + g + 1)
                            vector.tensor_mul(
                                mtall[:, g, :, :],
                                expDb[:, g % 2, :].rearrange("p (h l) -> p h l", h=8),
                                cbm[:, :].unsqueeze(1).broadcast_to([128, 8, 128]))
                            _flush(vector, st, "vector").then_inc(s_pm_c.h, 1)
                    s_pm_c.inc(8)

                mmb = s_mm.n
                evb = s_ev.n
                stb = s_st.n
                with nc.Block() as blk:
                    @blk.tensor
                    def _(tensor, c=c, mmb=mmb, evb=evb):
                        for g in range(8):
                            if g >= 2:
                                tensor.wait_ge(s_ev.h, evb + 2 * (g - 1))
                            for hl in range(8):
                                tensor.matmul(pYd[:, g % 2, hl * 64:(hl + 1) * 64],
                                              mtall[:, g, hl, :],
                                              xd[:, (8 * g + hl) * 64:(8 * g + hl + 1) * 64],
                                              start=True, stop=(hl >= 0))
                            mmi = tensor.matmul(pYo[:, g % 2, :],
                                                cT[:, c * 128:(c + 1) * 128],
                                                Sb[:, g * 512:(g + 1) * 512],
                                                start=True, stop=True)
                            mmi.then_inc(s_mm.h, 1)
                    s_mm.inc(8)

                    @blk.vector
                    def _(vector, c=c, mmb=mmb, evb=evb, stb=stb):
                        for g in range(8):
                            vector.wait_ge(s_mm.h, mmb + g + 1)
                            if g >= 2:
                                vector.wait_ge(s_st.h, stb + 16 * (g - 1))
                            odr = odde[:, 0, 8 * g:8 * g + 8]
                            vector.tensor_mul(
                                ttmp[:, g % 2, :].rearrange("p (h d) -> p h d", h=8),
                                pYo[:, g % 2, :].rearrange("p (h d) -> p h d", h=8),
                                odr.unsqueeze(2).broadcast_to([128, 8, 64]))
                            vector.scalar_tensor_tensor(
                                ttmp[:, g % 2, :], pYd[:, g % 2, :], 1.0,
                                ttmp[:, g % 2, :], ALU.mult, ALU.add)
                            vector.tensor_mul(Yb[:, g % 2, :],
                                              dpB[:, g * 512:(g + 1) * 512],
                                              xtok[:, g * 512:(g + 1) * 512])
                            vector.tensor_add(Yb[:, g % 2, :], Yb[:, g % 2, :],
                                              ttmp[:, g % 2, :])
                            _flush(vector, st, "vector").then_inc(s_ev.h, 2)
                    s_ev.inc(16)

                    @blk.sync
                    def _(sync, c=c, evb=evb):
                        for g in range(8):
                            sync.wait_ge(s_ev.h, evb + 2 * (g + 1))
                            sync.dma_start(
                                yscr[c * 128:(c + 1) * 128, g * 512:(g + 1) * 512],
                                Yb[:, g % 2, :]).then_inc(s_st.h, 16)
                    s_st.inc(16 * 8)

                with nc.Block() as blk:
                    @blk.vector
                    def _(vector, c=c):
                        vector.tensor_mul(
                            xd[:, :].rearrange("p (h d) -> p h d", h=NHD),
                            xd[:, :].rearrange("p (h d) -> p h d", h=NHD),
                            odde[:, 2, :].unsqueeze(2).broadcast_to([128, NHD, HD]))
                mmb = s_mm.n
                evb = s_ev.n
                with nc.Block() as blk:
                    @blk.tensor
                    def _(tensor, c=c, mmb=mmb, evb=evb):
                        for g in range(8):
                            if g >= 2:
                                tensor.wait_ge(s_ev.h, evb + 2 * (g - 1))
                            tensor.matmul(pst[:, g % 2, :], btok[:, :],
                                          xd[:, g * 512:(g + 1) * 512],
                                          start=True, stop=True)
                            tensor.matmul(plm[:, g % 2, :], ones1_b[0:1, :],
                                          lrep[0:1, g * 512:(g + 1) * 512],
                                          start=True, stop=True).then_inc(s_mm.h, 1)
                    s_mm.inc(8)

                    @blk.vector
                    def _(vector, mmb=mmb):
                        for g in range(8):
                            vector.wait_ge(s_mm.h, mmb + g + 1)
                            vector.tensor_mul(S[:, g * 512:(g + 1) * 512],
                                              S[:, g * 512:(g + 1) * 512],
                                              plm[:, g % 2, :])
                            vector.tensor_add(S[:, g * 512:(g + 1) * 512],
                                              S[:, g * 512:(g + 1) * 512],
                                              pst[:, g % 2, :])
                            _flush(vector, st, "vector").then_inc(s_ev.h, 2)
                    s_ev.inc(16)
                with nc.Block() as blk:
                    @blk.vector
                    def _(vector):
                        vector.tensor_copy(Sb[:, :], S[:, :])

            with nc.Block() as blk:
                @blk.sync
                def _(sync):
                    sync.dma_start(cc_in[:, 0:DIN], S[:, :]).then_inc(s_ld.h, 16)
                    sync.wait_ge(s_ld.h, s_ld.inc(16))
            with nc.Block() as blk:
                @blk.gpsimd
                def _(gp):
                    gp.collective_compute(
                        "AllGather", ALU.bypass,
                        replica_groups=[[0, 1], [2, 3], [4, 5], [6, 7]],
                        ins=[cc_in.ap().opt()], outs=[cc_out.ap().opt()],
                    ).then_inc(s_cc.h, 1)
                    gp.wait_ge(s_cc.h, s_cc.inc(1))
            with nc.Block() as blk:
                @blk.sync
                def _(sync):
                    sync.dma_start(S[:, :], cc_out[0:128, 0:DIN]).then_inc(s_ld.h, 16)
                    sync.wait_ge(s_ld.h, s_ld.inc(16))
            with nc.Block() as blk:
                @blk.vector
                def _(vector):
                    vector.tensor_scalar(S[:, :], S[:, :], flags[:, 0:1], None, ALU.mult)
            exf = expDs[:, :, :].rearrange("p a b -> p (a b)")
            for half in range(2):
                with nc.Block() as blk:
                    @blk.sync
                    def _(sync, half=half):
                        sync.dma_start(exf[:, :],
                                       cc_out[128:256, half * 2048:(half + 1) * 2048]
                                       ).then_inc(s_ld.h, 16)
                        sync.wait_ge(s_ld.h, s_ld.inc(16))
                with nc.Block() as blk:
                    @blk.vector
                    def _(vector, half=half):
                        vector.scalar_tensor_tensor(
                            S[:, half * 2048:(half + 1) * 2048], exf[:, :],
                            flags[:, 1:2], S[:, half * 2048:(half + 1) * 2048],
                            ALU.mult, ALU.add)
            with nc.Block() as blk:
                @blk.vector
                def _(vector):
                    vector.tensor_copy(Sb[:, :], S[:, :])

            with nc.Block() as blk:
                @blk.sync
                def _(sync):
                    sync.dma_start(prerow[0:1, :],
                                   prescr[:, :].rearrange("a b -> (a b)").unsqueeze(0)
                                   ).then_inc(s_ld.h, 16)
                    sync.wait_ge(s_ld.h, s_ld.inc(16))
            for c in range(NCH):
                with nc.Block() as blk:
                    @blk.sync
                    def _(sync, c=c):
                        sync.dma_start(xtok[:, :],
                                       yscr[c * 128:(c + 1) * 128, :]).then_inc(s_ld.h, 16)
                        sync.dma_start(xd[:, :],
                                       zscr[c * 128:(c + 1) * 128, :]).then_inc(s_ld.h, 16)
                        sync.wait_ge(s_ld.h, s_ld.inc(32))
                    @blk.tensor
                    def _(tensor, c=c):
                        tensor.matmul(pod[:, :], st["ones1_f32"][0:1, :],
                                      prerow[0:1, :].rearrange("a (h c2) -> a h c2",
                                                               c2=NCH)[:, :, c],
                                      start=True, stop=True)
                with nc.Block() as blk:
                    @blk.vector
                    def _(vector, c=c):
                        vector.tensor_add(odde[:, 1, :], pod[:, :], acum_tok[:, c, :])
                with nc.Block() as blk:
                    @blk.scalar
                    def _(scalar):
                        scalar.activation(odde[:, 0, :], odde[:, 1, :], AF.Exp)
                mmb = s_mm.n
                with nc.Block() as blk:
                    @blk.tensor
                    def _(tensor, c=c, mmb=mmb):
                        for g in range(8):
                            if g >= 2:
                                tensor.wait_ge(s_ev.h, s_ev.n + g - 1)
                            tensor.matmul(pYo[:, g % 2, :],
                                          cT[:, c * 128:(c + 1) * 128],
                                          Sb[:, g * 512:(g + 1) * 512],
                                          start=True, stop=True).then_inc(s_mm.h, 1)

                    @blk.vector
                    def _(vector, c=c, mmb=mmb):
                        for g in range(8):
                            vector.wait_ge(s_mm.h, mmb + g + 1)
                            odr = odde[:, 0, 8 * g:8 * g + 8]
                            vector.tensor_mul(
                                ttmp[:, g % 2, :].rearrange("p (h d) -> p h d", h=8),
                                pYo[:, g % 2, :].rearrange("p (h d) -> p h d", h=8),
                                odr.unsqueeze(2).broadcast_to([128, 8, 64]))
                            vector.tensor_add(xtok[:, g * 512:(g + 1) * 512],
                                              xtok[:, g * 512:(g + 1) * 512],
                                              ttmp[:, g % 2, :])
                            _flush(vector, st, "vector").then_inc(s_ev.h, 1)
                    s_mm.inc(8)
                    s_ev.inc(8)
                with nc.Block() as blk:
                    @blk.vector
                    def _(vector):
                        vector.tensor_mul(xtok[:, :], xtok[:, :], xd[:, :])
                with nc.Block() as blk:
                    @blk.scalar
                    def _(scalar):
                        scalar.activation(S[:, :], xtok[:, :], AF.Square,
                                          accum_out=odde[:, 3, 0:1])
                with nc.Block() as blk:
                    @blk.vector
                    def _(vector):
                        vector.tensor_scalar(odde[:, 3, 1:2], odde[:, 3, 0:1],
                                             1.0 / DIN, 1e-5, ALU.mult, ALU.add)
                with nc.Block() as blk:
                    @blk.scalar
                    def _(scalar):
                        scalar.activation(odde[:, 3, 1:2], odde[:, 3, 1:2], AF.Sqrt)
                with nc.Block() as blk:
                    @blk.vector
                    def _(vector):
                        vector.reciprocal(odde[:, 3, 2:3], odde[:, 3, 1:2])
                with nc.Block() as blk:
                    @blk.gpsimd
                    def _(gp):
                        gp.tensor_scalar(xtok[:, :], xtok[:, :], odde[:, 3, 2:3],
                                         None, ALU.mult)
                with nc.Block() as blk:
                    @blk.sync
                    def _(sync, c=c):
                        sync.dma_start(yscr[c * 128:(c + 1) * 128, :], xtok[:, :]
                                       ).then_inc(s_ld.h, 16)
                        sync.wait_ge(s_ld.h, s_ld.inc(16))

        with ExitStack() as s5:
            e5 = s5.enter_context
            xT = e5(nc.sbuf_tensor("xT", [128, KC, T], F32, side="right"))
            st["xT"] = xT
            alloc_ln_state(nc, ctx, st, T)
            wstg3 = e5(nc.sbuf_tensor("wstg3", [128, 1, 32, 128], F32, side="right"))
            wb3 = e5(nc.sbuf_tensor("wb3", [128, 1, 32, 128], BF16, side="right"))
            rhsb = e5(nc.sbuf_tensor("rhsb", [128, 32, 512], BF16, side="right"))
            with nc.Block() as blk:
                @blk.sync
                def _(sync):
                    sync.dma_start(xT[:, :, :],
                                   x2_d[:, :].rearrange("(k p) t -> p k t", p=128)
                                   ).then_inc(s_ld.h, 16)
                    sync.wait_ge(s_ld.h, s_ld.inc(16))
            nwc = nw_cols
            pwo = psum_pool[:, 0:512]

            for ps in range(2):
                ldb = s_ld.n
                with nc.Block() as blk:
                    @blk.sync
                    def _(sync, ps=ps, ldb=ldb):
                        for k in range(32):
                            sync.dma_start(rhsb[:, k, :],
                                           yscr[ps * 512:(ps + 1) * 512,
                                                k * 128:(k + 1) * 128],
                                           transpose=True).then_inc(s_ld.h, 16)
                        sync.wait_ge(s_ld.h, s_ld.inc(16 * 32))
                with nc.Block() as blk:
                    @blk.vector
                    def _(vector):
                        for k in range(32):
                            vector.tensor_scalar(rhsb[:, k, :], rhsb[:, k, :],
                                                 nwc[:, k:k + 1], None, ALU.mult)
                ldb = s_ld.n
                cb_ = s_cast.n
                mmb = s_mm.n
                evb = s_ev.n
                plan = Plan()
                for m in range(KC):
                    def dma(sync, m=m, ps=ps, ldb=ldb, cb_=cb_, mmb=mmb):
                        if m >= 1:
                            sync.wait_ge(s_mm.h, mmb + m)
                        sync.dma_start(wstg3[:, 0, :, :],
                                       wout_d[m, :].rearrange("(p k c) -> p k c",
                                                              p=128, k=32)
                                       ).then_inc(s_ld.h, 16)
                    plan.add("sync", dma)
                    s_ld.inc(16)

                    def cast(gp, m=m, ldb=ldb, cb_=cb_, mmb=mmb):
                        gp.wait_ge(s_ld.h, ldb + 16 * (m + 1))
                        gp.tensor_copy(wb3[:, 0, :, :], wstg3[:, 0, :, :])
                        _flush(gp, st, "gpsimd").then_inc(s_cast.h, 1)
                    plan.add("gpsimd", cast)
                    s_cast.inc(1)

                    def mms(tensor, m=m, cb_=cb_, evb=evb):
                        tensor.wait_ge(s_cast.h, cb_ + m + 1)
                        if m >= 1:
                            tensor.wait_ge(s_ev.h, evb + m)
                        for k in range(32):
                            mmi = tensor.matmul(pwo[:, :], wb3[:, 0, k, :],
                                                rhsb[:, k, :],
                                                start=(k == 0), stop=(k == 31))
                        mmi.then_inc(s_mm.h, 1)
                    plan.add("tensor", mms)
                    s_mm.inc(1)

                    def evac(vector, m=m, ps=ps, mmb=mmb):
                        vector.wait_ge(s_mm.h, mmb + m + 1)
                        vector.scalar_tensor_tensor(
                            xT[:, m, ps * 512:(ps + 1) * 512], pwo[:, :], 1.0,
                            xT[:, m, ps * 512:(ps + 1) * 512], ALU.mult, ALU.add)
                        _flush(vector, st, "vector").then_inc(s_ev.h, 1)
                    plan.add("vector", evac)
                    s_ev.inc(1)
                plan.emit(nc)

            emit_ln(nc, st, xT, xb, KC, T, post_bf16=False)
            with nc.Block() as blk:
                @blk.sync
                def _(sync):
                    sync.dma_start(out_d[:, :].rearrange("(k p) t -> p k t", p=128),
                                   xT[:, :, :]).then_inc(s_st.h, 16)
                    sync.wait_ge(s_st.h, s_st.inc(16))

    return nc


def host_ssd_inputs(x2_full, W_in, conv_w, conv_b, dt_bias, A_log, D_param,
                    norm_w, W_out):
    import ml_dtypes
    winp = np.zeros((DM, 8576), np.float32)
    winp[:, :8512] = W_in
    win_t = _tile_w(winp).reshape(67, -1)
    wout_t = _tile_w(W_out).reshape(KC, -1)
    cw = np.ascontiguousarray(conv_w.reshape(XBC_CH, 128, 4).transpose(1, 0, 2)
                              ).reshape(128, -1)
    cb = np.ascontiguousarray(conv_b.reshape(XBC_CH, 128).T)
    nw = np.ascontiguousarray(norm_w.reshape(32, 128).T)
    dp_rep = np.repeat(D_param, HD)[None, :]
    idnf = np.eye(128, dtype=np.float32)
    idnb = np.eye(128, dtype=np.float32).astype(ml_dtypes.bfloat16)
    tri = np.triu(np.ones((128, 128), np.float32)).astype(ml_dtypes.bfloat16)
    maps = []
    for c in range(NCORE):
        b, h = c // 2, c % 2
        x2t = _feat_major(x2_full[b, h * T:(h + 1) * T, :])
        fl = np.zeros((128, 2), np.float32)
        if h == 1:
            fl[:, 0] = 1.0
        maps.append(dict(
            x2t=x2t, win=win_t, wout=wout_t, cw=cw, cb=cb,
            dtb=dt_bias[None, :].astype(np.float32),
            alog=A_log[None, :].astype(np.float32),
            dp=dp_rep.astype(np.float32), nw=nw,
            idnf=idnf, idnb=idnb, tri=tri, flags=fl))
    return maps



def kernel(**ins):
    nc = build_attn()
    maps = host_attn_inputs(ins["L0_attn_x"], ins["L0_attn_wq"], ins["L0_attn_wk"],
                            ins["L0_attn_wv"], ins["L0_attn_wo"])
    res = run_bass_kernel_spmd(nc, maps, core_ids=list(range(NCORE)))
    x1 = gather_std(res.results)

    nc = build_moe()
    maps = host_moe_inputs(x1, ins["L0_ffn_gate_weight"], ins["L0_ffn_gate_proj"],
                           ins["L0_ffn_up_proj"], ins["L0_ffn_down_proj"],
                           ins["L0_ffn_shared_gate"], ins["L0_ffn_shared_up"],
                           ins["L0_ffn_shared_down"])
    res = run_bass_kernel_spmd(nc, maps, core_ids=list(range(NCORE)))
    x2 = gather_std(res.results)

    nc = build_ssd()
    maps = host_ssd_inputs(x2, ins["L1_attn_W_in"], ins["L1_attn_conv_w"],
                           ins["L1_attn_conv_b"], ins["L1_attn_dt_bias"],
                           ins["L1_attn_A_log"], ins["L1_attn_D_param"],
                           ins["L1_attn_norm_w"], ins["L1_attn_W_out"])
    res = run_bass_kernel_spmd(nc, maps, core_ids=list(range(NCORE)))
    x3 = gather_std(res.results)

    nc = build_tail()
    maps = host_tail_inputs(x3, ins["L1_ffn_factors"], ins["L1_ffn_bias"])
    res = run_bass_kernel_spmd(nc, maps, core_ids=list(range(NCORE)))
    return gather_tail(res.results)


# version 02
# speedup vs baseline: 1.0029x; 1.0029x over previous
import math
from contextlib import ExitStack

import numpy as np
import concourse.bass as bass
import concourse.mybir as mybir
from concourse.bass_utils import run_bass_kernel_spmd

F32 = mybir.dt.float32
BF16 = mybir.dt.bfloat16
AF = mybir.ActivationFunctionType
ALU = mybir.AluOpType
AX = mybir.AxisListType

NCORE = 8
T = 1024
DM = 2048
KC = DM // 128
LN_EPS = 1e-5


def _tile_w(w, tile_cols=128):
    K, M = w.shape
    Kc, Mc = K // 128, M // tile_cols
    return np.ascontiguousarray(w.reshape(Kc, 128, Mc, tile_cols).transpose(2, 1, 0, 3))


def _feat_major(x):
    return np.ascontiguousarray(x.T)


class Sem:
    def __init__(self, ctx, nc, name):
        self.h = ctx.enter_context(nc.semaphore(name))
        self.n = 0

    def inc(self, k=1):
        self.n += k
        return self.n



def emit_ln(nc, st, xT, xcast, nch, tok, post_bf16=True):
    ones_b = st["ones128_bf16"]
    ones_f = st["ones1_f32"]
    rows = st["lnrows"]
    sq = st["lnsq"]
    ab = st["lnab"]
    pool = st["psum_pool"]
    ps = pool[0:1, 0:2 * tok]
    pab = pool[:, 2 * tok:4 * tok].rearrange("p (a b) -> p a b", a=2)
    s_sq, s_fr = st["s_ln_sq"], st["s_ln_fr"]
    nsl = tok // 512
    inv_n = 1.0 / (nch * 128)

    with nc.Block() as blk:
        @blk.vector
        def _(vector):
            for k in range(nch):
                vector.tensor_copy(xcast[:, k, :], xT[:, k, :])

    with nc.Block() as blk:
        @blk.scalar
        def _(scalar):
            for k in range(nch):
                if k >= 2:
                    scalar.wait_ge(s_fr.h, s_fr.n + (k - 1))
                scalar.activation(sq[:, k % 2, :], xcast[:, k, :], AF.Square)
                _flush(scalar, st, "scalar").then_inc(s_sq.h, 1)

        @blk.tensor
        def _(tensor):
            for k in range(nch):
                tensor.wait_ge(s_sq.h, s_sq.n + (k + 1))
                for n in range(nsl):
                    tensor.matmul(ps[0:1, n * 512:(n + 1) * 512],
                                  ones_b[:, 0:1], xcast[:, k, n * 512:(n + 1) * 512],
                                  start=(k == 0), stop=(k == nch - 1))
                for n in range(nsl):
                    mmi = tensor.matmul(ps[0:1, tok + n * 512:tok + (n + 1) * 512],
                                        ones_b[:, 0:1], sq[:, k % 2, n * 512:(n + 1) * 512],
                                        start=(k == 0), stop=(k == nch - 1))
                mmi.then_inc(s_fr.h, 1)
        s_sq.inc(nch)
        s_fr.inc(nch)

    m_v = rows[0:1, 0:tok]
    r_v = rows[0:1, tok:2 * tok]
    t_v = rows[0:1, 2 * tok:3 * tok]
    with nc.Block() as blk:
        @blk.vector
        def _(vector):
            vector.tensor_scalar_mul(m_v, ps[0:1, 0:tok], inv_n)
            vector.tensor_scalar_mul(r_v, ps[0:1, tok:2 * tok], inv_n)
            vector.tensor_mul(t_v, m_v, m_v)
            vector.tensor_sub(r_v, r_v, t_v)
            vector.tensor_scalar_add(r_v, r_v, LN_EPS)
    with nc.Block() as blk:
        @blk.scalar
        def _(scalar):
            scalar.activation(t_v, r_v, AF.Sqrt)
    with nc.Block() as blk:
        @blk.vector
        def _(vector):
            vector.reciprocal(r_v, t_v)
            vector.tensor_mul(t_v, m_v, r_v)

    with nc.Block() as blk:
        @blk.tensor
        def _(tensor):
            for n in range(nsl):
                tensor.matmul(pab[:, 0, n * 512:(n + 1) * 512], ones_f[0:1, :],
                              r_v[0:1, n * 512:(n + 1) * 512], start=True, stop=True)
            for n in range(nsl):
                tensor.matmul(pab[:, 1, n * 512:(n + 1) * 512], ones_f[0:1, :],
                              t_v[0:1, n * 512:(n + 1) * 512], start=True, stop=True)


    with nc.Block() as blk:
        @blk.scalar
        def _(scalar):
            scalar.activation(ab[:, 0, :], pab[:, 0, :], AF.Copy)
            scalar.activation(ab[:, 1, :], pab[:, 1, :], AF.Copy)

    with nc.Block() as blk:
        half = nch // 2

        @blk.vector
        def _(vector):
            for k in range(half):
                vector.tensor_mul(xT[:, k, :], xT[:, k, :], ab[:, 0, :])
                vector.tensor_sub(xT[:, k, :], xT[:, k, :], ab[:, 1, :])
                if post_bf16:
                    vector.tensor_copy(xcast[:, k, :], xT[:, k, :])

        @blk.gpsimd
        def _(gp):
            for k in range(half, nch):
                gp.tensor_mul(xT[:, k, :], xT[:, k, :], ab[:, 0, :])
                gp.tensor_sub(xT[:, k, :], xT[:, k, :], ab[:, 1, :])
                if post_bf16:
                    gp.tensor_copy(xcast[:, k, :], xT[:, k, :])


def alloc_ln_state(nc, ctx, st, tok):
    e = ctx.enter_context
    st["lnrows"] = e(nc.sbuf_tensor("lnrows", [1, 3 * tok], F32))
    st["lnsq"] = e(nc.sbuf_tensor("lnsq", [128, 2, tok], BF16))
    st["lnab"] = e(nc.sbuf_tensor("lnab", [128, 2, tok], F32))
    st["s_ln_sq"] = Sem(ctx, nc, "s_ln_sq")
    st["s_ln_fr"] = Sem(ctx, nc, "s_ln_fr")


def alloc_consts(nc, ctx, st):
    e = ctx.enter_context
    st["ones128_bf16"] = e(nc.sbuf_tensor("ones128b", [128, 1], BF16))
    st["ones1_f32"] = e(nc.sbuf_tensor("ones1f", [1, 128], F32))
    st["ones1_bf16"] = e(nc.sbuf_tensor("ones1fb", [1, 128], BF16))
    st["junk"] = e(nc.sbuf_tensor("junk", [1, 8], F32))


_FLUSH_SLOT = {"scalar": 0, "vector": 1, "gpsimd": 2}


def _flush(E, st, eng):
    sl = _FLUSH_SLOT[eng]
    src = st["ones1_f32"][0:1, 0:1]
    dst = st["junk"][0:1, sl:sl + 1]
    if eng == "scalar":
        return E.copy(dst, src)
    return E.tensor_copy(dst, src)


def emit_consts_init(nc, st):
    with nc.Block() as blk:
        @blk.vector
        def _(vector):
            vector.memset(st["ones128_bf16"][:, :], 1.0)
            vector.memset(st["ones1_f32"][:, :], 1.0)
            vector.memset(st["ones1_bf16"][:, :], 1.0)



def build_tail():
    nc = bass.Bass("TRN2", target_bir_lowering=False)
    with ExitStack() as ctx:
        st = make_base(nc, ctx)
        emit_tail(nc, ctx, st, fused=False)
    return nc


def emit_tail(nc, ctx, st, fused):
    fac_d = nc.dram_tensor("facs", [KC, 128 * KC * 8 * 128], F32, kind="ExternalInput")
    bias_d = nc.dram_tensor("biasc", [128, KC], F32, kind="ExternalInput")
    out_d = nc.dram_tensor("out", [DM, T], F32, kind="ExternalOutput")
    if not fused:
        x3_d = nc.dram_tensor("x3t", [DM, T], F32, kind="ExternalInput")
    isq8 = 1.0 / math.sqrt(8.0)

    with ExitStack() as lctx:
        e = lctx.enter_context
        if not fused:
            _alloc_xT(nc, st)
        xT = st["xT"]
        xb = st["xb"]
        fst = e(nc.sbuf_tensor("d_fst", [128, KC, 4, 128], F32, side="right"))
        wsum = e(nc.sbuf_tensor("d_wsum", [128, KC, 128], F32, side="right"))
        wb16 = e(nc.sbuf_tensor("d_wb16", [128, KC, 128], BF16, side="right"))
        bias_s = e(nc.sbuf_tensor("d_bias_s", [128, KC], F32, side="right"))
        ev = e(nc.sbuf_tensor("d_ev", [128, 2, 512], F32, side="right"))
        psum_pool = st["psum_pool"]
        pm = psum_pool[:, 0:1024].rearrange("p (a b) -> p a b", a=2)
        sx = st["sctx"]
        s_ld = Sem(sx, nc, "d_ld")
        s_fs = Sem(sx, nc, "d_fs")
        s_w = Sem(sx, nc, "d_w")
        s_mm = Sem(sx, nc, "d_mm")
        s_ev = Sem(sx, nc, "d_ev")
        s_res = Sem(sx, nc, "d_res")
        s_st = Sem(sx, nc, "d_st")

        with nc.Block() as blk:
            @blk.sync
            def _(sync):
                if not fused:
                    sync.dma_start(xT[:, :, :],
                                   x3_d[:, :].rearrange("(k p) t -> p k t", p=128)
                                   ).then_inc(s_ld.h, 16)
                    s_ld.inc(16)
                sync.dma_start(bias_s[:, :], bias_d[:, :]).then_inc(s_ld.h, 16)
                sync.wait_ge(s_ld.h, s_ld.inc(16))

        if not fused:
            with nc.Block() as blk:
                @blk.vector
                def _(vector):
                    for k in range(KC):
                        vector.tensor_copy(xb[:, k, :], xT[:, k, :])

        ldb = s_ld.n
        with nc.Block() as blk:
            @blk.sync
            def _(sync, ldb=ldb):
                for m in range(KC):
                    if m >= 1:
                        sync.wait_ge(s_fs.h, 2 * m)
                    sync.dma_start(fst[:, :, :, :],
                                   fac_d[m, :].rearrange("(p k r c) -> p k r c",
                                                         p=128, k=KC, r=8)
                                   [:, :, 0:4, :]).then_inc(s_ld.h, 16)
                    sync.wait_ge(s_fs.h, 2 * m + 1)
                    sync.dma_start(fst[:, :, :, :],
                                   fac_d[m, :].rearrange("(p k r c) -> p k r c",
                                                         p=128, k=KC, r=8)
                                   [:, :, 4:8, :]).then_inc(s_ld.h, 16)
            s_ld.inc(32 * KC)

            @blk.vector
            def _(vector, ldb=ldb):
                for m in range(KC):
                    vector.wait_ge(s_ld.h, ldb + 32 * m + 16)
                    if m >= 1:
                        vector.wait_ge(s_mm.h, 2 * m)
                    vector.tensor_add(wsum[:, :, :], fst[:, :, 0, :], fst[:, :, 1, :])
                    for r in range(2, 4):
                        vector.tensor_add(wsum[:, :, :], wsum[:, :, :], fst[:, :, r, :])
                    _flush(vector, st, "vector").then_inc(s_fs.h, 1)
                    vector.wait_ge(s_ld.h, ldb + 32 * (m + 1))
                    for r in range(4):
                        vector.tensor_add(wsum[:, :, :], wsum[:, :, :], fst[:, :, r, :])
                    vector.tensor_copy(wb16[:, :, :], wsum[:, :, :])
                    _flush(vector, st, "vector").then_inc(s_fs.h, 1)
                    _flush(vector, st, "vector").then_inc(s_w.h, 1)
            s_fs.inc(2 * KC)
            s_w.inc(KC)

            @blk.tensor
            def _(tensor):
                for m in range(KC):
                    tensor.wait_ge(s_w.h, m + 1)
                    if m >= 1:
                        tensor.wait_ge(s_ev.h, 2 * m)
                    for n in range(2):
                        for k in range(KC):
                            mmi = tensor.matmul(pm[:, n, :], wb16[:, k, :],
                                                xb[:, k, n * 512:(n + 1) * 512],
                                                start=(k == 0), stop=(k == KC - 1))
                        mmi.then_inc(s_mm.h, 1)
            s_mm.inc(2 * KC)

            @blk.scalar
            def _(scalar):
                for m in range(KC):
                    if m >= 1:
                        scalar.wait_ge(s_res.h, m)
                    for n in range(2):
                        scalar.wait_ge(s_mm.h, 2 * m + n + 1)
                        scalar.activation(ev[:, n, :], pm[:, n, :], AF.Identity,
                                          bias=bias_s[:, m:m + 1], scale=isq8)
                    _flush(scalar, st, "scalar").then_inc(s_ev.h, 2)
            s_ev.inc(2 * KC)

            @blk.gpsimd
            def _(gp):
                for m in range(KC):
                    gp.wait_ge(s_ev.h, 2 * m + 2)
                    gp.tensor_add(xT[:, m, :], xT[:, m, :],
                                  ev[:, :, :].rearrange("p a b -> p (a b)"))
                    _flush(gp, st, "gpsimd").then_inc(s_res.h, 1)
            s_res.inc(KC)

        emit_ln(nc, st, xT, xb, KC, T, post_bf16=False)

        with nc.Block() as blk:
            @blk.sync
            def _(sync):
                sync.dma_start(out_d[:, :].rearrange("(k p) t -> p k t", p=128),
                               xT[:, :, :]).then_inc(s_st.h, 16)
                sync.wait_ge(s_st.h, s_st.inc(16))


def host_tail_inputs(x3_full, factors, bias):
    fac = np.ascontiguousarray(
        factors.reshape(8, KC, 128, KC, 128).transpose(3, 2, 1, 0, 4)
    ).reshape(KC, -1)
    biasc = np.ascontiguousarray(bias.reshape(KC, 128).T)
    maps = []
    for c in range(NCORE):
        b, h = c // 2, c % 2
        x3t = _feat_major(x3_full[b, h * T:(h + 1) * T, :])
        maps.append(dict(x3t=x3t, facs=fac, biasc=biasc))
    return maps


def gather_tail(results):
    out = np.empty((4, 2048, DM), np.float32)
    for c in range(NCORE):
        b, h = c // 2, c % 2
        out[b, h * T:(h + 1) * T, :] = results[c]["out"].T
    return out



class Plan:
    ENGINES = ("sync", "tensor", "vector", "scalar", "gpsimd")

    def __init__(self):
        self.q = {e: [] for e in self.ENGINES}

    def add(self, eng, fn):
        self.q[eng].append(fn)

    def emit(self, nc):
        with nc.Block() as blk:
            for eng in self.ENGINES:
                fns = self.q[eng]
                if fns:
                    getattr(blk, eng)(lambda E, fns=fns: [f(E) for f in fns])


def make_base(nc, ctx):
    st = {}
    alloc_consts(nc, ctx, st)
    alloc_ln_state(nc, ctx, st, T)
    st["psum_pool"] = ctx.enter_context(nc.psum_tensor("psum_pool", [128, 4096], F32))
    st["xb"] = ctx.enter_context(nc.sbuf_tensor("xbs", [128, KC, T], BF16))
    st["sctx"] = ctx
    emit_consts_init(nc, st)
    return st


def _alloc_xT(nc, st):
    g = nc.sbuf_tensor("xT%d" % st.setdefault("xT_id", 0), [128, KC, T], F32,
                       side="right")
    st["xT_id"] += 1
    st["xT_guard"] = g
    st["xT"] = g.__enter__()
    return st["xT"]


def _free_xT(st):
    st["xT_guard"].__exit__(None, None, None)
    st["xT"] = None



MOE_F = 1408
MOE_FC = MOE_F // 128
NEXP = 10


def build_moe(nexp=NEXP, do_ln=True, dbg=False):
    nc = bass.Bass("TRN2", target_bir_lowering=False)
    with ExitStack() as ctx:
        st = make_base(nc, ctx)
        emit_moe(nc, ctx, st, fused=False, nexp=nexp, do_ln=do_ln, dbg=dbg)
    return nc


def emit_moe(nc, ctx, st, fused, nexp=NEXP, do_ln=True, dbg=False):
    if not fused:
        x1_d = nc.dram_tensor("x1t", [DM, T], F32, kind="ExternalInput")
    gwt_d = nc.dram_tensor("gwt", [128, KC, 8], F32, kind="ExternalInput")
    gates_d = nc.dram_tensor("gates", [NEXP * MOE_FC, 128 * KC * 128], F32, kind="ExternalInput")
    ups_d = nc.dram_tensor("ups", [NEXP * MOE_FC, 128 * KC * 128], F32, kind="ExternalInput")
    downs_d = nc.dram_tensor("downs", [NEXP * KC, 128 * MOE_FC * 128], F32, kind="ExternalInput")
    idn_d = nc.dram_tensor("idn", [128, 128], F32, kind="ExternalInput")
    if not fused:
        out_d = nc.dram_tensor("out", [DM, T], F32, kind="ExternalOutput")
    scr_d = nc.dram_tensor("scr", [NEXP, T], F32, kind="Internal")

    with ExitStack() as lctx:
        e = lctx.enter_context
        if not fused:
            _alloc_xT(nc, st)
        xT = st["xT"]
        xb = st["xb"]
        wstg = e(nc.sbuf_tensor("b_wstg", [128, 2, KC * 128], F32, side="right"))
        wb = e(nc.sbuf_tensor("b_wb", [128, 2, KC * 128], BF16, side="right"))
        sg = e(nc.sbuf_tensor("b_sg", [128, MOE_FC, T], BF16, side="right"))
        uc = e(nc.sbuf_tensor("b_uc", [128, 2, T], BF16, side="right"))
        cbc = e(nc.sbuf_tensor("b_cbc", [128, NEXP, T], BF16, side="right"))
        idn = e(nc.sbuf_tensor("b_idn_s", [128, 128], F32, side="right"))
        gw_s = e(nc.sbuf_tensor("b_gw_s", [128, KC, 8], F32, side="right"))
        gw_b = e(nc.sbuf_tensor("b_gw_b", [128, KC, 8], BF16, side="right"))
        srow = e(nc.sbuf_tensor("b_srow", [8, T], F32, side="right"))
        combT = e(nc.sbuf_tensor("b_combT", [8, T], F32, side="right"))
        crow1 = e(nc.sbuf_tensor("b_crow1", [1, T], F32, side="right"))
        stok = e(nc.sbuf_tensor("b_stok", [128, 8, 8], F32, side="right"))
        ctok = e(nc.sbuf_tensor("b_ctok", [128, 8, 8], F32, side="right"))
        rtree = e(nc.sbuf_tensor("b_rtree", [128, 64], F32, side="right"))
        psum_pool = st["psum_pool"]

        sx = st["sctx"]
        s_ld = Sem(sx, nc, "b_ld")
        s_cast = Sem(sx, nc, "b_cast")
        s_mmg = Sem(sx, nc, "b_mmg")
        s_mmu = Sem(sx, nc, "b_mmu")
        s_mmd = Sem(sx, nc, "b_mmd")
        s_evg = Sem(sx, nc, "b_evg")
        s_evu = Sem(sx, nc, "b_evu")
        s_h = Sem(sx, nc, "b_h")
        s_acc = Sem(sx, nc, "b_acc")
        s_r = Sem(sx, nc, "b_r")
        s_st = Sem(sx, nc, "b_st")

        pg = psum_pool[:, 0:1024].rearrange("p (a b) -> p a b", a=2)
        pu = psum_pool[:, 1024:2048].rearrange("p (a b) -> p a b", a=2)
        pd = psum_pool[:, 2048:3072].rearrange("p (a b) -> p a b", a=2)

        with nc.Block() as blk:
            @blk.sync
            def _(sync):
                if not fused:
                    sync.dma_start(xT[:, :, :],
                                   x1_d[:, :].rearrange("(k p) t -> p k t", p=128)
                                   ).then_inc(s_ld.h, 16)
                    s_ld.inc(16)
                sync.dma_start(gw_s[:, :, :], gwt_d[:, :, :]).then_inc(s_ld.h, 16)
                sync.dma_start(idn[:, :], idn_d[:, :]).then_inc(s_ld.h, 16)
                sync.wait_ge(s_ld.h, s_ld.inc(32))

        with nc.Block() as blk:
            @blk.vector
            def _(vector):
                if not fused:
                    for k in range(KC):
                        vector.tensor_copy(xb[:, k, :], xT[:, k, :])
                vector.tensor_copy(gw_b[:, :, :], gw_s[:, :, :])

        psc = psum_pool[0:8, 1024:2048]
        with nc.Block() as blk:
            @blk.tensor
            def _(tensor):
                for n in range(2):
                    for k in range(KC):
                        mmi = tensor.matmul(psc[:, n * 512:(n + 1) * 512],
                                            gw_b[:, k, :], xb[:, k, n * 512:(n + 1) * 512],
                                            start=(k == 0), stop=(k == KC - 1))
                mmi.then_inc(s_r.h, 1)
            s_r.inc(1)

            @blk.scalar
            def _(scalar):
                scalar.wait_ge(s_r.h, 1)
                scalar.activation(srow[:, :], psc[:, :], AF.Identity)

        ptok = psum_pool[:, 0:64].rearrange("p (a b) -> p a b", a=8)
        with nc.Block() as blk:
            @blk.tensor
            def _(tensor):
                for j in range(8):
                    tensor.transpose(ptok[:, j, :], srow[:, j * 128:(j + 1) * 128],
                                     idn[0:8, 0:8])
            @blk.scalar
            def _(scalar):
                pass

        with nc.Block() as blk:
            @blk.scalar
            def _(scalar):
                for j in range(8):
                    scalar.activation(stok[:, j, :], ptok[:, j, :], AF.Identity)

        def vstep(fn):
            with nc.Block() as blk:
                blk.vector(fn)

        with nc.Block() as blk:
            @blk.scalar
            def _(scalar):
                for j in range(8):
                    scalar.activation(ctok[:, j, :], stok[:, j, :], AF.Exp)

        t1 = rtree[:, 0:32].rearrange("p (a b) -> p a b", a=8)
        t2 = rtree[:, 32:48].rearrange("p (a b) -> p a b", a=8)
        sums = rtree[:, 48:56]
        rs = rtree[:, 56:64]
        cs = ctok[:, :, :]
        ss = stok[:, :, :]
        vstep(lambda v: v.tensor_add(t1[:, :, :], cs[:, :, 0:4], cs[:, :, 4:8]))
        vstep(lambda v: v.tensor_add(t2[:, :, :], t1[:, :, 0:2], t1[:, :, 2:4]))
        vstep(lambda v: v.tensor_add(sums, t2[:, :, 0], t2[:, :, 1]))
        vstep(lambda v: v.reciprocal(rs, sums))
        vstep(lambda v: [v.tensor_scalar(cs[:, j, :], cs[:, j, :], rs[:, j:j + 1],
                                         None, ALU.mult) for j in range(8)])
        vstep(lambda v: v.tensor_max(t1[:, :, :], cs[:, :, 0:4], cs[:, :, 4:8]))
        vstep(lambda v: v.tensor_max(t2[:, :, :], t1[:, :, 0:2], t1[:, :, 2:4]))
        vstep(lambda v: v.tensor_max(sums, t2[:, :, 0], t2[:, :, 1]))
        vstep(lambda v: [v.tensor_scalar(ss[:, j, :], cs[:, j, :], sums[:, j:j + 1],
                                         None, ALU.is_lt) for j in range(8)])
        vstep(lambda v: v.tensor_mul(ss, ss, cs))
        vstep(lambda v: v.tensor_max(t1[:, :, :], ss[:, :, 0:4], ss[:, :, 4:8]))
        vstep(lambda v: v.tensor_max(t2[:, :, :], t1[:, :, 0:2], t1[:, :, 2:4]))
        vstep(lambda v: v.tensor_max(sums, t2[:, :, 0], t2[:, :, 1]))
        vstep(lambda v: [v.tensor_scalar(ss[:, j, :], cs[:, j, :], sums[:, j:j + 1],
                                         None, ALU.is_ge) for j in range(8)])
        vstep(lambda v: v.tensor_mul(cs, cs, ss))

        pct = psum_pool[0:8, 2048:3072].rearrange("p (a b) -> p a b", a=8)
        with nc.Block() as blk:
            @blk.tensor
            def _(tensor):
                for j in range(8):
                    tensor.transpose(pct[:, j, :], ctok[:, j, :], idn[:, :])
        with nc.Block() as blk:
            @blk.scalar
            def _(scalar):
                for j in range(8):
                    scalar.activation(combT[:, j * 128:(j + 1) * 128], pct[:, j, :],
                                      AF.Identity)
        with nc.Block() as blk:
            @blk.sync
            def _(sync):
                sync.dma_start(scr_d[0:8, :], combT[:, :]).then_inc(s_ld.h, 16)
                sync.wait_ge(s_ld.h, s_ld.inc(16))
            @blk.vector
            def _(vector):
                vector.memset(crow1[:, :], 1.0)
        with nc.Block() as blk:
            @blk.sync
            def _(sync):
                sync.dma_start(scr_d[8:9, :], crow1[:, :]).then_inc(s_ld.h, 16)
                sync.dma_start(scr_d[9:10, :], crow1[:, :]).then_inc(s_ld.h, 16)
                sync.wait_ge(s_ld.h, s_ld.inc(32))

        pcb = psum_pool[:, 0:1024]
        s_cbm = Sem(sx, nc, "b_cbm")
        s_cbe = Sem(sx, nc, "b_cbe")
        with nc.Block() as blk:
            @blk.sync
            def _(sync):
                for ex in range(NEXP):
                    if ex >= 1:
                        sync.wait_ge(s_cbm.h, 2 * ex)
                    sync.dma_start(crow1[:, :], scr_d[ex:ex + 1, :]).then_inc(s_ld.h, 16)
            base_ld = s_ld.n
            s_ld.inc(16 * NEXP)

            @blk.tensor
            def _(tensor):
                for ex in range(NEXP):
                    tensor.wait_ge(s_ld.h, base_ld + 16 * (ex + 1))
                    if ex >= 1:
                        tensor.wait_ge(s_cbe.h, ex)
                    for n in range(2):
                        tensor.matmul(pcb[:, n * 512:(n + 1) * 512], st["ones1_f32"][0:1, :],
                                      crow1[0:1, n * 512:(n + 1) * 512],
                                      start=True, stop=True).then_inc(s_cbm.h, 1)
            s_cbm.inc(2 * NEXP)

            @blk.scalar
            def _(scalar):
                for ex in range(NEXP):
                    scalar.wait_ge(s_cbm.h, 2 * (ex + 1))
                    scalar.activation(cbc[:, ex, :], pcb[:, :], AF.Identity)
                    _flush(scalar, st, "scalar").then_inc(s_cbe.h, 1)
            s_cbe.inc(NEXP)

        plan = Plan()
        stamp = {}
        sctr = [0]
        gctr = [0]
        uctr = [0]
        dctr = [0]
        ld_base = s_ld.n

        def emit_strip(dram, row, ncols_k, fam_sem, fam_ctr, psum_view, mm_rhs_fn,
                       evac_fn, evac_eng, evac_sem, pre_mm_waits=(), pre_evac_waits=()):
            s_i = sctr[0]; sctr[0] += 1
            fs = fam_ctr[0]; fam_ctr[0] += 1
            slot = s_i % 2
            ncol = ncols_k * 128

            def dma(sync, s_i=s_i, slot=slot, row=row, ncol=ncol, dram=dram,
                    ncols_k=ncols_k):
                if s_i >= 2:
                    sync.wait_ge(s_cast.h, s_i - 1)
                sync.dma_start(wstg[:, slot, 0:ncol],
                               dram[row, :].rearrange("(p k c) -> p (k c)", p=128,
                                                      k=ncols_k)
                               ).then_inc(s_ld.h, 16)
            plan.add("sync", dma)
            s_ld.inc(16)

            def cast(gp, s_i=s_i, slot=slot, ncol=ncol):
                gp.wait_ge(s_ld.h, ld_base + 16 * (s_i + 1))
                if s_i >= 2:
                    sem, cnt = stamp[s_i - 2]
                    gp.wait_ge(sem.h, cnt)
                gp.tensor_copy(wb[:, slot, 0:ncol], wstg[:, slot, 0:ncol])
                _flush(gp, st, "gpsimd").then_inc(s_cast.h, 1)
            plan.add("gpsimd", cast)
            s_cast.inc(1)

            def mms(tensor, s_i=s_i, slot=slot, ncols_k=ncols_k,
                    waits=tuple(pre_mm_waits)):
                tensor.wait_ge(s_cast.h, s_i + 1)
                for sem, cnt in waits:
                    if cnt > 0:
                        tensor.wait_ge(sem.h, cnt)
                wv = wb[:, slot, 0:ncols_k * 128].rearrange("p (k c) -> p k c",
                                                            k=ncols_k)
                for n in range(2):
                    for k in range(ncols_k):
                        mmi = tensor.matmul(psum_view[:, n, :], wv[:, k, :],
                                            mm_rhs_fn(k, n),
                                            start=(k == 0), stop=(k == ncols_k - 1))
                    mmi.then_inc(fam_sem.h, 1)
            plan.add("tensor", mms)
            fam_sem.inc(2)
            stamp[s_i] = (fam_sem, fam_sem.n)

            def evac(E, fs=fs, waits=tuple(pre_evac_waits)):
                for n in range(2):
                    E.wait_ge(fam_sem.h, 2 * fs + n + 1)
                    for sem, cnt in waits:
                        if cnt > 0:
                            E.wait_ge(sem.h, cnt)
                    evac_fn(E, n)
                if evac_sem is not None:
                    _flush(E, st, evac_eng).then_inc(evac_sem.h, 2)
            plan.add(evac_eng, evac)
            if evac_sem is not None:
                evac_sem.inc(2)

        for ex in range(nexp):
            for m in range(MOE_FC):
                def g_evac_f(m=m):
                    def f(E, n, m=m):
                        E.activation(sg[:, m, n * 512:(n + 1) * 512],
                                     pg[:, n, :], AF.Silu)
                    return f
                emit_strip(gates_d, ex * MOE_FC + m, KC, s_mmg, gctr, pg,
                           lambda k, n: xb[:, k, n * 512:(n + 1) * 512],
                           g_evac_f(), "scalar", s_evg,
                           pre_mm_waits=[(s_evg, 2 * gctr[0])],
                           pre_evac_waits=[(s_mmd, 32 * ex)] if ex >= 1 else [])

                def u_evac_f(ex=ex, m=m):
                    def f(E, n, ex=ex, m=m):
                        E.tensor_mul(uc[:, m % 2, n * 512:(n + 1) * 512],
                                     pu[:, n, :],
                                     cbc[:, ex, n * 512:(n + 1) * 512])
                    return f
                emit_strip(ups_d, ex * MOE_FC + m, KC, s_mmu, uctr, pu,
                           lambda k, n: xb[:, k, n * 512:(n + 1) * 512],
                           u_evac_f(), "vector", None,
                           pre_mm_waits=[(s_h, 11 * ex + m)])

                def hmul(vector, ex=ex, m=m, tg=s_evg.n):
                    vector.wait_ge(s_evg.h, tg)
                    vector.tensor_mul(sg[:, m, :], sg[:, m, :], uc[:, m % 2, :])
                    _flush(vector, st, "vector").then_inc(s_h.h, 1)
                plan.add("vector", hmul)
                s_h.inc(1)

            for m in range(KC):
                def d_evac_f(m=m):
                    def f(E, n, m=m):
                        E.scalar_tensor_tensor(xT[:, m, n * 512:(n + 1) * 512],
                                               pd[:, n, :], 1.0,
                                               xT[:, m, n * 512:(n + 1) * 512],
                                               ALU.mult, ALU.add)
                    return f
                pre_mm = [(s_acc, 2 * dctr[0])]
                if m == 0:
                    pre_mm.append((s_h, 11 * (ex + 1)))
                emit_strip(downs_d, ex * KC + m, MOE_FC, s_mmd, dctr, pd,
                           lambda k, n: sg[:, k, n * 512:(n + 1) * 512],
                           d_evac_f(), "vector", s_acc,
                           pre_mm_waits=pre_mm)

        plan.emit(nc)

        if do_ln:
            emit_ln(nc, st, xT, xb, KC, T, post_bf16=fused)
        if dbg:
            dbg_sc = nc.dram_tensor("dbg_sc", [8, T], F32, kind="ExternalOutput")
            dbg_cb = nc.dram_tensor("dbg_cb", [8, T], F32, kind="ExternalOutput")
            dbg_cbc = nc.dram_tensor("dbg_cbc", [128, NEXP, T], BF16, kind="ExternalOutput")
            dbg_sg = nc.dram_tensor("dbg_sg", [128, MOE_FC, T], BF16, kind="ExternalOutput")
            with nc.Block() as blk:
                @blk.sync
                def _(sync):
                    sync.dma_start(dbg_sc[:, :], srow[:, :]).then_inc(s_st.h, 16)
                    sync.dma_start(dbg_cb[:, :], combT[:, :]).then_inc(s_st.h, 16)
                    sync.dma_start(dbg_cbc[:, :, :], cbc[:, :, :]).then_inc(s_st.h, 16)
                    sync.dma_start(dbg_sg[:, :, :], sg[:, :, :]).then_inc(s_st.h, 16)
                    sync.wait_ge(s_st.h, s_st.inc(64))
        if not fused:
            with nc.Block() as blk:
                @blk.sync
                def _(sync):
                    sync.dma_start(out_d[:, :].rearrange("(k p) t -> p k t", p=128),
                                   xT[:, :, :]).then_inc(s_st.h, 16)
                    sync.wait_ge(s_st.h, s_st.inc(16))


def host_moe_inputs(x1_full, gate_weight, gate_proj, up_proj, down_proj,
                    shared_gate, shared_up, shared_down):
    gs = [_tile_w(gate_proj[e]) for e in range(8)]
    gs += [_tile_w(shared_gate[:, :MOE_F]), _tile_w(shared_gate[:, MOE_F:])]
    us = [_tile_w(up_proj[e]) for e in range(8)]
    us += [_tile_w(shared_up[:, :MOE_F]), _tile_w(shared_up[:, MOE_F:])]
    ds = [_tile_w(down_proj[e]) for e in range(8)]
    ds += [_tile_w(shared_down[:MOE_F, :]), _tile_w(shared_down[MOE_F:, :])]
    gates = np.concatenate(gs, 0).reshape(NEXP * MOE_FC, -1)
    ups = np.concatenate(us, 0).reshape(NEXP * MOE_FC, -1)
    downs = np.concatenate(ds, 0).reshape(NEXP * KC, -1)
    gwt = np.ascontiguousarray(
        gate_weight.T.reshape(KC, 128, 1, 8).transpose(1, 0, 3, 2))[:, :, :, 0]
    idn = np.eye(128, dtype=np.float32)
    maps = []
    for c in range(NCORE):
        b, h = c // 2, c % 2
        x1t = _feat_major(x1_full[b, h * T:(h + 1) * T, :])
        maps.append(dict(x1t=x1t, gwt=gwt, gates=gates, ups=ups, downs=downs,
                         idn=idn))
    return maps


def gather_std(results, key="out"):
    out = np.empty((4, 2048, DM), np.float32)
    for c in range(NCORE):
        b, h = c // 2, c % 2
        out[b, h * T:(h + 1) * T, :] = results[c][key].T
    return out



HKV = 4
NH = 16
SEQ = 2048
SCALE = 1.0 / math.sqrt(128.0)


def build_attn():
    nc = bass.Bass("TRN2", target_bir_lowering=False)
    with ExitStack() as ctx:
        st = make_base(nc, ctx)
        emit_attn(nc, ctx, st, fused=False)
    return nc


def emit_attn(nc, ctx, st, fused):
    xseq_d = nc.dram_tensor("xseq", [DM, SEQ], F32, kind="ExternalInput")
    xown_d = nc.dram_tensor("xown", [DM, T], F32, kind="ExternalInput")
    wq_d = nc.dram_tensor("wq", [KC, 128 * KC * 128], F32, kind="ExternalInput")
    wkt_d = nc.dram_tensor("wkt", [128, HKV * KC * 128], F32, kind="ExternalInput")
    wvt_d = nc.dram_tensor("wvt", [128, KC * 512], F32, kind="ExternalInput")
    wo_d = nc.dram_tensor("wo", [KC, 128 * KC * 128], F32, kind="ExternalInput")
    mask_d = nc.dram_tensor("mask", [9 * 128, 128], BF16, kind="ExternalInput")
    if not fused:
        out_d = nc.dram_tensor("out", [DM, T], F32, kind="ExternalOutput")

    with ExitStack() as lctx:
        e = lctx.enter_context
        xa = st["xb"]
        ones1_b = st["ones1_bf16"]
        psum_pool = st["psum_pool"]
        sx = st["sctx"]
        s_ld = Sem(sx, nc, "a_ld")
        s_cast = Sem(sx, nc, "a_cast")
        s_mm = Sem(sx, nc, "a_mm")
        s_ev = Sem(sx, nc, "a_ev")
        s_pe = Sem(sx, nc, "a_pe")
        s_pm = Sem(sx, nc, "a_pm")
        s_pv = Sem(sx, nc, "a_pv")
        s_tl = Sem(sx, nc, "a_tl")
        s_ao = Sem(sx, nc, "a_ao")
        s_acc = Sem(sx, nc, "a_acc")
        s_st = Sem(sx, nc, "a_st")

        def run_strips(wstg, wb, jobs):
            plan = Plan()
            ld_base = s_ld.n
            cast_base = s_cast.n
            mm_stamp = {}
            for s_i, jb in enumerate(jobs):
                slot = s_i % 2

                def dma(sync, s_i=s_i, slot=slot, jb=jb):
                    if s_i >= 2:
                        sync.wait_ge(s_cast.h, cast_base + s_i - 1)
                    sync.dma_start(wstg[:, slot, :],
                                   jb["dram"][jb["row"], :]
                                   .rearrange("(p x) -> p x", p=128)).then_inc(s_ld.h, 16)
                plan.add("sync", dma)
                s_ld.inc(16)

                def cast(gp, s_i=s_i, slot=slot):
                    gp.wait_ge(s_ld.h, ld_base + 16 * (s_i + 1))
                    if s_i >= 2:
                        gp.wait_ge(s_mm.h, mm_stamp[s_i - 2])
                    gp.tensor_copy(wb[:, slot, :], wstg[:, slot, :])
                    _flush(gp, st, "gpsimd").then_inc(s_cast.h, 1)
                plan.add("gpsimd", cast)
                s_cast.inc(1)

                ev_base = s_ev.n

                def mms(tensor, s_i=s_i, slot=slot, jb=jb, ev_base=ev_base):
                    tensor.wait_ge(s_cast.h, cast_base + s_i + 1)
                    if s_i >= 1:
                        tensor.wait_ge(s_ev.h, ev_base)
                    for sem, cnt in jb.get("pre_mm", ()):
                        if cnt > 0:
                            tensor.wait_ge(sem.h, cnt)
                    wv_ = wb[:, slot, :].rearrange("p (k c) -> p k c", k=KC)
                    for n in range(2):
                        for k in range(KC):
                            mmi = tensor.matmul(jb["psv"][:, n, :], wv_[:, k, :],
                                                jb["rhs_fn"](k, n),
                                                start=(k == 0), stop=(k == KC - 1))
                        mmi.then_inc(s_mm.h, 1)
                plan.add("tensor", mms)
                s_mm.inc(2)
                mm_stamp[s_i] = s_mm.n

                def evac(E, jb=jb, mm_base=s_mm.n - 2):
                    for n in range(2):
                        E.wait_ge(s_mm.h, mm_base + n + 1)
                        jb["evac_fn"](E, n)
                    _flush(E, st, jb.get("evac_eng", "scalar")).then_inc(s_ev.h, 1)
                plan.add(jb.get("evac_eng", "scalar"), evac)
                s_ev.inc(1)
            plan.emit(nc)

        pq = psum_pool[:, 0:1024].rearrange("p (a b) -> p a b", a=2)

        with ExitStack() as actx:
            ea = actx.enter_context
            qT = ea(nc.sbuf_tensor("a_qT", [128, NH, T], BF16, side="right"))
            kT = ea(nc.sbuf_tensor("a_kT", [128, HKV, SEQ], BF16, side="right"))
            vtok = ea(nc.sbuf_tensor("a_vtok", [128, 16, 512], BF16, side="right"))

            with ExitStack() as qctx:
                eq = qctx.enter_context
                wstg = eq(nc.sbuf_tensor("a_wstg", [128, 2, KC * 128], F32, side="right"))
                wb = eq(nc.sbuf_tensor("a_wb", [128, 2, KC * 128], BF16, side="right"))

                for g in range(8):
                    with nc.Block() as blk:
                        @blk.sync
                        def _(sync, g=g):
                            sync.dma_start(
                                wstg[:, :, 0:T].rearrange("p s t -> p s t"),
                                xown_d[:, :].rearrange("(k p) t -> p k t", p=128)
                                [:, 2 * g:2 * g + 2, :]).then_inc(s_ld.h, 16)
                            sync.wait_ge(s_ld.h, s_ld.inc(16))
                    with nc.Block() as blk:
                        @blk.vector
                        def _(vector, g=g):
                            vector.tensor_copy(xa[:, 2 * g:2 * g + 2, :],
                                               wstg[:, :, 0:T])

                jobs = []
                for m in range(NH):
                    def q_ev(m=m):
                        def f(E, n, m=m):
                            E.activation(qT[:, m, n * 512:(n + 1) * 512],
                                         pq[:, n, :], AF.Identity)
                        return f
                    jobs.append(dict(dram=wq_d, row=m, psv=pq,
                                     rhs_fn=lambda k, n: xa[:, k, n * 512:(n + 1) * 512],
                                     evac_fn=q_ev()))
                run_strips(wstg, wb, jobs)

            pk = psum_pool[:, 1024:1536].unsqueeze(1)
            pv = psum_pool[:, 1536:2048].unsqueeze(1)
            with ExitStack() as kctx:
                ek = kctx.enter_context
                sstg = ek(nc.sbuf_tensor("a_sstg", [128, KC, 256], F32, side="right"))
                xsq = ek(nc.sbuf_tensor("a_xsq", [128, KC, 256], BF16, side="right"))
                wkb = ek(nc.sbuf_tensor("a_wkb", [128, HKV, KC, 128], BF16, side="right"))
                wvb = ek(nc.sbuf_tensor("a_wvb", [128, KC, 512], BF16, side="right"))

                for half in range(2):
                    with nc.Block() as blk:
                        @blk.sync
                        def _(sync, half=half):
                            sync.dma_start(
                                sstg[:, :, :].rearrange("p k c -> p (k c)"),
                                wkt_d[:, half * 4096:(half + 1) * 4096]
                                ).then_inc(s_ld.h, 16)
                            sync.wait_ge(s_ld.h, s_ld.inc(16))
                    with nc.Block() as blk:
                        @blk.vector
                        def _(vector, half=half):
                            vector.tensor_copy(
                                wkb[:, 2 * half:2 * half + 2, :, :],
                                sstg[:, :, :].rearrange("p k c -> p (k c)")
                                .rearrange("p (h k c) -> p h k c", h=2, k=KC))
                for half in range(2):
                    with nc.Block() as blk:
                        @blk.sync
                        def _(sync, half=half):
                            sync.dma_start(
                                sstg[:, :, :].rearrange("p k c -> p (k c)"),
                                wvt_d[:, half * 4096:(half + 1) * 4096]
                                ).then_inc(s_ld.h, 16)
                            sync.wait_ge(s_ld.h, s_ld.inc(16))
                    with nc.Block() as blk:
                        @blk.vector
                        def _(vector, half=half):
                            vector.tensor_copy(
                                wvb[:, 8 * half:8 * half + 8, :],
                                sstg[:, :, :].rearrange("p k c -> p (k c)")
                                .rearrange("p (k c) -> p k c", k=8))

                for eth in range(8):
                    with nc.Block() as blk:
                        @blk.sync
                        def _(sync, eth=eth):
                            sync.dma_start(
                                sstg[:, :, :],
                                xseq_d[:, :].rearrange("(k p) t -> p k t", p=128)
                                [:, :, eth * 256:(eth + 1) * 256]).then_inc(s_ld.h, 16)
                            sync.wait_ge(s_ld.h, s_ld.inc(16))
                    with nc.Block() as blk:
                        @blk.vector
                        def _(vector):
                            vector.tensor_copy(xsq[:, :, :], sstg[:, :, :])
                    mm_base = s_mm.n
                    ev_base = s_ev.n
                    with nc.Block() as blk:
                        @blk.tensor
                        def _(tensor, eth=eth, mm_base=mm_base, ev_base=ev_base):
                            for m in range(HKV):
                                if m >= 1:
                                    tensor.wait_ge(s_ev.h, ev_base + m)
                                for k in range(KC):
                                    mmi = tensor.matmul(pk[:, 0, 0:256],
                                                        wkb[:, m, k, :],
                                                        xsq[:, k, :],
                                                        start=(k == 0), stop=(k == KC - 1))
                                mmi.then_inc(s_mm.h, 1)
                            for tt in range(2):
                                if tt >= 1:
                                    tensor.wait_ge(s_ev.h, ev_base + HKV + tt)
                                for k in range(KC):
                                    mmi = tensor.matmul(
                                        pv[:, 0, :],
                                        xsq[:, k, tt * 128:(tt + 1) * 128],
                                        wvb[:, k, :], start=(k == 0), stop=(k == KC - 1))
                                mmi.then_inc(s_mm.h, 1)

                        @blk.scalar
                        def _(scalar, eth=eth, mm_base=mm_base):
                            for m in range(HKV):
                                scalar.wait_ge(s_mm.h, mm_base + m + 1)
                                scalar.activation(
                                    kT[:, m, eth * 256:(eth + 1) * 256],
                                    pk[:, 0, 0:256], AF.Identity)
                                _flush(scalar, st, "scalar").then_inc(s_ev.h, 1)
                            for tt in range(2):
                                scalar.wait_ge(s_mm.h, mm_base + HKV + tt + 1)
                                scalar.activation(vtok[:, eth * 2 + tt, :],
                                                  pv[:, 0, :], AF.Identity)
                                _flush(scalar, st, "scalar").then_inc(s_ev.h, 1)
                    s_mm.inc(HKV + 2)
                    s_ev.inc(HKV + 2)

            negC = ea(nc.sbuf_tensor("a_negC", [1, 4, T], BF16, side="right"))
            sq2 = ea(nc.sbuf_tensor("a_sq2", [128, 2048], BF16, side="right"))
            qrow = ea(nc.sbuf_tensor("a_qrow", [1, 4, T], BF16, side="right"))
            masks = ea(nc.sbuf_tensor("a_masks", [128, 9, 128], BF16, side="right"))
            pbuf = ea(nc.sbuf_tensor("a_pbuf", [128, 2, 512], BF16, side="right"))
            rlb = ea(nc.sbuf_tensor("a_rlb", [128, 512], F32, side="right"))
            rl = ea(nc.sbuf_tensor("a_rl", [1, 512], F32, side="right"))
            kmaxs = ea(nc.sbuf_tensor("a_kmaxs", [1, 8], F32, side="right"))
            with nc.Block() as blk:
                @blk.sync
                def _(sync):
                    sync.dma_start(masks[:, :, :],
                                   mask_d[:, :].rearrange("(r p) c -> p r c", p=128)
                                   ).then_inc(s_ld.h, 16)
                    sync.wait_ge(s_ld.h, s_ld.inc(16))
            def emit_cprime(kvh):
                with nc.Block() as blk:
                    @blk.scalar
                    def _(scalar, kvh=kvh):
                        scalar.activation(sq2[:, :], kT[:, kvh, :], AF.Square)
                with nc.Block() as blk:
                    @blk.tensor
                    def _(tensor):
                        for n in range(4):
                            tensor.matmul(psum_pool[0:1, n * 512:(n + 1) * 512],
                                          st["ones128_bf16"][:, 0:1],
                                          sq2[:, n * 512:(n + 1) * 512],
                                          start=True, stop=True)
                with nc.Block() as blk:
                    @blk.vector
                    def _(vector):
                        vector.tensor_reduce(kmaxs[0:1, 0:1], psum_pool[0:1, 0:2048],
                                             AX.X, ALU.max)
                with nc.Block() as blk:
                    @blk.scalar
                    def _(scalar):
                        scalar.activation(kmaxs[0:1, 1:2], kmaxs[0:1, 0:1], AF.Sqrt)
                with nc.Block() as blk:
                    @blk.scalar
                    def _(scalar, kvh=kvh):
                        for i in range(4):
                            scalar.activation(sq2[:, i * 512:(i + 1) * 512],
                                              qT[:, 4 * kvh + i, 0:512], AF.Square)
                with nc.Block() as blk:
                    @blk.tensor
                    def _(tensor):
                        for i in range(4):
                            tensor.matmul(psum_pool[0:1, i * 512:(i + 1) * 512],
                                          st["ones128_bf16"][:, 0:1],
                                          sq2[:, i * 512:(i + 1) * 512],
                                          start=True, stop=True)
                with nc.Block() as blk:
                    @blk.scalar
                    def _(scalar, kvh=kvh):
                        for i in range(4):
                            scalar.activation(sq2[:, i * 512:(i + 1) * 512],
                                              qT[:, 4 * kvh + i, 512:1024], AF.Square)
                with nc.Block() as blk:
                    @blk.tensor
                    def _(tensor):
                        for i in range(4):
                            tensor.matmul(
                                psum_pool[0:1, 2048 + i * 512:2048 + (i + 1) * 512],
                                st["ones128_bf16"][:, 0:1],
                                sq2[:, i * 512:(i + 1) * 512], start=True, stop=True)
                with nc.Block() as blk:
                    @blk.scalar
                    def _(scalar):
                        for i in range(4):
                            scalar.activation(qrow[0:1, i, 0:512],
                                              psum_pool[0:1, i * 512:(i + 1) * 512],
                                              AF.Sqrt)
                            scalar.activation(
                                qrow[0:1, i, 512:1024],
                                psum_pool[0:1, 2048 + i * 512:2048 + (i + 1) * 512],
                                AF.Sqrt)
                with nc.Block() as blk:
                    @blk.vector
                    def _(vector):
                        for i in range(4):
                            vector.tensor_scalar(negC[0:1, i, :], qrow[0:1, i, :],
                                                 kmaxs[0:1, 1:2], -1.0,
                                                 ALU.mult, ALU.mult)

            psS = psum_pool[:, 0:1024].rearrange("p (a b) -> p a b", a=2)
            psL = psum_pool[0:1, 1024:1536]
            psO = psum_pool[:, 1536:2048]
            psRB = psum_pool[:, 2048:2560]
            for kvh in range(HKV):
                emit_cprime(kvh)
                with nc.Block() as blk:
                    tjobs, sjobs, vjobs = [], [], []
                    for j in range(8):
                        nch = j + 9
                        qrhs = qT[:, 4 * kvh:4 * kvh + 4, j * 128:(j + 1) * 128]
                        crhs = negC[0:1, 0:4, j * 128:(j + 1) * 128]
                        mm_base = s_mm.n
                        pe_base = s_pe.n
                        pm_base = s_pm.n
                        pv_base = s_pv.n
                        ao_base = s_ao.n
                        tl_base = s_tl.n

                        def tjob(tensor, kvh=kvh, j=j, nch=nch, qrhs=qrhs, crhs=crhs,
                                 mm_base=mm_base, pe_base=pe_base, pm_base=pm_base,
                                 ao_base=ao_base, tl_base=tl_base):
                            def lpv(c):
                                if c >= j:
                                    tensor.wait_ge(s_pm.h, pm_base + (c - j) + 1)
                                else:
                                    tensor.wait_ge(s_pe.h, pe_base + c + 1)
                                if c == 0 and ao_base > 0:
                                    tensor.wait_ge(s_ao.h, ao_base)
                                tensor.matmul(psL[0:1, :], st["ones128_bf16"][:, 0:1],
                                              pbuf[:, c % 2, :],
                                              start=(c == 0), stop=(c == nch - 1))
                                tensor.matmul(psO[:, :],
                                              vtok[:, c, kvh * 128:(kvh + 1) * 128],
                                              pbuf[:, c % 2, :],
                                              start=(c == 0), stop=(c == nch - 1)
                                              ).then_inc(s_pv.h, 1)
                            for c in range(nch):
                                if c >= 2:
                                    tensor.wait_ge(s_pe.h, pe_base + c - 1)
                                elif c == 0 and pe_base > 0:
                                    tensor.wait_ge(s_pe.h, pe_base)
                                tensor.matmul(psS[:, c % 2, :],
                                              kT[:, kvh, c * 128:(c + 1) * 128],
                                              qrhs, start=True, stop=False)
                                tensor.matmul(psS[:, c % 2, :], ones1_b[0:1, :],
                                              crhs, start=False, stop=True
                                              ).then_inc(s_mm.h, 1)
                                if c >= 1:
                                    lpv(c - 1)
                            lpv(nch - 1)
                            tensor.wait_ge(s_tl.h, tl_base + 1)
                            tensor.matmul(psRB[:, :], st["ones1_f32"][0:1, :],
                                          rl[0:1, :], start=True, stop=True
                                          ).then_inc(s_tl.h, 1)
                        tjobs.append(tjob)
                        s_mm.inc(nch)
                        s_pv.inc(nch)

                        def sjob(scalar, nch=nch, mm_base=mm_base, pv_base=pv_base,
                                 tl_base=tl_base):
                            for c in range(nch):
                                scalar.wait_ge(s_mm.h, mm_base + c + 1)
                                if c >= 2:
                                    scalar.wait_ge(s_pv.h, pv_base + c - 1)
                                scalar.activation(pbuf[:, c % 2, :], psS[:, c % 2, :],
                                                  AF.Exp, scale=SCALE)
                                _flush(scalar, st, "scalar").then_inc(s_pe.h, 1)
                            scalar.wait_ge(s_tl.h, tl_base + 2)
                            scalar.activation(rlb[:, :], psRB[:, :], AF.Identity)
                            _flush(scalar, st, "scalar").then_inc(s_tl.h, 1)
                        sjobs.append(sjob)
                        s_pe.inc(nch)

                        def vjob(vector, kvh=kvh, j=j, nch=nch, pe_base=pe_base,
                                 pv_base=pv_base, tl_base=tl_base):
                            for c in range(j, nch):
                                vector.wait_ge(s_pe.h, pe_base + c + 1)
                                r = c - j
                                vector.tensor_mul(
                                    pbuf[:, c % 2, :].rearrange("p (a b) -> p a b", a=4),
                                    pbuf[:, c % 2, :].rearrange("p (a b) -> p a b", a=4),
                                    masks[:, r, :].unsqueeze(1).broadcast_to([128, 4, 128]))
                                _flush(vector, st, "vector").then_inc(s_pm.h, 1)
                            vector.wait_ge(s_pv.h, pv_base + nch)
                            vector.reciprocal(rl[0:1, :], psL[0:1, :])
                            _flush(vector, st, "vector").then_inc(s_tl.h, 1)
                            vector.wait_ge(s_tl.h, tl_base + 3)
                            vector.tensor_mul(
                                xa[:, 4 * kvh:4 * kvh + 4, j * 128:(j + 1) * 128],
                                psO[:, :].rearrange("p (a b) -> p a b", a=4),
                                rlb[:, :].rearrange("p (a b) -> p a b", a=4))
                            _flush(vector, st, "vector").then_inc(s_ao.h, 1)
                        vjobs.append(vjob)
                        s_pm.inc(nch - j)
                        s_tl.inc(3)
                        s_ao.inc(1)

                    blk.tensor(lambda E, fns=tjobs: [f(E) for f in fns])
                    blk.scalar(lambda E, fns=sjobs: [f(E) for f in fns])
                    blk.vector(lambda E, fns=vjobs: [f(E) for f in fns])

        _alloc_xT(nc, st)
        with ExitStack() as wctx:
            ew = wctx.enter_context
            xT = st["xT"]
            wstg2 = ew(nc.sbuf_tensor("a_wstg2", [128, 2, KC * 128], F32, side="right"))
            wb2 = ew(nc.sbuf_tensor("a_wb2", [128, 2, KC * 128], BF16, side="right"))

            with nc.Block() as blk:
                @blk.sync
                def _(sync):
                    sync.dma_start(xT[:, :, :],
                                   xown_d[:, :].rearrange("(k p) t -> p k t", p=128)
                                   ).then_inc(s_ld.h, 16)
                    sync.wait_ge(s_ld.h, s_ld.inc(16))

            jobs = []
            for m in range(KC):
                def wo_ev(m=m):
                    def f(E, n, m=m):
                        E.scalar_tensor_tensor(xT[:, m, n * 512:(n + 1) * 512],
                                               pq[:, n, :], 1.0,
                                               xT[:, m, n * 512:(n + 1) * 512],
                                               ALU.mult, ALU.add)
                    return f
                jobs.append(dict(dram=wo_d, row=m, psv=pq,
                                 rhs_fn=lambda k, n: xa[:, k, n * 512:(n + 1) * 512],
                                 evac_fn=wo_ev(), evac_eng="vector"))
            run_strips(wstg2, wb2, jobs)

            emit_ln(nc, st, xT, xa, KC, T, post_bf16=fused)

            if not fused:
                with nc.Block() as blk:
                    @blk.sync
                    def _(sync):
                        sync.dma_start(out_d[:, :].rearrange("(k p) t -> p k t", p=128),
                                       xT[:, :, :]).then_inc(s_st.h, 16)
                        sync.wait_ge(s_st.h, s_st.inc(16))


def host_attn_inputs(x_full, wq, wk, wv, wo):
    wq_t = _tile_w(wq).reshape(KC, -1)
    wo_t = _tile_w(wo).reshape(KC, -1)
    wkt = np.ascontiguousarray(
        wk.reshape(KC, 128, HKV, 128).transpose(1, 2, 0, 3)).reshape(128, -1)
    wvt = np.ascontiguousarray(
        wv.reshape(KC, 128, 512).transpose(1, 0, 2)).reshape(128, -1)
    maps = []
    for c in range(NCORE):
        b, h = c // 2, c % 2
        xseq = _feat_major(x_full[b])
        xown = np.ascontiguousarray(xseq[:, h * T:(h + 1) * T])
        import ml_dtypes
        mask = np.zeros((9, 128, 128), np.float32)
        for r in range(9):
            if r < 8 * h:
                mask[r, :, :] = 1.0
            elif r == 8 * h:
                mask[r] = np.triu(np.ones((128, 128), np.float32))
        mask = mask.astype(ml_dtypes.bfloat16)
        maps.append(dict(xseq=xseq, xown=xown, wq=wq_t, wkt=wkt, wvt=wvt, wo=wo_t,
                         mask=mask.reshape(9 * 128, 128)))
    return maps



DIN = 4096
NHD = 64
HD = 64
DST = 128
CONVD = 4352
XBC_CH = 34
ZB = 8
NCH = 8


def build_ssd(dbg=False):
    nc = bass.Bass("TRN2", target_bir_lowering=False)
    with ExitStack() as ctx:
        st = make_base(nc, ctx)
        emit_ssd(nc, ctx, st, fused=False)
    return nc


def emit_ssd(nc, ctx, st, fused):
    if fused:
        x2_d = nc.dram_tensor("xspill", [DM, T], F32, kind="Internal")
    else:
        x2_d = nc.dram_tensor("x2t", [DM, T], F32, kind="ExternalInput")
    win_d = nc.dram_tensor("win", [67, 128 * KC * 128], F32, kind="ExternalInput")
    wout_d = nc.dram_tensor("wout", [KC, 128 * 32 * 128], F32, kind="ExternalInput")
    cw_d = nc.dram_tensor("cw", [128, XBC_CH * 4], F32, kind="ExternalInput")
    cb_d = nc.dram_tensor("cb", [128, XBC_CH], F32, kind="ExternalInput")
    dtb_d = nc.dram_tensor("dtb", [1, NHD], F32, kind="ExternalInput")
    alog_d = nc.dram_tensor("alog", [1, NHD], F32, kind="ExternalInput")
    dp_d = nc.dram_tensor("dp", [1, DIN], F32, kind="ExternalInput")
    nw_d = nc.dram_tensor("nw", [128, 32], F32, kind="ExternalInput")
    idn_d = nc.dram_tensor("idnf", [128, 128], F32, kind="ExternalInput")
    idnb_d = nc.dram_tensor("idnb", [128, 128], BF16, kind="ExternalInput")
    tri_d = nc.dram_tensor("tri", [128, 128], BF16, kind="ExternalInput")
    flags_d = nc.dram_tensor("flags", [128, 2], F32, kind="ExternalInput")
    if not fused:
        out_d = nc.dram_tensor("out", [DM, T], F32, kind="ExternalOutput")
    zscr = nc.dram_tensor("zscr", [T, DIN], BF16, kind="Internal")
    xscr = nc.dram_tensor("xscr", [DIN, T], BF16, kind="Internal")
    yscr = nc.dram_tensor("yscr", [T, DIN], BF16, kind="Internal")
    acscr = nc.dram_tensor("acscr", [NCH, 2, NHD, 128], F32, kind="Internal")
    prescr = nc.dram_tensor("prescr", [NHD, NCH], F32, kind="Internal")
    sumscr = nc.dram_tensor("sumscr", [NHD, NCH], F32, kind="Internal")
    cc_in = nc.dram_tensor("cc_in", [128, DIN + 2], F32, kind="Internal")
    cc_out = nc.dram_tensor("cc_out", [256, DIN + 2], F32, kind="Internal")
    edscr = nc.dram_tensor("edscr", [128, XBC_CH * 3], F32, kind="Internal")
    edscr2 = nc.dram_tensor("edscr2", [256, XBC_CH * 3], F32, kind="Internal")

    with ExitStack() as lctx:
        e = lctx.enter_context
        ones1_b = st["ones1_bf16"]
        xb = st["xb"]
        psum_pool = st["psum_pool"]
        idn = e(nc.sbuf_tensor("c_idn_s", [128, 128], F32))
        idnb = e(nc.sbuf_tensor("c_idnb_s", [128, 128], BF16))
        tri = e(nc.sbuf_tensor("c_tri_s", [128, 128], BF16))
        bT = e(nc.sbuf_tensor("c_bT", [128, T], BF16))
        cT = e(nc.sbuf_tensor("c_cT", [128, T], BF16))
        dt_tok = e(nc.sbuf_tensor("c_dt_tok", [128, NCH, NHD], F32))
        acum_tok = e(nc.sbuf_tensor("c_acum_tok", [128, NCH, NHD], F32))
        rows = e(nc.sbuf_tensor("c_rows_c", [1, 4 * NHD], F32))
        nw_cols = e(nc.sbuf_tensor("c_nw_cols", [128, 32], F32))
        flags = e(nc.sbuf_tensor("c_flags_s", [128, 2], F32))

        sx = st["sctx"]
        s_ld = Sem(sx, nc, "c_ld")
        s_cast = Sem(sx, nc, "c_cast")
        s_mm = Sem(sx, nc, "c_mm")
        s_ev = Sem(sx, nc, "c_ev")
        s_cc = Sem(sx, nc, "c_cc")
        s_pm_c = Sem(sx, nc, "c_pm_c")
        s_st = Sem(sx, nc, "c_st")

        if fused:
            with nc.Block() as blk:
                @blk.sync
                def _(sync):
                    sync.dma_start(x2_d[:, :].rearrange("(k p) t -> p k t", p=128),
                                   st["xT"][:, :, :]).then_inc(s_ld.h, 16)
                    sync.wait_ge(s_ld.h, s_ld.inc(16))
            _free_xT(st)

        with nc.Block() as blk:
            @blk.sync
            def _(sync):
                for dst, src in [(idn[:, :], idn_d[:, :]), (idnb[:, :], idnb_d[:, :]),
                                 (tri[:, :], tri_d[:, :]), (nw_cols[:, :], nw_d[:, :]),
                                 (flags[:, :], flags_d[:, :]),
                                 (rows[0:1, 0:NHD], dtb_d[:, :]),
                                 (rows[0:1, NHD:2 * NHD], alog_d[:, :])]:
                    sync.dma_start(dst, src).then_inc(s_ld.h, 16)
                sync.wait_ge(s_ld.h, s_ld.inc(16 * 7))
        if not fused:
            with ExitStack() as xctx:
                xstg = xctx.enter_context(
                    nc.sbuf_tensor("c_xstg", [128, 2, T], F32, side="right"))
                for g in range(8):
                    with nc.Block() as blk:
                        @blk.sync
                        def _(sync, g=g):
                            sync.dma_start(xstg[:, :, :],
                                           x2_d[:, :].rearrange("(k p) t -> p k t",
                                                                p=128)
                                           [:, 2 * g:2 * g + 2, :]).then_inc(s_ld.h, 16)
                            sync.wait_ge(s_ld.h, s_ld.inc(16))
                    with nc.Block() as blk:
                        @blk.vector
                        def _(vector, g=g):
                            vector.tensor_copy(xb[:, 2 * g:2 * g + 2, :],
                                               xstg[:, :, :])
        with ExitStack() as zctx:
            ez = zctx.enter_context
            wstg = ez(nc.sbuf_tensor("c_wstg", [128, 2, 4, KC, 128], F32, side="right"))
            wbz = ez(nc.sbuf_tensor("c_wbz", [128, 2, 4, KC, 128], BF16, side="right"))
            szb = ez(nc.sbuf_tensor("c_szb", [128, 2, 512], BF16, side="right"))
            pz = psum_pool[:, 0:1024].rearrange("p (a b) -> p a b", a=2)
            plan = Plan()
            ld_base = s_ld.n
            cast_base = s_cast.n
            mm_base0 = s_mm.n
            ev_base0 = s_ev.n
            st_base = s_st.n
            mm_stamp = {}
            for zg in range(ZB):
                def dma(sync, zg=zg):
                    if zg >= 2:
                        sync.wait_ge(s_cast.h, cast_base + zg - 1)
                    for i in range(4):
                        sync.dma_start(wstg[:, zg % 2, i, :, :],
                                       win_d[4 * zg + i, :]
                                       .rearrange("(p x) -> p x", p=128)
                                       .rearrange("p (k c) -> p k c", k=KC)
                                       ).then_inc(s_ld.h, 16)
                plan.add("sync", dma)
                s_ld.inc(64)

                def cast(gp, zg=zg):
                    gp.wait_ge(s_ld.h, ld_base + 64 * (zg + 1))
                    if zg >= 2:
                        gp.wait_ge(s_mm.h, mm_stamp[zg - 2])
                    gp.tensor_copy(wbz[:, zg % 2, :, :, :], wstg[:, zg % 2, :, :, :])
                    _flush(gp, st, "gpsimd").then_inc(s_cast.h, 1)
                plan.add("gpsimd", cast)
                s_cast.inc(1)

                def mms(tensor, zg=zg, ev_base=s_ev.n):
                    tensor.wait_ge(s_cast.h, cast_base + zg + 1)
                    for tt in range(8):
                        if not (zg == 0 and tt == 0):
                            tensor.wait_ge(s_ev.h, ev_base0 + (zg * 8 + tt) - 1)
                        for k in range(KC):
                            mmi = tensor.matmul(
                                pz[:, tt % 2, :],
                                xb[:, k, tt * 128:(tt + 1) * 128],
                                wbz[:, zg % 2, :, k, :],
                                start=(k == 0), stop=(k == KC - 1))
                        mmi.then_inc(s_mm.h, 1)
                plan.add("tensor", mms)
                s_mm.inc(8)
                mm_stamp[zg] = s_mm.n

                def evz(scalar, zg=zg):
                    for tt in range(8):
                        scalar.wait_ge(s_mm.h, mm_base0 + zg * 8 + tt + 1)
                        if zg * 8 + tt >= 2:
                            scalar.wait_ge(s_st.h, st_base + 16 * (zg * 8 + tt - 1))
                        scalar.activation(szb[:, tt % 2, :], pz[:, tt % 2, :], AF.Silu)
                        _flush(scalar, st, "scalar").then_inc(s_ev.h, 1)
                plan.add("scalar", evz)
                s_ev.inc(8)

                def stz(sync, zg=zg):
                    for tt in range(8):
                        sync.wait_ge(s_ev.h, ev_base0 + zg * 8 + tt + 1)
                        sync.dma_start(
                            zscr[tt * 128:(tt + 1) * 128, zg * 512:(zg + 1) * 512],
                            szb[:, tt % 2, :]).then_inc(s_st.h, 16)
                plan.add("sync", stz)
                s_st.inc(16 * 8)
            plan.emit(nc)

        sumrow = e(nc.sbuf_tensor("c_sumrow", [1, 512], F32))
        prerow = e(nc.sbuf_tensor("c_prerow", [1, 512], F32))
        xfix_tok = e(nc.sbuf_tensor("c_xfix_tok", [3, 32 * 128], BF16))
        with ExitStack() as pctx:
            ep = pctx.enter_context
            afm = ep(nc.sbuf_tensor("c_afm", [NHD, NCH, 128], F32, side="right"))
            acum_fm = ep(nc.sbuf_tensor("c_acum_fm", [NHD, NCH, 128], F32, side="right"))
            nacum_fm = ep(nc.sbuf_tensor("c_nacum_fm", [NHD, NCH, 128], F32, side="right"))
            scol = ep(nc.sbuf_tensor("c_scol", [NHD, 2, NCH], F32, side="right"))
            edgeb = ep(nc.sbuf_tensor("c_edgeb", [128, XBC_CH * 3], F32, side="right"))
            ownb = ep(nc.sbuf_tensor("c_ownb", [128, XBC_CH * 3], F32, side="right"))
            negAb = ep(nc.sbuf_tensor("c_negAb", [128, NHD], F32, side="right"))
            atok = ep(nc.sbuf_tensor("c_atok", [128, NCH, NHD], F32, side="right"))
            wstg2 = ep(nc.sbuf_tensor("c_wstg2", [128, 2, KC, 128], F32, side="right"))
            wb2 = ep(nc.sbuf_tensor("c_wb2", [128, 2, KC, 128], BF16, side="right"))
            xbct = ep(nc.sbuf_tensor("c_xbct", [128, 2, 1030], BF16, side="right"))
            dvec = ep(nc.sbuf_tensor("c_dvec", [128, 4, 128], BF16, side="right"))
            cwS = ep(nc.sbuf_tensor("c_cwS", [128, XBC_CH * 4], F32, side="right"))
            cbS = ep(nc.sbuf_tensor("c_cbS", [128, XBC_CH], F32, side="right"))
            xsil = ep(nc.sbuf_tensor("c_xsil", [128, 2, 1024], BF16, side="right"))
            sfp = ep(nc.sbuf_tensor("c_sfp", [128, 4, NCH, NHD], F32, side="right"))

            pxbc = psum_pool[:, 0:1024].rearrange("p (a b) -> p a b", a=2)
            pcv = psum_pool[:, 1024:2048].rearrange("p (a b) -> p a b", a=2)
            pdt = psum_pool[:, 2048:2560].rearrange("p (a b) -> p a b", a=NCH)
            ptr = psum_pool[:, 2560:3072]

            with nc.Block() as blk:
                @blk.sync
                def _(sync):
                    sync.dma_start(cwS[:, :], cw_d[:, :]).then_inc(s_ld.h, 16)
                    sync.dma_start(cbS[:, :], cb_d[:, :]).then_inc(s_ld.h, 16)
                    sync.wait_ge(s_ld.h, s_ld.inc(32))
                @blk.vector
                def _(vector):
                    vector.memset(xbct[:, 0, 0:3], 0.0)
                    vector.memset(xbct[:, 1, 0:3], 0.0)

            plan = Plan()
            ld_base = s_ld.n
            cast_base = s_cast.n
            mm_base = s_mm.n
            ev_base = s_ev.n
            st_base = s_st.n
            s_cv = Sem(sx, nc, "c_cv")
            s_cve = Sem(sx, nc, "c_cve")
            s_dg = Sem(sx, nc, "c_dg")
            s_ed = Sem(sx, nc, "c_ed")
            mm_stamp = {}
            for m in range(XBC_CH):
                slot = m % 2

                def dma(sync, m=m, slot=slot):
                    if m >= 2:
                        sync.wait_ge(s_cast.h, cast_base + m - 1)
                    sync.dma_start(wstg2[:, slot, :, :],
                                   win_d[32 + m, :].rearrange("(p k c) -> p k c",
                                                              p=128, k=KC)
                                   ).then_inc(s_ld.h, 16)
                plan.add("sync", dma)
                s_ld.inc(16)

                def cast(gp, m=m, slot=slot):
                    gp.wait_ge(s_ld.h, ld_base + 16 * (m + 1))
                    if m >= 2:
                        gp.wait_ge(s_mm.h, mm_stamp[m - 2])
                    gp.tensor_copy(wb2[:, slot, :, :], wstg2[:, slot, :, :])
                    _flush(gp, st, "gpsimd").then_inc(s_cast.h, 1)
                plan.add("gpsimd", cast)
                s_cast.inc(1)

                def mms(tensor, m=m, slot=slot):
                    tensor.wait_ge(s_cast.h, cast_base + m + 1)
                    if m >= 1:
                        tensor.wait_ge(s_ev.h, ev_base + 2 * m)
                    for n in range(2):
                        for k in range(KC):
                            mmi = tensor.matmul(pxbc[:, n, :], wb2[:, slot, k, :],
                                                xb[:, k, n * 512:(n + 1) * 512],
                                                start=(k == 0), stop=(k == KC - 1))
                        mmi.then_inc(s_mm.h, 1)
                plan.add("tensor", mms)
                s_mm.inc(2)
                mm_stamp[m] = s_mm.n

                def evx(scalar, m=m, slot=slot):
                    for n in range(2):
                        scalar.wait_ge(s_mm.h, mm_base + 2 * m + n + 1)
                        if m >= 2:
                            scalar.wait_ge(s_cv.h, 2 * (m - 1))
                            scalar.wait_ge(s_ed.h, m - 1)
                        scalar.activation(xbct[:, slot, 3 + n * 512:3 + (n + 1) * 512],
                                          pxbc[:, n, :], AF.Identity)
                    _flush(scalar, st, "scalar").then_inc(s_ev.h, 2)
                plan.add("scalar", evx)
                s_ev.inc(2)

                def edge(vector, m=m, slot=slot, evt=s_ev.n):
                    vector.wait_ge(s_ev.h, evt)
                    if m >= 1:
                        vector.wait_ge(s_cv.h, 2 * m)
                    vector.tensor_copy(edgeb[:, 3 * m:3 * m + 3],
                                       xbct[:, slot, 1024:1027])
                    vector.tensor_copy(ownb[:, 3 * m:3 * m + 3],
                                       xbct[:, slot, 3:6])
                    for w in range(4):
                        vector.tensor_scalar(dvec[:, w, :], idnb[:, :],
                                             cwS[:, 4 * m + w:4 * m + w + 1],
                                             None, ALU.mult)
                    _flush(vector, st, "vector").then_inc(s_ed.h, 1)
                    _flush(vector, st, "vector").then_inc(s_dg.h, 1)
                plan.add("vector", edge)
                s_ed.inc(1)
                s_dg.inc(1)

                def cmms(tensor, m=m, slot=slot, dgt=s_dg.n, evt=s_ev.n):
                    tensor.wait_ge(s_dg.h, dgt)
                    tensor.wait_ge(s_ev.h, evt)
                    if m >= 1:
                        tensor.wait_ge(s_cve.h, m)
                    for n in range(2):
                        for w in range(4):
                            mmi = tensor.matmul(
                                pcv[:, n, :], dvec[:, w, :],
                                xbct[:, slot, n * 512 + w:n * 512 + w + 512],
                                start=(w == 0), stop=(w == 3))
                        mmi.then_inc(s_cv.h, 1)
                plan.add("tensor", cmms)
                s_cv.inc(2)

                def evc(scalar, m=m, cvt=s_cv.n, stt=s_st.n):
                    for n in range(2):
                        scalar.wait_ge(s_cv.h, cvt - 2 + n + 1)
                        if m < 32 and m >= 2:
                            scalar.wait_ge(s_st.h, st_base + 16 * (m - 1))
                        if m < 32:
                            scalar.activation(xsil[:, m % 2, n * 512:(n + 1) * 512],
                                              pcv[:, n, :], AF.Silu,
                                              bias=cbS[:, m:m + 1])
                        elif m == 32:
                            scalar.activation(bT[:, n * 512:(n + 1) * 512],
                                              pcv[:, n, :], AF.Silu,
                                              bias=cbS[:, m:m + 1])
                        else:
                            scalar.activation(cT[:, n * 512:(n + 1) * 512],
                                              pcv[:, n, :], AF.Silu,
                                              bias=cbS[:, m:m + 1])
                    _flush(scalar, st, "scalar").then_inc(s_cve.h, 1)
                plan.add("scalar", evc)
                s_cve.inc(1)

                if m < 32:
                    def stx(sync, m=m, cvet=s_cve.n):
                        sync.wait_ge(s_cve.h, cvet)
                        sync.dma_start(xscr[m * 128:(m + 1) * 128, :],
                                       xsil[:, m % 2, :]).then_inc(s_st.h, 16)
                    plan.add("sync", stx)
                    s_st.inc(16)
            plan.emit(nc)

            with nc.Block() as blk:
                @blk.sync
                def _(sync):
                    sync.dma_start(wstg2[:, 0, :, :],
                                   win_d[66, :].rearrange("(p k c) -> p k c",
                                                          p=128, k=KC)
                                   ).then_inc(s_ld.h, 16)
                    sync.wait_ge(s_ld.h, s_ld.inc(16))
            with nc.Block() as blk:
                @blk.vector
                def _(vector):
                    vector.tensor_copy(wb2[:, 0, :, :], wstg2[:, 0, :, :])
            with nc.Block() as blk:
                @blk.tensor
                def _(tensor):
                    for tt in range(NCH):
                        for k in range(KC):
                            tensor.matmul(pdt[:, tt, :],
                                          xb[:, k, tt * 128:(tt + 1) * 128],
                                          wb2[:, 0, k, 0:64],
                                          start=(k == 0), stop=False)
                        tensor.matmul(pdt[:, tt, :], st["ones1_f32"][0:1, :],
                                      rows[0:1, 0:NHD],
                                      start=False, stop=True)
            with nc.Block() as blk:
                @blk.scalar
                def _(scalar):
                    for tt in range(NCH):
                        scalar.activation(sfp[:, 0, tt, :], pdt[:, tt, :], AF.Relu)
                        scalar.activation(sfp[:, 1, tt, :], pdt[:, tt, :], AF.Abs)
            with nc.Block() as blk:
                @blk.scalar
                def _(scalar):
                    for tt in range(NCH):
                        scalar.activation(sfp[:, 2, tt, :], sfp[:, 1, tt, :],
                                          AF.Exp, scale=-1.0)
            with nc.Block() as blk:
                @blk.scalar
                def _(scalar):
                    for tt in range(NCH):
                        scalar.activation(sfp[:, 3, tt, :], sfp[:, 2, tt, :],
                                          AF.Ln, bias=1.0)
            with nc.Block() as blk:
                @blk.vector
                def _(vector):
                    for tt in range(NCH):
                        vector.tensor_add(dt_tok[:, tt, :], sfp[:, 0, tt, :],
                                          sfp[:, 3, tt, :])
            with nc.Block() as blk:
                @blk.scalar
                def _(scalar):
                    scalar.activation(rows[0:1, 2 * NHD:3 * NHD],
                                      rows[0:1, NHD:2 * NHD], AF.Exp)
            with nc.Block() as blk:
                @blk.vector
                def _(vector):
                    vector.tensor_scalar_mul(rows[0:1, 3 * NHD:4 * NHD],
                                             rows[0:1, 2 * NHD:3 * NHD], -1.0)
            with nc.Block() as blk:
                @blk.tensor
                def _(tensor):
                    tensor.matmul(ptr[:, 0:64], st["ones1_f32"][0:1, :],
                                  rows[0:1, 3 * NHD:4 * NHD], start=True, stop=True)
            with nc.Block() as blk:
                @blk.scalar
                def _(scalar):
                    scalar.activation(negAb[:, :], ptr[:, 0:64], AF.Identity)
            with nc.Block() as blk:
                @blk.vector
                def _(vector):
                    vector.tensor_mul(atok[:, :, :], dt_tok[:, :, :],
                                      negAb[:, :].unsqueeze(1)
                                      .broadcast_to([128, NCH, NHD]))
            for rnd in range(2):
                with nc.Block() as blk:
                    @blk.tensor
                    def _(tensor, rnd=rnd):
                        for i in range(4):
                            c = rnd * 4 + i
                            tensor.transpose(
                                ptr[0:64, i * 128:(i + 1) * 128]
                                if False else psum_pool[0:64, 2560 + i * 128:2560 + (i + 1) * 128],
                                atok[:, c, :], idn[:, :])
                with nc.Block() as blk:
                    @blk.scalar
                    def _(scalar, rnd=rnd):
                        for i in range(4):
                            c = rnd * 4 + i
                            scalar.activation(afm[:, c, :],
                                              psum_pool[0:64, 2560 + i * 128:2560 + (i + 1) * 128],
                                              AF.Identity)
            with nc.Block() as blk:
                @blk.vector
                def _(vector):
                    for c in range(NCH):
                        vector.tensor_tensor_scan(acum_fm[:, c, :], afm[:, c, :],
                                                  afm[:, c, :], 0.0, ALU.add, ALU.bypass)
            with nc.Block() as blk:
                @blk.vector
                def _(vector):
                    vector.tensor_scalar_mul(
                        nacum_fm[:, :, :].rearrange("p a b -> p (a b)"),
                        acum_fm[:, :, :].rearrange("p a b -> p (a b)"), -1.0)
                    for c in range(NCH):
                        vector.tensor_copy(scol[:, 0, c:c + 1], acum_fm[:, c, 127:128])
            with nc.Block() as blk:
                @blk.sync
                def _(sync):
                    sync.dma_start(sumscr[:, :], scol[:, 0, :]).then_inc(s_ld.h, 16)
                    sync.wait_ge(s_ld.h, s_ld.inc(16))
            with nc.Block() as blk:
                @blk.vector
                def _(vector):
                    vector.tensor_tensor_scan(scol[:, 1, :], scol[:, 0, :],
                                              scol[:, 0, :], 0.0, ALU.add, ALU.bypass)
            with nc.Block() as blk:
                @blk.vector
                def _(vector):
                    vector.memset(scol[:, 0, 0:1], 0.0)
                    vector.tensor_copy(scol[:, 0, 1:NCH], scol[:, 1, 0:NCH - 1])
            for rnd in range(2):
                with nc.Block() as blk:
                    @blk.tensor
                    def _(tensor, rnd=rnd):
                        for i in range(4):
                            c = rnd * 4 + i
                            tensor.transpose(
                                psum_pool[:, 2560 + i * 64:2560 + (i + 1) * 64],
                                acum_fm[:, c, :], idn[0:64, 0:64])
                with nc.Block() as blk:
                    @blk.scalar
                    def _(scalar, rnd=rnd):
                        for i in range(4):
                            c = rnd * 4 + i
                            scalar.activation(acum_tok[:, c, :],
                                              psum_pool[:, 2560 + i * 64:2560 + (i + 1) * 64],
                                              AF.Identity)
            with nc.Block() as blk:
                @blk.sync
                def _(sync):
                    for c in range(NCH):
                        sync.dma_start(acscr[c, 0, :, :], acum_fm[:, c, :]
                                       ).then_inc(s_ld.h, 16)
                        sync.dma_start(acscr[c, 1, :, :], nacum_fm[:, c, :]
                                       ).then_inc(s_ld.h, 16)
                    sync.dma_start(prescr[:, :], scol[:, 0, :]).then_inc(s_ld.h, 16)
                    sync.wait_ge(s_ld.h, s_ld.inc(16 * 17))


            edr2 = ep(nc.sbuf_tensor("c_edr2", [128, 2, XBC_CH * 3], F32, side="right"))
            edge_eff = ep(nc.sbuf_tensor("c_edge_eff", [128, XBC_CH * 3], F32, side="right"))
            fxin = ep(nc.sbuf_tensor("c_fxin", [128, XBC_CH, 6], F32, side="right"))
            fxm = ep(nc.sbuf_tensor("c_fxm", [128, 12, XBC_CH], F32, side="right"))
            fxs = ep(nc.sbuf_tensor("c_fxs", [128, 6, XBC_CH], F32, side="right"))
            fxout = ep(nc.sbuf_tensor("c_fxout", [128, 3, XBC_CH], F32, side="right"))
            fxsil = ep(nc.sbuf_tensor("c_fxsil", [128, 3, XBC_CH], BF16, side="right"))
            cwV = cwS[:, :].rearrange("p (m w) -> p m w", m=XBC_CH)

            with nc.Block() as blk:
                @blk.sync
                def _(sync):
                    sync.dma_start(edscr[:, :], edgeb[:, :]).then_inc(s_ld.h, 16)
                    sync.wait_ge(s_ld.h, s_ld.inc(16))
            with nc.Block() as blk:
                @blk.gpsimd
                def _(gp):
                    gp.collective_compute(
                        "AllGather", ALU.bypass,
                        replica_groups=[[0, 1], [2, 3], [4, 5], [6, 7]],
                        ins=[edscr.ap().opt()], outs=[edscr2.ap().opt()],
                    ).then_inc(s_cc.h, 1)
                    gp.wait_ge(s_cc.h, s_cc.inc(1))
            with nc.Block() as blk:
                @blk.sync
                def _(sync):
                    sync.dma_start(edr2[:, 0, :], edscr2[0:128, :]).then_inc(s_ld.h, 16)
                    sync.dma_start(edr2[:, 1, :], edscr2[128:256, :]).then_inc(s_ld.h, 16)
                    sync.wait_ge(s_ld.h, s_ld.inc(32))
            with nc.Block() as blk:
                @blk.vector
                def _(vector):
                    vector.tensor_scalar(edge_eff[:, :], edr2[:, 0, :],
                                         flags[:, 0:1], None, ALU.mult)
                    vector.tensor_scalar(edr2[:, 1, :], edr2[:, 1, :],
                                         flags[:, 1:2], None, ALU.mult)
            with nc.Block() as blk:
                @blk.vector
                def _(vector):
                    vector.tensor_add(edge_eff[:, :], edge_eff[:, :], edr2[:, 1, :])
            with nc.Block() as blk:
                @blk.vector
                def _(vector):
                    vector.tensor_copy(fxin[:, :, 0:3],
                                       edge_eff[:, :].rearrange("p (m w) -> p m w",
                                                                m=XBC_CH))
                    vector.tensor_copy(fxin[:, :, 3:6],
                                       ownb[:, :].rearrange("p (m w) -> p m w",
                                                            m=XBC_CH))
            with nc.Block() as blk:
                @blk.vector
                def _(vector):
                    for t in range(3):
                        for w in range(4):
                            vector.tensor_mul(fxm[:, t * 4 + w, :],
                                              fxin[:, :, t + w], cwV[:, :, w])
            with nc.Block() as blk:
                @blk.vector
                def _(vector):
                    for t in range(3):
                        vector.tensor_add(fxs[:, 2 * t, :], fxm[:, t * 4, :],
                                          fxm[:, t * 4 + 1, :])
                        vector.tensor_add(fxs[:, 2 * t + 1, :], fxm[:, t * 4 + 2, :],
                                          fxm[:, t * 4 + 3, :])
            with nc.Block() as blk:
                @blk.vector
                def _(vector):
                    for t in range(3):
                        vector.tensor_add(fxout[:, t, :], fxs[:, 2 * t, :],
                                          fxs[:, 2 * t + 1, :])
            with nc.Block() as blk:
                @blk.vector
                def _(vector):
                    vector.tensor_add(fxout[:, :, :], fxout[:, :, :],
                                      cbS[:, :].unsqueeze(1)
                                      .broadcast_to([128, 3, XBC_CH]))
            with nc.Block() as blk:
                @blk.scalar
                def _(scalar):
                    scalar.activation(fxsil[:, :, :], fxout[:, :, :], AF.Silu)
            with nc.Block() as blk:
                @blk.vector
                def _(vector):
                    vector.tensor_copy(bT[:, 0:3], fxsil[:, :, 32])
                    vector.tensor_copy(cT[:, 0:3], fxsil[:, :, 33])
            for rnd in range(8):
                with nc.Block() as blk:
                    @blk.tensor
                    def _(tensor, rnd=rnd):
                        for i in range(4):
                            m = rnd * 4 + i
                            tensor.transpose(
                                psum_pool[0:3, 2560 + i * 128:2560 + (i + 1) * 128]
                                .bitcast(BF16)[0:3, 0:128],
                                fxsil[:, :, m], idnb[:, :])
                with nc.Block() as blk:
                    @blk.scalar
                    def _(scalar, rnd=rnd):
                        for i in range(4):
                            m = rnd * 4 + i
                            scalar.activation(
                                xfix_tok[0:3, m * 128:(m + 1) * 128],
                                psum_pool[0:3, 2560 + i * 128:2560 + (i + 1) * 128]
                                .bitcast(BF16)[0:3, 0:128], AF.Identity)

        with ExitStack() as s3:
            e3 = s3.enter_context
            S = e3(nc.sbuf_tensor("c_S", [128, DIN], F32, side="right"))
            Sb = e3(nc.sbuf_tensor("c_Sb", [128, DIN], BF16, side="right"))
            xtok = e3(nc.sbuf_tensor("c_xtok", [128, DIN], BF16, side="right"))
            xd = e3(nc.sbuf_tensor("c_xd", [128, DIN], BF16, side="right"))
            Yb = e3(nc.sbuf_tensor("c_Yb", [128, 2, 512], BF16, side="right"))
            mtall = e3(nc.sbuf_tensor("c_mtall", [128, 8, 8, 128], BF16, side="right"))
            cbm = e3(nc.sbuf_tensor("c_cbm", [128, 128], BF16, side="right"))
            btok = e3(nc.sbuf_tensor("c_btok", [128, 128], BF16, side="right"))
            expDs = e3(nc.sbuf_tensor("c_expDs", [128, 2, 1024], F32, side="right"))
            expDb = e3(nc.sbuf_tensor("c_expDb", [128, 2, 1024], BF16, side="right"))
            grows = e3(nc.sbuf_tensor("c_grows", [1, 2, 2, 1024], F32, side="right"))
            odde = e3(nc.sbuf_tensor("c_odde", [128, 4, NHD], F32, side="right"))
            lrow = e3(nc.sbuf_tensor("c_lrow", [1, NHD], F32, side="right"))
            lrep = e3(nc.sbuf_tensor("c_lrep", [1, DIN], BF16, side="right"))
            ttmp = e3(nc.sbuf_tensor("c_ttmp", [128, 2, 512], BF16, side="right"))
            dpB = e3(nc.sbuf_tensor("c_dpB", [128, DIN], BF16, side="right"))
            with ExitStack() as dctx:
                dpRow = dctx.enter_context(nc.sbuf_tensor("c_dpRow", [1, DIN], F32))
                with nc.Block() as blk:
                    @blk.sync
                    def _(sync):
                        sync.dma_start(dpRow[:, :], dp_d[:, :]).then_inc(s_ld.h, 16)
                        sync.wait_ge(s_ld.h, s_ld.inc(16))
                mmb = s_mm.n
                evb = s_ev.n
                with nc.Block() as blk:
                    @blk.tensor
                    def _(tensor, mmb=mmb, evb=evb):
                        for n in range(8):
                            if n >= 1:
                                tensor.wait_ge(s_ev.h, evb + n)
                            tensor.matmul(psum_pool[:, 0:512], st["ones1_f32"][0:1, :],
                                          dpRow[0:1, n * 512:(n + 1) * 512],
                                          start=True, stop=True).then_inc(s_mm.h, 1)
                    @blk.scalar
                    def _(scalar, mmb=mmb, evb=evb):
                        for n in range(8):
                            scalar.wait_ge(s_mm.h, mmb + n + 1)
                            scalar.activation(dpB[:, n * 512:(n + 1) * 512],
                                              psum_pool[:, 0:512], AF.Identity)
                            _flush(scalar, st, "scalar").then_inc(s_ev.h, 1)
                s_mm.inc(8)
                s_ev.inc(8)


            with nc.Block() as blk:
                @blk.sync
                def _(sync):
                    sync.dma_start(sumrow[0:1, :],
                                   sumscr[:, :].rearrange("a b -> (a b)").unsqueeze(0)
                                   ).then_inc(s_ld.h, 16)
                    sync.wait_ge(s_ld.h, s_ld.inc(16))
                @blk.vector
                def _(vector):
                    vector.memset(S[:, :], 0.0)
                    vector.memset(Sb[:, :], 0.0)

            pD = psum_pool[:, 0:2048].rearrange("p (a b) -> p a b", a=2)
            pYd = psum_pool[:, 2048:3072].rearrange("p (a b) -> p a b", a=2)
            pYo = psum_pool[:, 3072:4096].rearrange("p (a b) -> p a b", a=2)
            pCB = psum_pool[:, 0:128]
            pod = psum_pool[:, 128:192]
            pst = psum_pool[:, 0:1024].rearrange("p (a b) -> p a b", a=2)
            plm = psum_pool[:, 1024:2048].rearrange("p (a b) -> p a b", a=2)

            for c in range(NCH):
                with nc.Block() as blk:
                    @blk.sync
                    def _(sync, c=c):
                        sync.dma_start(xtok[:, :], xscr[:, c * 128:(c + 1) * 128],
                                       transpose=True).then_inc(s_ld.h, 16)
                        sync.wait_ge(s_ld.h, s_ld.inc(16))
                if c == 0:
                    with nc.Block() as blk:
                        @blk.vector
                        def _(vector):
                            vector.tensor_copy(xtok[0:3, :], xfix_tok[0:3, :])
                with nc.Block() as blk:
                    @blk.vector
                    def _(vector, c=c):
                        vector.tensor_mul(
                            xd[:, :].rearrange("p (h d) -> p h d", h=NHD),
                            xtok[:, :].rearrange("p (h d) -> p h d", h=NHD),
                            dt_tok[:, c, :].unsqueeze(2).broadcast_to([128, NHD, HD]))
                    @blk.tensor
                    def _(tensor, c=c):
                        tensor.matmul(pod[:, :], st["ones1_f32"][0:1, :],
                                      sumrow[0:1, :].rearrange("a (h c2) -> a h c2",
                                                               c2=NCH)[:, :, c],
                                      start=True, stop=True)
                with nc.Block() as blk:
                    @blk.scalar
                    def _(scalar, c=c):
                        scalar.activation(odde[:, 0, :], acum_tok[:, c, :], AF.Exp)
                        scalar.activation(lrow[0:1, :],
                                          sumrow[0:1, :].rearrange("a (h c2) -> a h c2",
                                                                   c2=NCH)[:, :, c],
                                          AF.Exp)
                    @blk.vector
                    def _(vector, c=c):
                        vector.tensor_sub(odde[:, 1, :], pod[:, :], acum_tok[:, c, :])
                with nc.Block() as blk:
                    @blk.scalar
                    def _(scalar):
                        scalar.activation(odde[:, 2, :], odde[:, 1, :], AF.Exp)
                    @blk.vector
                    def _(vector):
                        vector.tensor_copy(lrep[0:1, :],
                                           lrow[0:1, :].unsqueeze(2)
                                           .broadcast_to([1, NHD, HD]))
                with nc.Block() as blk:
                    @blk.tensor
                    def _(tensor, c=c):
                        tensor.matmul(pCB[:, :], bT[:, c * 128:(c + 1) * 128],
                                      cT[:, c * 128:(c + 1) * 128],
                                      start=True, stop=True)
                with nc.Block() as blk:
                    @blk.vector
                    def _(vector):
                        vector.tensor_mul(cbm[:, :], pCB[:, :], tri[:, :])
                with nc.Block() as blk:
                    @blk.tensor
                    def _(tensor, c=c):
                        tensor.transpose(
                            psum_pool[:, 192:256].bitcast(BF16)[:, 0:128],
                            bT[:, c * 128:(c + 1) * 128], idnb[:, :])
                with nc.Block() as blk:
                    @blk.scalar
                    def _(scalar):
                        scalar.activation(btok[:, :],
                                          psum_pool[:, 192:256].bitcast(BF16)[:, 0:128],
                                          AF.Identity)
                ldb = s_ld.n
                mmb = s_mm.n
                evb = s_ev.n
                with nc.Block() as blk:
                    @blk.sync
                    def _(sync, c=c, ldb=ldb, mmb=mmb):
                        for g in range(8):
                            if g >= 2:
                                sync.wait_ge(s_mm.h, mmb + g - 1)
                            sync.dma_start(
                                grows[0:1, g % 2, 0, :],
                                acscr[c, 0, :, :].rearrange("a b -> (a b)")
                                [8 * g * 128:(8 * g + 8) * 128].unsqueeze(0)
                                ).then_inc(s_ld.h, 16)
                            sync.dma_start(
                                grows[0:1, g % 2, 1, :],
                                acscr[c, 1, :, :].rearrange("a b -> (a b)")
                                [8 * g * 128:(8 * g + 8) * 128].unsqueeze(0)
                                ).then_inc(s_ld.h, 16)
                    s_ld.inc(32 * 8)

                    @blk.tensor
                    def _(tensor, c=c, ldb=ldb, mmb=mmb, evb=evb):
                        for g in range(8):
                            tensor.wait_ge(s_ld.h, ldb + 32 * (g + 1))
                            if g >= 2:
                                tensor.wait_ge(s_ev.h, evb + g - 1)
                            for n in range(2):
                                tensor.matmul(pD[:, g % 2, n * 512:(n + 1) * 512],
                                              st["ones1_f32"][0:1, :],
                                              grows[0:1, g % 2, 0, n * 512:(n + 1) * 512],
                                              start=True, stop=False)
                            for i in range(8):
                                mmi = tensor.matmul(
                                    pD[:, g % 2, i * 128:(i + 1) * 128],
                                    grows[0:1, g % 2, 1, i * 128:(i + 1) * 128],
                                    st["ones1_f32"][0:1, :],
                                    start=False, stop=(i == 3 or i == 7))
                            mmi.then_inc(s_mm.h, 1)
                    s_mm.inc(8)

                    @blk.scalar
                    def _(scalar, mmb=mmb, evb=evb):
                        for g in range(8):
                            scalar.wait_ge(s_mm.h, mmb + g + 1)
                            scalar.activation(expDs[:, g % 2, :], pD[:, g % 2, :],
                                              AF.Relu, scale=-1.0)
                            scalar.activation(expDb[:, g % 2, :], expDs[:, g % 2, :],
                                              AF.Exp, scale=-1.0)
                            _flush(scalar, st, "scalar").then_inc(s_ev.h, 1)
                    s_ev.inc(8)

                    @blk.vector
                    def _(vector, evb=evb):
                        for g in range(8):
                            vector.wait_ge(s_ev.h, evb + g + 1)
                            vector.tensor_mul(
                                mtall[:, g, :, :],
                                expDb[:, g % 2, :].rearrange("p (h l) -> p h l", h=8),
                                cbm[:, :].unsqueeze(1).broadcast_to([128, 8, 128]))
                            _flush(vector, st, "vector").then_inc(s_pm_c.h, 1)
                    s_pm_c.inc(8)

                mmb = s_mm.n
                evb = s_ev.n
                stb = s_st.n
                with nc.Block() as blk:
                    @blk.tensor
                    def _(tensor, c=c, mmb=mmb, evb=evb):
                        for g in range(8):
                            if g >= 2:
                                tensor.wait_ge(s_ev.h, evb + 2 * (g - 1))
                            for hl in range(8):
                                tensor.matmul(pYd[:, g % 2, hl * 64:(hl + 1) * 64],
                                              mtall[:, g, hl, :],
                                              xd[:, (8 * g + hl) * 64:(8 * g + hl + 1) * 64],
                                              start=True, stop=(hl >= 0))
                            mmi = tensor.matmul(pYo[:, g % 2, :],
                                                cT[:, c * 128:(c + 1) * 128],
                                                Sb[:, g * 512:(g + 1) * 512],
                                                start=True, stop=True)
                            mmi.then_inc(s_mm.h, 1)
                    s_mm.inc(8)

                    @blk.vector
                    def _(vector, c=c, mmb=mmb, evb=evb, stb=stb):
                        for g in range(8):
                            vector.wait_ge(s_mm.h, mmb + g + 1)
                            if g >= 2:
                                vector.wait_ge(s_st.h, stb + 16 * (g - 1))
                            odr = odde[:, 0, 8 * g:8 * g + 8]
                            vector.tensor_mul(
                                ttmp[:, g % 2, :].rearrange("p (h d) -> p h d", h=8),
                                pYo[:, g % 2, :].rearrange("p (h d) -> p h d", h=8),
                                odr.unsqueeze(2).broadcast_to([128, 8, 64]))
                            vector.scalar_tensor_tensor(
                                ttmp[:, g % 2, :], pYd[:, g % 2, :], 1.0,
                                ttmp[:, g % 2, :], ALU.mult, ALU.add)
                            vector.tensor_mul(Yb[:, g % 2, :],
                                              dpB[:, g * 512:(g + 1) * 512],
                                              xtok[:, g * 512:(g + 1) * 512])
                            vector.tensor_add(Yb[:, g % 2, :], Yb[:, g % 2, :],
                                              ttmp[:, g % 2, :])
                            _flush(vector, st, "vector").then_inc(s_ev.h, 2)
                    s_ev.inc(16)

                    @blk.sync
                    def _(sync, c=c, evb=evb):
                        for g in range(8):
                            sync.wait_ge(s_ev.h, evb + 2 * (g + 1))
                            sync.dma_start(
                                yscr[c * 128:(c + 1) * 128, g * 512:(g + 1) * 512],
                                Yb[:, g % 2, :]).then_inc(s_st.h, 16)
                    s_st.inc(16 * 8)

                with nc.Block() as blk:
                    @blk.vector
                    def _(vector, c=c):
                        vector.tensor_mul(
                            xd[:, :].rearrange("p (h d) -> p h d", h=NHD),
                            xd[:, :].rearrange("p (h d) -> p h d", h=NHD),
                            odde[:, 2, :].unsqueeze(2).broadcast_to([128, NHD, HD]))
                mmb = s_mm.n
                evb = s_ev.n
                with nc.Block() as blk:
                    @blk.tensor
                    def _(tensor, c=c, mmb=mmb, evb=evb):
                        for g in range(8):
                            if g >= 2:
                                tensor.wait_ge(s_ev.h, evb + 2 * (g - 1))
                            tensor.matmul(pst[:, g % 2, :], btok[:, :],
                                          xd[:, g * 512:(g + 1) * 512],
                                          start=True, stop=True)
                            tensor.matmul(plm[:, g % 2, :], ones1_b[0:1, :],
                                          lrep[0:1, g * 512:(g + 1) * 512],
                                          start=True, stop=True).then_inc(s_mm.h, 1)
                    s_mm.inc(8)

                    @blk.vector
                    def _(vector, mmb=mmb):
                        for g in range(8):
                            vector.wait_ge(s_mm.h, mmb + g + 1)
                            vector.tensor_mul(S[:, g * 512:(g + 1) * 512],
                                              S[:, g * 512:(g + 1) * 512],
                                              plm[:, g % 2, :])
                            vector.tensor_add(S[:, g * 512:(g + 1) * 512],
                                              S[:, g * 512:(g + 1) * 512],
                                              pst[:, g % 2, :])
                            _flush(vector, st, "vector").then_inc(s_ev.h, 2)
                    s_ev.inc(16)
                with nc.Block() as blk:
                    @blk.vector
                    def _(vector):
                        vector.tensor_copy(Sb[:, :], S[:, :])

            with nc.Block() as blk:
                @blk.sync
                def _(sync):
                    sync.dma_start(cc_in[:, 0:DIN], S[:, :]).then_inc(s_ld.h, 16)
                    sync.wait_ge(s_ld.h, s_ld.inc(16))
            with nc.Block() as blk:
                @blk.gpsimd
                def _(gp):
                    gp.collective_compute(
                        "AllGather", ALU.bypass,
                        replica_groups=[[0, 1], [2, 3], [4, 5], [6, 7]],
                        ins=[cc_in.ap().opt()], outs=[cc_out.ap().opt()],
                    ).then_inc(s_cc.h, 1)
                    gp.wait_ge(s_cc.h, s_cc.inc(1))
            with nc.Block() as blk:
                @blk.sync
                def _(sync):
                    sync.dma_start(S[:, :], cc_out[0:128, 0:DIN]).then_inc(s_ld.h, 16)
                    sync.wait_ge(s_ld.h, s_ld.inc(16))
            with nc.Block() as blk:
                @blk.vector
                def _(vector):
                    vector.tensor_scalar(S[:, :], S[:, :], flags[:, 0:1], None, ALU.mult)
            exf = expDs[:, :, :].rearrange("p a b -> p (a b)")
            for half in range(2):
                with nc.Block() as blk:
                    @blk.sync
                    def _(sync, half=half):
                        sync.dma_start(exf[:, :],
                                       cc_out[128:256, half * 2048:(half + 1) * 2048]
                                       ).then_inc(s_ld.h, 16)
                        sync.wait_ge(s_ld.h, s_ld.inc(16))
                with nc.Block() as blk:
                    @blk.vector
                    def _(vector, half=half):
                        vector.scalar_tensor_tensor(
                            S[:, half * 2048:(half + 1) * 2048], exf[:, :],
                            flags[:, 1:2], S[:, half * 2048:(half + 1) * 2048],
                            ALU.mult, ALU.add)
            with nc.Block() as blk:
                @blk.vector
                def _(vector):
                    vector.tensor_copy(Sb[:, :], S[:, :])

            with nc.Block() as blk:
                @blk.sync
                def _(sync):
                    sync.dma_start(prerow[0:1, :],
                                   prescr[:, :].rearrange("a b -> (a b)").unsqueeze(0)
                                   ).then_inc(s_ld.h, 16)
                    sync.wait_ge(s_ld.h, s_ld.inc(16))
            for c in range(NCH):
                with nc.Block() as blk:
                    @blk.sync
                    def _(sync, c=c):
                        sync.dma_start(xtok[:, :],
                                       yscr[c * 128:(c + 1) * 128, :]).then_inc(s_ld.h, 16)
                        sync.dma_start(xd[:, :],
                                       zscr[c * 128:(c + 1) * 128, :]).then_inc(s_ld.h, 16)
                        sync.wait_ge(s_ld.h, s_ld.inc(32))
                    @blk.tensor
                    def _(tensor, c=c):
                        tensor.matmul(pod[:, :], st["ones1_f32"][0:1, :],
                                      prerow[0:1, :].rearrange("a (h c2) -> a h c2",
                                                               c2=NCH)[:, :, c],
                                      start=True, stop=True)
                with nc.Block() as blk:
                    @blk.vector
                    def _(vector, c=c):
                        vector.tensor_add(odde[:, 1, :], pod[:, :], acum_tok[:, c, :])
                with nc.Block() as blk:
                    @blk.scalar
                    def _(scalar):
                        scalar.activation(odde[:, 0, :], odde[:, 1, :], AF.Exp)
                mmb = s_mm.n
                with nc.Block() as blk:
                    @blk.tensor
                    def _(tensor, c=c, mmb=mmb):
                        for g in range(8):
                            if g >= 2:
                                tensor.wait_ge(s_ev.h, s_ev.n + g - 1)
                            tensor.matmul(pYo[:, g % 2, :],
                                          cT[:, c * 128:(c + 1) * 128],
                                          Sb[:, g * 512:(g + 1) * 512],
                                          start=True, stop=True).then_inc(s_mm.h, 1)

                    @blk.vector
                    def _(vector, c=c, mmb=mmb):
                        for g in range(8):
                            vector.wait_ge(s_mm.h, mmb + g + 1)
                            odr = odde[:, 0, 8 * g:8 * g + 8]
                            vector.tensor_mul(
                                ttmp[:, g % 2, :].rearrange("p (h d) -> p h d", h=8),
                                pYo[:, g % 2, :].rearrange("p (h d) -> p h d", h=8),
                                odr.unsqueeze(2).broadcast_to([128, 8, 64]))
                            vector.tensor_add(xtok[:, g * 512:(g + 1) * 512],
                                              xtok[:, g * 512:(g + 1) * 512],
                                              ttmp[:, g % 2, :])
                            _flush(vector, st, "vector").then_inc(s_ev.h, 1)
                    s_mm.inc(8)
                    s_ev.inc(8)
                with nc.Block() as blk:
                    @blk.vector
                    def _(vector):
                        vector.tensor_mul(xtok[:, :], xtok[:, :], xd[:, :])
                with nc.Block() as blk:
                    @blk.scalar
                    def _(scalar):
                        scalar.activation(S[:, :], xtok[:, :], AF.Square,
                                          accum_out=odde[:, 3, 0:1])
                with nc.Block() as blk:
                    @blk.vector
                    def _(vector):
                        vector.tensor_scalar(odde[:, 3, 1:2], odde[:, 3, 0:1],
                                             1.0 / DIN, 1e-5, ALU.mult, ALU.add)
                with nc.Block() as blk:
                    @blk.scalar
                    def _(scalar):
                        scalar.activation(odde[:, 3, 1:2], odde[:, 3, 1:2], AF.Sqrt)
                with nc.Block() as blk:
                    @blk.vector
                    def _(vector):
                        vector.reciprocal(odde[:, 3, 2:3], odde[:, 3, 1:2])
                with nc.Block() as blk:
                    @blk.gpsimd
                    def _(gp):
                        gp.tensor_scalar(xtok[:, :], xtok[:, :], odde[:, 3, 2:3],
                                         None, ALU.mult)
                with nc.Block() as blk:
                    @blk.sync
                    def _(sync, c=c):
                        sync.dma_start(yscr[c * 128:(c + 1) * 128, :], xtok[:, :]
                                       ).then_inc(s_ld.h, 16)
                        sync.wait_ge(s_ld.h, s_ld.inc(16))

        with ExitStack() as s5:
            e5 = s5.enter_context
            xT = _alloc_xT(nc, st)
            wstg3 = e5(nc.sbuf_tensor("c_wstg3", [128, 1, 32, 128], F32, side="right"))
            wb3 = e5(nc.sbuf_tensor("c_wb3", [128, 1, 32, 128], BF16, side="right"))
            rhsb = e5(nc.sbuf_tensor("c_rhsb", [128, 32, 512], BF16, side="right"))
            with nc.Block() as blk:
                @blk.sync
                def _(sync):
                    sync.dma_start(xT[:, :, :],
                                   x2_d[:, :].rearrange("(k p) t -> p k t", p=128)
                                   ).then_inc(s_ld.h, 16)
                    sync.wait_ge(s_ld.h, s_ld.inc(16))
            nwc = nw_cols
            pwo = psum_pool[:, 0:512]

            for ps in range(2):
                ldb = s_ld.n
                with nc.Block() as blk:
                    @blk.sync
                    def _(sync, ps=ps, ldb=ldb):
                        for k in range(32):
                            sync.dma_start(rhsb[:, k, :],
                                           yscr[ps * 512:(ps + 1) * 512,
                                                k * 128:(k + 1) * 128],
                                           transpose=True).then_inc(s_ld.h, 16)
                        sync.wait_ge(s_ld.h, s_ld.inc(16 * 32))
                with nc.Block() as blk:
                    @blk.vector
                    def _(vector):
                        for k in range(32):
                            vector.tensor_scalar(rhsb[:, k, :], rhsb[:, k, :],
                                                 nwc[:, k:k + 1], None, ALU.mult)
                ldb = s_ld.n
                cb_ = s_cast.n
                mmb = s_mm.n
                evb = s_ev.n
                plan = Plan()
                for m in range(KC):
                    def dma(sync, m=m, ps=ps, ldb=ldb, cb_=cb_, mmb=mmb):
                        if m >= 1:
                            sync.wait_ge(s_mm.h, mmb + m)
                        sync.dma_start(wstg3[:, 0, :, :],
                                       wout_d[m, :].rearrange("(p k c) -> p k c",
                                                              p=128, k=32)
                                       ).then_inc(s_ld.h, 16)
                    plan.add("sync", dma)
                    s_ld.inc(16)

                    def cast(gp, m=m, ldb=ldb, cb_=cb_, mmb=mmb):
                        gp.wait_ge(s_ld.h, ldb + 16 * (m + 1))
                        gp.tensor_copy(wb3[:, 0, :, :], wstg3[:, 0, :, :])
                        _flush(gp, st, "gpsimd").then_inc(s_cast.h, 1)
                    plan.add("gpsimd", cast)
                    s_cast.inc(1)

                    def mms(tensor, m=m, cb_=cb_, evb=evb):
                        tensor.wait_ge(s_cast.h, cb_ + m + 1)
                        if m >= 1:
                            tensor.wait_ge(s_ev.h, evb + m)
                        for k in range(32):
                            mmi = tensor.matmul(pwo[:, :], wb3[:, 0, k, :],
                                                rhsb[:, k, :],
                                                start=(k == 0), stop=(k == 31))
                        mmi.then_inc(s_mm.h, 1)
                    plan.add("tensor", mms)
                    s_mm.inc(1)

                    def evac(vector, m=m, ps=ps, mmb=mmb):
                        vector.wait_ge(s_mm.h, mmb + m + 1)
                        vector.scalar_tensor_tensor(
                            xT[:, m, ps * 512:(ps + 1) * 512], pwo[:, :], 1.0,
                            xT[:, m, ps * 512:(ps + 1) * 512], ALU.mult, ALU.add)
                        _flush(vector, st, "vector").then_inc(s_ev.h, 1)
                    plan.add("vector", evac)
                    s_ev.inc(1)
                plan.emit(nc)

            emit_ln(nc, st, xT, xb, KC, T, post_bf16=fused)
            if not fused:
                with nc.Block() as blk:
                    @blk.sync
                    def _(sync):
                        sync.dma_start(out_d[:, :].rearrange("(k p) t -> p k t",
                                                             p=128),
                                       xT[:, :, :]).then_inc(s_st.h, 16)
                        sync.wait_ge(s_st.h, s_st.inc(16))


def host_ssd_inputs(x2_full, W_in, conv_w, conv_b, dt_bias, A_log, D_param,
                    norm_w, W_out):
    import ml_dtypes
    winp = np.zeros((DM, 8576), np.float32)
    winp[:, :8512] = W_in
    win_t = _tile_w(winp).reshape(67, -1)
    wout_t = _tile_w(W_out).reshape(KC, -1)
    cw = np.ascontiguousarray(conv_w.reshape(XBC_CH, 128, 4).transpose(1, 0, 2)
                              ).reshape(128, -1)
    cb = np.ascontiguousarray(conv_b.reshape(XBC_CH, 128).T)
    nw = np.ascontiguousarray(norm_w.reshape(32, 128).T)
    dp_rep = np.repeat(D_param, HD)[None, :]
    idnf = np.eye(128, dtype=np.float32)
    idnb = np.eye(128, dtype=np.float32).astype(ml_dtypes.bfloat16)
    tri = np.triu(np.ones((128, 128), np.float32)).astype(ml_dtypes.bfloat16)
    maps = []
    for c in range(NCORE):
        b, h = c // 2, c % 2
        x2t = _feat_major(x2_full[b, h * T:(h + 1) * T, :])
        fl = np.zeros((128, 2), np.float32)
        if h == 1:
            fl[:, 0] = 1.0
        maps.append(dict(
            x2t=x2t, win=win_t, wout=wout_t, cw=cw, cb=cb,
            dtb=dt_bias[None, :].astype(np.float32),
            alog=A_log[None, :].astype(np.float32),
            dp=dp_rep.astype(np.float32), nw=nw,
            idnf=idnf, idnb=idnb, tri=tri, flags=fl))
    return maps



def build_fused(upto=4):
    nc = bass.Bass("TRN2", target_bir_lowering=False)
    with ExitStack() as ctx:
        st = make_base(nc, ctx)
        emit_attn(nc, ctx, st, fused=True)
        if upto >= 2:
            emit_moe(nc, ctx, st, fused=True)
        if upto >= 3:
            emit_ssd(nc, ctx, st, fused=True)
        if upto >= 4:
            emit_tail(nc, ctx, st, fused=True)
        else:
            out_d = nc.dram_tensor("out", [DM, T], F32, kind="ExternalOutput")
            s_fin = Sem(st["sctx"], nc, "fin")
            with nc.Block() as blk:
                @blk.sync
                def _(sync):
                    sync.dma_start(out_d[:, :].rearrange("(k p) t -> p k t", p=128),
                                   st["xT"][:, :, :]).then_inc(s_fin.h, 16)
                    sync.wait_ge(s_fin.h, s_fin.inc(16))
    return nc


def host_fused_inputs(ins, upto=4):
    dummy = np.zeros((4, 2048, DM), np.float32)
    per_core = host_attn_inputs(ins["L0_attn_x"], ins["L0_attn_wq"],
                                ins["L0_attn_wk"], ins["L0_attn_wv"],
                                ins["L0_attn_wo"])
    if upto >= 2:
        mb = host_moe_inputs(dummy, ins["L0_ffn_gate_weight"], ins["L0_ffn_gate_proj"],
                             ins["L0_ffn_up_proj"], ins["L0_ffn_down_proj"],
                             ins["L0_ffn_shared_gate"], ins["L0_ffn_shared_up"],
                             ins["L0_ffn_shared_down"])
        for c in range(NCORE):
            m = dict(mb[c]); m.pop("x1t")
            per_core[c].update(m)
    if upto >= 3:
        mc = host_ssd_inputs(dummy, ins["L1_attn_W_in"], ins["L1_attn_conv_w"],
                             ins["L1_attn_conv_b"], ins["L1_attn_dt_bias"],
                             ins["L1_attn_A_log"], ins["L1_attn_D_param"],
                             ins["L1_attn_norm_w"], ins["L1_attn_W_out"])
        for c in range(NCORE):
            m = dict(mc[c]); m.pop("x2t")
            per_core[c].update(m)
    if upto >= 4:
        md = host_tail_inputs(dummy, ins["L1_ffn_factors"], ins["L1_ffn_bias"])
        for c in range(NCORE):
            m = dict(md[c]); m.pop("x3t")
            per_core[c].update(m)
    return per_core


def kernel(**ins):
    nc = build_fused(upto=4)
    maps = host_fused_inputs(ins, upto=4)
    res = run_bass_kernel_spmd(nc, maps, core_ids=list(range(NCORE)))
    return gather_tail(res.results)


def kernel_multi(**ins):
    nc = build_attn()
    maps = host_attn_inputs(ins["L0_attn_x"], ins["L0_attn_wq"], ins["L0_attn_wk"],
                            ins["L0_attn_wv"], ins["L0_attn_wo"])
    res = run_bass_kernel_spmd(nc, maps, core_ids=list(range(NCORE)))
    x1 = gather_std(res.results)
    nc = build_moe()
    maps = host_moe_inputs(x1, ins["L0_ffn_gate_weight"], ins["L0_ffn_gate_proj"],
                           ins["L0_ffn_up_proj"], ins["L0_ffn_down_proj"],
                           ins["L0_ffn_shared_gate"], ins["L0_ffn_shared_up"],
                           ins["L0_ffn_shared_down"])
    res = run_bass_kernel_spmd(nc, maps, core_ids=list(range(NCORE)))
    x2 = gather_std(res.results)
    nc = build_ssd()
    maps = host_ssd_inputs(x2, ins["L1_attn_W_in"], ins["L1_attn_conv_w"],
                           ins["L1_attn_conv_b"], ins["L1_attn_dt_bias"],
                           ins["L1_attn_A_log"], ins["L1_attn_D_param"],
                           ins["L1_attn_norm_w"], ins["L1_attn_W_out"])
    res = run_bass_kernel_spmd(nc, maps, core_ids=list(range(NCORE)))
    x3 = gather_std(res.results)
    nc = build_tail()
    maps = host_tail_inputs(x3, ins["L1_ffn_factors"], ins["L1_ffn_bias"])
    res = run_bass_kernel_spmd(nc, maps, core_ids=list(range(NCORE)))
    return gather_tail(res.results)


# version 04
# speedup vs baseline: 1.1544x; 1.1511x over previous
import math
from contextlib import ExitStack

import numpy as np
import concourse.bass as bass
import concourse.mybir as mybir
from concourse.bass_utils import run_bass_kernel_spmd

F32 = mybir.dt.float32
BF16 = mybir.dt.bfloat16
AF = mybir.ActivationFunctionType
ALU = mybir.AluOpType
AX = mybir.AxisListType

NCORE = 8
T = 1024
DM = 2048
KC = DM // 128
LN_EPS = 1e-5


def _tile_w(w, tile_cols=128):
    K, M = w.shape
    Kc, Mc = K // 128, M // tile_cols
    return np.ascontiguousarray(w.reshape(Kc, 128, Mc, tile_cols).transpose(2, 1, 0, 3))


def _feat_major(x):
    return np.ascontiguousarray(x.T)


class Sem:
    def __init__(self, ctx, nc, name):
        self.h = ctx.enter_context(nc.semaphore(name))
        self.n = 0

    def inc(self, k=1):
        self.n += k
        return self.n



def emit_ln(nc, st, xT, xcast, nch, tok, post_bf16=True):
    ones_b = st["ones128_bf16"]
    ones_f = st["ones1_f32"]
    rows = st["lnrows"]
    sq = st["lnsq"]
    ab = st["lnab"]
    pool = st["psum_pool"]
    ps = pool[0:1, 0:2 * tok]
    pab = pool[:, 2 * tok:4 * tok].rearrange("p (a b) -> p a b", a=2)
    s_sq, s_fr = st["s_ln_sq"], st["s_ln_fr"]
    nsl = tok // 512
    inv_n = 1.0 / (nch * 128)

    with nc.Block() as blk:
        @blk.vector
        def _(vector):
            for k in range(nch):
                vector.tensor_copy(xcast[:, k, :], xT[:, k, :])

    with nc.Block() as blk:
        @blk.scalar
        def _(scalar):
            for k in range(nch):
                if k >= 2:
                    scalar.wait_ge(s_fr.h, s_fr.n + (k - 1))
                scalar.activation(sq[:, k % 2, :], xcast[:, k, :], AF.Square)
                _flush(scalar, st, "scalar").then_inc(s_sq.h, 1)

        @blk.tensor
        def _(tensor):
            for k in range(nch):
                tensor.wait_ge(s_sq.h, s_sq.n + (k + 1))
                for n in range(nsl):
                    tensor.matmul(ps[0:1, n * 512:(n + 1) * 512],
                                  ones_b[:, 0:1], xcast[:, k, n * 512:(n + 1) * 512],
                                  start=(k == 0), stop=(k == nch - 1))
                for n in range(nsl):
                    mmi = tensor.matmul(ps[0:1, tok + n * 512:tok + (n + 1) * 512],
                                        ones_b[:, 0:1], sq[:, k % 2, n * 512:(n + 1) * 512],
                                        start=(k == 0), stop=(k == nch - 1))
                mmi.then_inc(s_fr.h, 1)
        s_sq.inc(nch)
        s_fr.inc(nch)

    m_v = rows[0:1, 0:tok]
    r_v = rows[0:1, tok:2 * tok]
    t_v = rows[0:1, 2 * tok:3 * tok]
    with nc.Block() as blk:
        @blk.vector
        def _(vector):
            vector.tensor_scalar_mul(m_v, ps[0:1, 0:tok], inv_n)
            vector.tensor_scalar_mul(r_v, ps[0:1, tok:2 * tok], inv_n)
            vector.tensor_mul(t_v, m_v, m_v)
            vector.tensor_sub(r_v, r_v, t_v)
            vector.tensor_scalar_add(r_v, r_v, LN_EPS)
    with nc.Block() as blk:
        @blk.scalar
        def _(scalar):
            scalar.activation(t_v, r_v, AF.Sqrt)
    with nc.Block() as blk:
        @blk.vector
        def _(vector):
            vector.reciprocal(r_v, t_v)
            vector.tensor_mul(t_v, m_v, r_v)

    with nc.Block() as blk:
        @blk.tensor
        def _(tensor):
            for n in range(nsl):
                tensor.matmul(pab[:, 0, n * 512:(n + 1) * 512], ones_f[0:1, :],
                              r_v[0:1, n * 512:(n + 1) * 512], start=True, stop=True)
            for n in range(nsl):
                tensor.matmul(pab[:, 1, n * 512:(n + 1) * 512], ones_f[0:1, :],
                              t_v[0:1, n * 512:(n + 1) * 512], start=True, stop=True)


    with nc.Block() as blk:
        @blk.scalar
        def _(scalar):
            scalar.activation(ab[:, 0, :], pab[:, 0, :], AF.Copy)
            scalar.activation(ab[:, 1, :], pab[:, 1, :], AF.Copy)

    with nc.Block() as blk:
        half = nch // 2

        @blk.vector
        def _(vector):
            for k in range(half):
                vector.tensor_mul(xT[:, k, :], xT[:, k, :], ab[:, 0, :])
                vector.tensor_sub(xT[:, k, :], xT[:, k, :], ab[:, 1, :])
                if post_bf16:
                    vector.tensor_copy(xcast[:, k, :], xT[:, k, :])

        @blk.gpsimd
        def _(gp):
            for k in range(half, nch):
                gp.tensor_mul(xT[:, k, :], xT[:, k, :], ab[:, 0, :])
                gp.tensor_sub(xT[:, k, :], xT[:, k, :], ab[:, 1, :])
                if post_bf16:
                    gp.tensor_copy(xcast[:, k, :], xT[:, k, :])


def alloc_ln_state(nc, ctx, st, tok):
    e = ctx.enter_context
    st["lnrows"] = e(nc.sbuf_tensor("lnrows", [1, 3 * tok], F32))
    st["lnsq"] = e(nc.sbuf_tensor("lnsq", [128, 2, tok], BF16))
    st["lnab"] = e(nc.sbuf_tensor("lnab", [128, 2, tok], F32))
    st["s_ln_sq"] = Sem(ctx, nc, "s_ln_sq")
    st["s_ln_fr"] = Sem(ctx, nc, "s_ln_fr")


def alloc_consts(nc, ctx, st):
    e = ctx.enter_context
    st["ones128_bf16"] = e(nc.sbuf_tensor("ones128b", [128, 1], BF16))
    st["ones1_f32"] = e(nc.sbuf_tensor("ones1f", [1, 128], F32))
    st["ones1_bf16"] = e(nc.sbuf_tensor("ones1fb", [1, 128], BF16))
    st["junk"] = e(nc.sbuf_tensor("junk", [1, 8], F32))


_FLUSH_SLOT = {"scalar": 0, "vector": 1, "gpsimd": 2}


def _flush(E, st, eng):
    sl = _FLUSH_SLOT[eng]
    src = st["ones1_f32"][0:1, 0:1]
    dst = st["junk"][0:1, sl:sl + 1]
    if eng == "scalar":
        return E.copy(dst, src)
    return E.tensor_copy(dst, src)


def emit_consts_init(nc, st):
    with nc.Block() as blk:
        @blk.vector
        def _(vector):
            vector.memset(st["ones128_bf16"][:, :], 1.0)
            vector.memset(st["ones1_f32"][:, :], 1.0)
            vector.memset(st["ones1_bf16"][:, :], 1.0)



def build_tail():
    nc = bass.Bass("TRN2", target_bir_lowering=False)
    with ExitStack() as ctx:
        st = make_base(nc, ctx)
        emit_tail(nc, ctx, st, fused=False)
    return nc


def emit_tail(nc, ctx, st, fused):
    fac_d = nc.dram_tensor("facs", [KC, 128 * KC * 8 * 128], F32, kind="ExternalInput")
    bias_d = nc.dram_tensor("biasc", [128, KC], F32, kind="ExternalInput")
    out_d = nc.dram_tensor("out", [DM, T], F32, kind="ExternalOutput")
    if not fused:
        x3_d = nc.dram_tensor("x3t", [DM, T], F32, kind="ExternalInput")
    isq8 = 1.0 / math.sqrt(8.0)

    with ExitStack() as lctx:
        e = lctx.enter_context
        if not fused:
            _alloc_xT(nc, st)
        xT = st["xT"]
        xb = st["xb"]
        fst = e(nc.sbuf_tensor("d_fst", [128, KC, 4, 128], F32, side="right"))
        wsum = e(nc.sbuf_tensor("d_wsum", [128, KC, 128], F32, side="right"))
        wb16 = e(nc.sbuf_tensor("d_wb16", [128, KC, 128], BF16, side="right"))
        bias_s = e(nc.sbuf_tensor("d_bias_s", [128, KC], F32, side="right"))
        ev = e(nc.sbuf_tensor("d_ev", [128, 2, 512], F32, side="right"))
        psum_pool = st["psum_pool"]
        pm = psum_pool[:, 0:1024].rearrange("p (a b) -> p a b", a=2)
        sx = st["sctx"]
        s_ld = Sem(sx, nc, "d_ld")
        s_fs = Sem(sx, nc, "d_fs")
        s_w = Sem(sx, nc, "d_w")
        s_mm = Sem(sx, nc, "d_mm")
        s_ev = Sem(sx, nc, "d_ev")
        s_res = Sem(sx, nc, "d_res")
        s_st = Sem(sx, nc, "d_st")

        with nc.Block() as blk:
            @blk.sync
            def _(sync):
                if not fused:
                    sync.dma_start(xT[:, :, :],
                                   x3_d[:, :].rearrange("(k p) t -> p k t", p=128)
                                   ).then_inc(s_ld.h, 16)
                    s_ld.inc(16)
                sync.dma_start(bias_s[:, :], bias_d[:, :]).then_inc(s_ld.h, 16)
                sync.wait_ge(s_ld.h, s_ld.inc(16))

        if not fused:
            with nc.Block() as blk:
                @blk.vector
                def _(vector):
                    for k in range(KC):
                        vector.tensor_copy(xb[:, k, :], xT[:, k, :])

        ldb = s_ld.n
        with nc.Block() as blk:
            @blk.sync
            def _(sync, ldb=ldb):
                for m in range(KC):
                    if m >= 1:
                        sync.wait_ge(s_fs.h, 2 * m)
                    sync.dma_start(fst[:, :, :, :],
                                   fac_d[m, :].rearrange("(p k r c) -> p k r c",
                                                         p=128, k=KC, r=8)
                                   [:, :, 0:4, :]).then_inc(s_ld.h, 16)
                    sync.wait_ge(s_fs.h, 2 * m + 1)
                    sync.dma_start(fst[:, :, :, :],
                                   fac_d[m, :].rearrange("(p k r c) -> p k r c",
                                                         p=128, k=KC, r=8)
                                   [:, :, 4:8, :]).then_inc(s_ld.h, 16)
            s_ld.inc(32 * KC)

            @blk.vector
            def _(vector, ldb=ldb):
                for m in range(KC):
                    vector.wait_ge(s_ld.h, ldb + 32 * m + 16)
                    if m >= 1:
                        vector.wait_ge(s_mm.h, 2 * m)
                    vector.tensor_add(wsum[:, :, :], fst[:, :, 0, :], fst[:, :, 1, :])
                    for r in range(2, 4):
                        vector.tensor_add(wsum[:, :, :], wsum[:, :, :], fst[:, :, r, :])
                    _flush(vector, st, "vector").then_inc(s_fs.h, 1)
                    vector.wait_ge(s_ld.h, ldb + 32 * (m + 1))
                    for r in range(4):
                        vector.tensor_add(wsum[:, :, :], wsum[:, :, :], fst[:, :, r, :])
                    vector.tensor_copy(wb16[:, :, :], wsum[:, :, :])
                    _flush(vector, st, "vector").then_inc(s_fs.h, 1)
                    _flush(vector, st, "vector").then_inc(s_w.h, 1)
            s_fs.inc(2 * KC)
            s_w.inc(KC)

            @blk.tensor
            def _(tensor):
                for m in range(KC):
                    tensor.wait_ge(s_w.h, m + 1)
                    if m >= 1:
                        tensor.wait_ge(s_ev.h, 2 * m)
                    for n in range(2):
                        for k in range(KC):
                            mmi = tensor.matmul(pm[:, n, :], wb16[:, k, :],
                                                xb[:, k, n * 512:(n + 1) * 512],
                                                start=(k == 0), stop=(k == KC - 1))
                        mmi.then_inc(s_mm.h, 1)
            s_mm.inc(2 * KC)

            @blk.scalar
            def _(scalar):
                for m in range(KC):
                    if m >= 1:
                        scalar.wait_ge(s_res.h, m)
                    for n in range(2):
                        scalar.wait_ge(s_mm.h, 2 * m + n + 1)
                        scalar.activation(ev[:, n, :], pm[:, n, :], AF.Identity,
                                          bias=bias_s[:, m:m + 1], scale=isq8)
                    _flush(scalar, st, "scalar").then_inc(s_ev.h, 2)
            s_ev.inc(2 * KC)

            @blk.gpsimd
            def _(gp):
                for m in range(KC):
                    gp.wait_ge(s_ev.h, 2 * m + 2)
                    gp.tensor_add(xT[:, m, :], xT[:, m, :],
                                  ev[:, :, :].rearrange("p a b -> p (a b)"))
                    _flush(gp, st, "gpsimd").then_inc(s_res.h, 1)
            s_res.inc(KC)

        emit_ln(nc, st, xT, xb, KC, T, post_bf16=False)

        with nc.Block() as blk:
            @blk.sync
            def _(sync):
                sync.dma_start(out_d[:, :].rearrange("(k p) t -> p k t", p=128),
                               xT[:, :, :]).then_inc(s_st.h, 16)
                sync.wait_ge(s_st.h, s_st.inc(16))


def host_tail_inputs(x3_full, factors, bias):
    fac = np.ascontiguousarray(
        factors.reshape(8, KC, 128, KC, 128).transpose(3, 2, 1, 0, 4)
    ).reshape(KC, -1)
    biasc = np.ascontiguousarray(bias.reshape(KC, 128).T)
    maps = []
    for c in range(NCORE):
        b, h = c // 2, c % 2
        x3t = _feat_major(x3_full[b, h * T:(h + 1) * T, :])
        maps.append(dict(x3t=x3t, facs=fac, biasc=biasc))
    return maps


def gather_tail(results):
    out = np.empty((4, 2048, DM), np.float32)
    for c in range(NCORE):
        b, h = c // 2, c % 2
        out[b, h * T:(h + 1) * T, :] = results[c]["out"].T
    return out



class Plan:
    ENGINES = ("sync", "tensor", "vector", "scalar", "gpsimd")

    def __init__(self):
        self.q = {e: [] for e in self.ENGINES}

    def add(self, eng, fn):
        self.q[eng].append(fn)

    def emit(self, nc):
        with nc.Block() as blk:
            for eng in self.ENGINES:
                fns = self.q[eng]
                if fns:
                    getattr(blk, eng)(lambda E, fns=fns: [f(E) for f in fns])


def make_base(nc, ctx):
    st = {}
    alloc_consts(nc, ctx, st)
    alloc_ln_state(nc, ctx, st, T)
    st["psum_pool"] = ctx.enter_context(nc.psum_tensor("psum_pool", [128, 4096], F32))
    st["xb"] = ctx.enter_context(nc.sbuf_tensor("xbs", [128, KC, T], BF16))
    st["sctx"] = ctx
    emit_consts_init(nc, st)
    return st


def _alloc_xT(nc, st):
    g = nc.sbuf_tensor("xT%d" % st.setdefault("xT_id", 0), [128, KC, T], F32,
                       side="right")
    st["xT_id"] += 1
    st["xT_guard"] = g
    st["xT"] = g.__enter__()
    return st["xT"]


def _free_xT(st):
    st["xT_guard"].__exit__(None, None, None)
    st["xT"] = None



MOE_F = 1408
MOE_FC = MOE_F // 128
NEXP = 10


def build_moe(nexp=NEXP, do_ln=True, dbg=False):
    nc = bass.Bass("TRN2", target_bir_lowering=False)
    with ExitStack() as ctx:
        st = make_base(nc, ctx)
        emit_moe(nc, ctx, st, fused=False, nexp=nexp, do_ln=do_ln, dbg=dbg)
    return nc


def emit_moe(nc, ctx, st, fused, nexp=NEXP, do_ln=True, dbg=False):
    if not fused:
        x1_d = nc.dram_tensor("x1t", [DM, T], F32, kind="ExternalInput")
    gwt_d = nc.dram_tensor("gwt", [128, KC, 8], F32, kind="ExternalInput")
    gates_d = nc.dram_tensor("gates", [NEXP * MOE_FC, 128 * KC * 128], F32, kind="ExternalInput")
    ups_d = nc.dram_tensor("ups", [NEXP * MOE_FC, 128 * KC * 128], F32, kind="ExternalInput")
    downs_d = nc.dram_tensor("downs", [NEXP * KC, 128 * MOE_FC * 128], F32, kind="ExternalInput")
    idn_d = nc.dram_tensor("idn", [128, 128], F32, kind="ExternalInput")
    if not fused:
        out_d = nc.dram_tensor("out", [DM, T], F32, kind="ExternalOutput")
    scr_d = nc.dram_tensor("scr", [NEXP, T], F32, kind="Internal")

    with ExitStack() as lctx:
        e = lctx.enter_context
        if not fused:
            _alloc_xT(nc, st)
        xT = st["xT"]
        xb = st["xb"]
        wstg = e(nc.sbuf_tensor("b_wstg", [128, 2, KC * 128], F32, side="right"))
        wb = e(nc.sbuf_tensor("b_wb", [128, 2, KC * 128], BF16, side="right"))
        sg = e(nc.sbuf_tensor("b_sg", [128, MOE_FC, T], BF16, side="right"))
        uc = e(nc.sbuf_tensor("b_uc", [128, 2, T], BF16, side="right"))
        cbc = e(nc.sbuf_tensor("b_cbc", [128, NEXP, T], BF16, side="right"))
        idn = e(nc.sbuf_tensor("b_idn_s", [128, 128], F32, side="right"))
        gw_s = e(nc.sbuf_tensor("b_gw_s", [128, KC, 8], F32, side="right"))
        gw_b = e(nc.sbuf_tensor("b_gw_b", [128, KC, 8], BF16, side="right"))
        srow = e(nc.sbuf_tensor("b_srow", [8, T], F32, side="right"))
        combT = e(nc.sbuf_tensor("b_combT", [8, T], F32, side="right"))
        crow1 = e(nc.sbuf_tensor("b_crow1", [1, T], F32, side="right"))
        stok = e(nc.sbuf_tensor("b_stok", [128, 8, 8], F32, side="right"))
        ctok = e(nc.sbuf_tensor("b_ctok", [128, 8, 8], F32, side="right"))
        rtree = e(nc.sbuf_tensor("b_rtree", [128, 64], F32, side="right"))
        psum_pool = st["psum_pool"]

        sx = st["sctx"]
        s_ld = Sem(sx, nc, "b_ld")
        s_cast = Sem(sx, nc, "b_cast")
        s_mmg = Sem(sx, nc, "b_mmg")
        s_mmu = Sem(sx, nc, "b_mmu")
        s_mmd = Sem(sx, nc, "b_mmd")
        s_evg = Sem(sx, nc, "b_evg")
        s_evu = Sem(sx, nc, "b_evu")
        s_h = Sem(sx, nc, "b_h")
        s_acc = Sem(sx, nc, "b_acc")
        s_r = Sem(sx, nc, "b_r")
        s_st = Sem(sx, nc, "b_st")

        pg = psum_pool[:, 0:1024].rearrange("p (a b) -> p a b", a=2)
        pu = psum_pool[:, 1024:2048].rearrange("p (a b) -> p a b", a=2)
        pd = psum_pool[:, 2048:3072].rearrange("p (a b) -> p a b", a=2)

        with nc.Block() as blk:
            @blk.sync
            def _(sync):
                if not fused:
                    sync.dma_start(xT[:, :, :],
                                   x1_d[:, :].rearrange("(k p) t -> p k t", p=128)
                                   ).then_inc(s_ld.h, 16)
                    s_ld.inc(16)
                sync.dma_start(gw_s[:, :, :], gwt_d[:, :, :]).then_inc(s_ld.h, 16)
                sync.dma_start(idn[:, :], idn_d[:, :]).then_inc(s_ld.h, 16)
                sync.wait_ge(s_ld.h, s_ld.inc(32))

        with nc.Block() as blk:
            @blk.vector
            def _(vector):
                if not fused:
                    for k in range(KC):
                        vector.tensor_copy(xb[:, k, :], xT[:, k, :])
                vector.tensor_copy(gw_b[:, :, :], gw_s[:, :, :])

        psc = psum_pool[0:8, 1024:2048]
        with nc.Block() as blk:
            @blk.tensor
            def _(tensor):
                for n in range(2):
                    for k in range(KC):
                        mmi = tensor.matmul(psc[:, n * 512:(n + 1) * 512],
                                            gw_b[:, k, :], xb[:, k, n * 512:(n + 1) * 512],
                                            start=(k == 0), stop=(k == KC - 1))
                mmi.then_inc(s_r.h, 1)
            s_r.inc(1)

            @blk.scalar
            def _(scalar):
                scalar.wait_ge(s_r.h, 1)
                scalar.activation(srow[:, :], psc[:, :], AF.Identity)

        ptok = psum_pool[:, 0:64].rearrange("p (a b) -> p a b", a=8)
        with nc.Block() as blk:
            @blk.tensor
            def _(tensor):
                for j in range(8):
                    tensor.transpose(ptok[:, j, :], srow[:, j * 128:(j + 1) * 128],
                                     idn[0:8, 0:8])
            @blk.scalar
            def _(scalar):
                pass

        with nc.Block() as blk:
            @blk.scalar
            def _(scalar):
                for j in range(8):
                    scalar.activation(stok[:, j, :], ptok[:, j, :], AF.Identity)

        def vstep(fn):
            with nc.Block() as blk:
                blk.vector(fn)

        with nc.Block() as blk:
            @blk.scalar
            def _(scalar):
                for j in range(8):
                    scalar.activation(ctok[:, j, :], stok[:, j, :], AF.Exp)

        t1 = rtree[:, 0:32].rearrange("p (a b) -> p a b", a=8)
        t2 = rtree[:, 32:48].rearrange("p (a b) -> p a b", a=8)
        sums = rtree[:, 48:56]
        rs = rtree[:, 56:64]
        cs = ctok[:, :, :]
        ss = stok[:, :, :]
        vstep(lambda v: v.tensor_add(t1[:, :, :], cs[:, :, 0:4], cs[:, :, 4:8]))
        vstep(lambda v: v.tensor_add(t2[:, :, :], t1[:, :, 0:2], t1[:, :, 2:4]))
        vstep(lambda v: v.tensor_add(sums, t2[:, :, 0], t2[:, :, 1]))
        vstep(lambda v: v.reciprocal(rs, sums))
        vstep(lambda v: [v.tensor_scalar(cs[:, j, :], cs[:, j, :], rs[:, j:j + 1],
                                         None, ALU.mult) for j in range(8)])
        vstep(lambda v: v.tensor_max(t1[:, :, :], cs[:, :, 0:4], cs[:, :, 4:8]))
        vstep(lambda v: v.tensor_max(t2[:, :, :], t1[:, :, 0:2], t1[:, :, 2:4]))
        vstep(lambda v: v.tensor_max(sums, t2[:, :, 0], t2[:, :, 1]))
        vstep(lambda v: [v.tensor_scalar(ss[:, j, :], cs[:, j, :], sums[:, j:j + 1],
                                         None, ALU.is_lt) for j in range(8)])
        vstep(lambda v: v.tensor_mul(ss, ss, cs))
        vstep(lambda v: v.tensor_max(t1[:, :, :], ss[:, :, 0:4], ss[:, :, 4:8]))
        vstep(lambda v: v.tensor_max(t2[:, :, :], t1[:, :, 0:2], t1[:, :, 2:4]))
        vstep(lambda v: v.tensor_max(sums, t2[:, :, 0], t2[:, :, 1]))
        vstep(lambda v: [v.tensor_scalar(ss[:, j, :], cs[:, j, :], sums[:, j:j + 1],
                                         None, ALU.is_ge) for j in range(8)])
        vstep(lambda v: v.tensor_mul(cs, cs, ss))

        pct = psum_pool[0:8, 2048:3072].rearrange("p (a b) -> p a b", a=8)
        with nc.Block() as blk:
            @blk.tensor
            def _(tensor):
                for j in range(8):
                    tensor.transpose(pct[:, j, :], ctok[:, j, :], idn[:, :])
        with nc.Block() as blk:
            @blk.scalar
            def _(scalar):
                for j in range(8):
                    scalar.activation(combT[:, j * 128:(j + 1) * 128], pct[:, j, :],
                                      AF.Identity)
        with nc.Block() as blk:
            @blk.sync
            def _(sync):
                sync.dma_start(scr_d[0:8, :], combT[:, :]).then_inc(s_ld.h, 16)
                sync.wait_ge(s_ld.h, s_ld.inc(16))
            @blk.vector
            def _(vector):
                vector.memset(crow1[:, :], 1.0)
        with nc.Block() as blk:
            @blk.sync
            def _(sync):
                sync.dma_start(scr_d[8:9, :], crow1[:, :]).then_inc(s_ld.h, 16)
                sync.dma_start(scr_d[9:10, :], crow1[:, :]).then_inc(s_ld.h, 16)
                sync.wait_ge(s_ld.h, s_ld.inc(32))

        pcb = psum_pool[:, 0:1024]
        s_cbm = Sem(sx, nc, "b_cbm")
        s_cbe = Sem(sx, nc, "b_cbe")
        with nc.Block() as blk:
            @blk.sync
            def _(sync):
                for ex in range(NEXP):
                    if ex >= 1:
                        sync.wait_ge(s_cbm.h, 2 * ex)
                    sync.dma_start(crow1[:, :], scr_d[ex:ex + 1, :]).then_inc(s_ld.h, 16)
            base_ld = s_ld.n
            s_ld.inc(16 * NEXP)

            @blk.tensor
            def _(tensor):
                for ex in range(NEXP):
                    tensor.wait_ge(s_ld.h, base_ld + 16 * (ex + 1))
                    if ex >= 1:
                        tensor.wait_ge(s_cbe.h, ex)
                    for n in range(2):
                        tensor.matmul(pcb[:, n * 512:(n + 1) * 512], st["ones1_f32"][0:1, :],
                                      crow1[0:1, n * 512:(n + 1) * 512],
                                      start=True, stop=True).then_inc(s_cbm.h, 1)
            s_cbm.inc(2 * NEXP)

            @blk.scalar
            def _(scalar):
                for ex in range(NEXP):
                    scalar.wait_ge(s_cbm.h, 2 * (ex + 1))
                    scalar.activation(cbc[:, ex, :], pcb[:, :], AF.Identity)
                    _flush(scalar, st, "scalar").then_inc(s_cbe.h, 1)
            s_cbe.inc(NEXP)

        plan = Plan()
        stamp = {}
        sctr = [0]
        gctr = [0]
        uctr = [0]
        dctr = [0]
        ld_base = s_ld.n

        def emit_strip(dram, row, ncols_k, fam_sem, fam_ctr, psum_view, mm_rhs_fn,
                       evac_fn, evac_eng, evac_sem, pre_mm_waits=(), pre_evac_waits=()):
            s_i = sctr[0]; sctr[0] += 1
            fs = fam_ctr[0]; fam_ctr[0] += 1
            slot = s_i % 2
            ncol = ncols_k * 128

            def dma(sync, s_i=s_i, slot=slot, row=row, ncol=ncol, dram=dram,
                    ncols_k=ncols_k):
                if s_i >= 2:
                    sync.wait_ge(s_cast.h, s_i - 1)
                sync.dma_start(wstg[:, slot, 0:ncol],
                               dram[row, :].rearrange("(p k c) -> p (k c)", p=128,
                                                      k=ncols_k)
                               ).then_inc(s_ld.h, 16)
            plan.add("sync", dma)
            s_ld.inc(16)

            def cast(gp, s_i=s_i, slot=slot, ncol=ncol):
                gp.wait_ge(s_ld.h, ld_base + 16 * (s_i + 1))
                if s_i >= 2:
                    sem, cnt = stamp[s_i - 2]
                    gp.wait_ge(sem.h, cnt)
                gp.tensor_copy(wb[:, slot, 0:ncol], wstg[:, slot, 0:ncol])
                _flush(gp, st, "vector").then_inc(s_cast.h, 1)
            plan.add("vector", cast)
            s_cast.inc(1)

            def mms(tensor, s_i=s_i, slot=slot, ncols_k=ncols_k,
                    waits=tuple(pre_mm_waits)):
                tensor.wait_ge(s_cast.h, s_i + 1)
                for sem, cnt in waits:
                    if cnt > 0:
                        tensor.wait_ge(sem.h, cnt)
                wv = wb[:, slot, 0:ncols_k * 128].rearrange("p (k c) -> p k c",
                                                            k=ncols_k)
                for n in range(2):
                    for k in range(ncols_k):
                        mmi = tensor.matmul(psum_view[:, n, :], wv[:, k, :],
                                            mm_rhs_fn(k, n),
                                            start=(k == 0), stop=(k == ncols_k - 1))
                    mmi.then_inc(fam_sem.h, 1)
            plan.add("tensor", mms)
            fam_sem.inc(2)
            stamp[s_i] = (fam_sem, fam_sem.n)

            def evac(E, fs=fs, waits=tuple(pre_evac_waits)):
                for n in range(2):
                    E.wait_ge(fam_sem.h, 2 * fs + n + 1)
                    for sem, cnt in waits:
                        if cnt > 0:
                            E.wait_ge(sem.h, cnt)
                    evac_fn(E, n)
                if evac_sem is not None:
                    _flush(E, st, evac_eng).then_inc(evac_sem.h, 2)
            plan.add(evac_eng, evac)
            if evac_sem is not None:
                evac_sem.inc(2)

        for ex in range(nexp):
            for m in range(MOE_FC):
                def g_evac_f(m=m):
                    def f(E, n, m=m):
                        E.activation(sg[:, m, n * 512:(n + 1) * 512],
                                     pg[:, n, :], AF.Silu)
                    return f
                emit_strip(gates_d, ex * MOE_FC + m, KC, s_mmg, gctr, pg,
                           lambda k, n: xb[:, k, n * 512:(n + 1) * 512],
                           g_evac_f(), "scalar", s_evg,
                           pre_mm_waits=[(s_evg, 2 * gctr[0])],
                           pre_evac_waits=[(s_mmd, 32 * ex)] if ex >= 1 else [])

                def u_evac_f(ex=ex, m=m):
                    def f(E, n, ex=ex, m=m):
                        E.tensor_mul(uc[:, m % 2, n * 512:(n + 1) * 512],
                                     pu[:, n, :],
                                     cbc[:, ex, n * 512:(n + 1) * 512])
                    return f
                emit_strip(ups_d, ex * MOE_FC + m, KC, s_mmu, uctr, pu,
                           lambda k, n: xb[:, k, n * 512:(n + 1) * 512],
                           u_evac_f(), "vector", None,
                           pre_mm_waits=[(s_h, 11 * ex + m)])

                def hmul(vector, ex=ex, m=m, tg=s_evg.n):
                    vector.wait_ge(s_evg.h, tg)
                    vector.tensor_mul(sg[:, m, :], sg[:, m, :], uc[:, m % 2, :])
                    _flush(vector, st, "vector").then_inc(s_h.h, 1)
                plan.add("vector", hmul)
                s_h.inc(1)

            for m in range(KC):
                def d_evac_f(m=m):
                    def f(E, n, m=m):
                        E.scalar_tensor_tensor(xT[:, m, n * 512:(n + 1) * 512],
                                               pd[:, n, :], 1.0,
                                               xT[:, m, n * 512:(n + 1) * 512],
                                               ALU.mult, ALU.add)
                    return f
                pre_mm = [(s_acc, 2 * dctr[0])]
                if m == 0:
                    pre_mm.append((s_h, 11 * (ex + 1)))
                emit_strip(downs_d, ex * KC + m, MOE_FC, s_mmd, dctr, pd,
                           lambda k, n: sg[:, k, n * 512:(n + 1) * 512],
                           d_evac_f(), "vector", s_acc,
                           pre_mm_waits=pre_mm)

        plan.emit(nc)

        if do_ln:
            emit_ln(nc, st, xT, xb, KC, T, post_bf16=fused)
        if dbg:
            dbg_sc = nc.dram_tensor("dbg_sc", [8, T], F32, kind="ExternalOutput")
            dbg_cb = nc.dram_tensor("dbg_cb", [8, T], F32, kind="ExternalOutput")
            dbg_cbc = nc.dram_tensor("dbg_cbc", [128, NEXP, T], BF16, kind="ExternalOutput")
            dbg_sg = nc.dram_tensor("dbg_sg", [128, MOE_FC, T], BF16, kind="ExternalOutput")
            with nc.Block() as blk:
                @blk.sync
                def _(sync):
                    sync.dma_start(dbg_sc[:, :], srow[:, :]).then_inc(s_st.h, 16)
                    sync.dma_start(dbg_cb[:, :], combT[:, :]).then_inc(s_st.h, 16)
                    sync.dma_start(dbg_cbc[:, :, :], cbc[:, :, :]).then_inc(s_st.h, 16)
                    sync.dma_start(dbg_sg[:, :, :], sg[:, :, :]).then_inc(s_st.h, 16)
                    sync.wait_ge(s_st.h, s_st.inc(64))
        if not fused:
            with nc.Block() as blk:
                @blk.sync
                def _(sync):
                    sync.dma_start(out_d[:, :].rearrange("(k p) t -> p k t", p=128),
                                   xT[:, :, :]).then_inc(s_st.h, 16)
                    sync.wait_ge(s_st.h, s_st.inc(16))


def host_moe_inputs(x1_full, gate_weight, gate_proj, up_proj, down_proj,
                    shared_gate, shared_up, shared_down):
    gs = [_tile_w(gate_proj[e]) for e in range(8)]
    gs += [_tile_w(shared_gate[:, :MOE_F]), _tile_w(shared_gate[:, MOE_F:])]
    us = [_tile_w(up_proj[e]) for e in range(8)]
    us += [_tile_w(shared_up[:, :MOE_F]), _tile_w(shared_up[:, MOE_F:])]
    ds = [_tile_w(down_proj[e]) for e in range(8)]
    ds += [_tile_w(shared_down[:MOE_F, :]), _tile_w(shared_down[MOE_F:, :])]
    gates = np.concatenate(gs, 0).reshape(NEXP * MOE_FC, -1)
    ups = np.concatenate(us, 0).reshape(NEXP * MOE_FC, -1)
    downs = np.concatenate(ds, 0).reshape(NEXP * KC, -1)
    gwt = np.ascontiguousarray(
        gate_weight.T.reshape(KC, 128, 1, 8).transpose(1, 0, 3, 2))[:, :, :, 0]
    idn = np.eye(128, dtype=np.float32)
    maps = []
    for c in range(NCORE):
        b, h = c // 2, c % 2
        x1t = _feat_major(x1_full[b, h * T:(h + 1) * T, :])
        maps.append(dict(x1t=x1t, gwt=gwt, gates=gates, ups=ups, downs=downs,
                         idn=idn))
    return maps


def gather_std(results, key="out"):
    out = np.empty((4, 2048, DM), np.float32)
    for c in range(NCORE):
        b, h = c // 2, c % 2
        out[b, h * T:(h + 1) * T, :] = results[c][key].T
    return out



HKV = 4
NH = 16
SEQ = 2048
SCALE = 1.0 / math.sqrt(128.0)


def build_attn():
    nc = bass.Bass("TRN2", target_bir_lowering=False)
    with ExitStack() as ctx:
        st = make_base(nc, ctx)
        emit_attn(nc, ctx, st, fused=False)
    return nc


def emit_attn(nc, ctx, st, fused):
    xseq_d = nc.dram_tensor("xseq", [DM, SEQ], F32, kind="ExternalInput")
    xown_d = nc.dram_tensor("xown", [DM, T], F32, kind="ExternalInput")
    wq_d = nc.dram_tensor("wq", [KC, 128 * KC * 128], F32, kind="ExternalInput")
    wkt_d = nc.dram_tensor("wkt", [128, HKV * KC * 128], F32, kind="ExternalInput")
    wvt_d = nc.dram_tensor("wvt", [128, KC * 512], F32, kind="ExternalInput")
    wo_d = nc.dram_tensor("wo", [KC, 128 * KC * 128], F32, kind="ExternalInput")
    mask_d = nc.dram_tensor("mask", [9 * 128, 128], BF16, kind="ExternalInput")
    if not fused:
        out_d = nc.dram_tensor("out", [DM, T], F32, kind="ExternalOutput")

    with ExitStack() as lctx:
        e = lctx.enter_context
        xa = st["xb"]
        ones1_b = st["ones1_bf16"]
        psum_pool = st["psum_pool"]
        sx = st["sctx"]
        s_ld = Sem(sx, nc, "a_ld")
        s_cast = Sem(sx, nc, "a_cast")
        s_mm = Sem(sx, nc, "a_mm")
        s_ev = Sem(sx, nc, "a_ev")
        s_pe = Sem(sx, nc, "a_pe")
        s_pm = Sem(sx, nc, "a_pm")
        s_pv = Sem(sx, nc, "a_pv")
        s_tl = Sem(sx, nc, "a_tl")
        s_ao = Sem(sx, nc, "a_ao")
        s_acc = Sem(sx, nc, "a_acc")
        s_st = Sem(sx, nc, "a_st")

        def run_strips(wstg, wb, jobs):
            plan = Plan()
            ld_base = s_ld.n
            cast_base = s_cast.n
            mm_stamp = {}
            for s_i, jb in enumerate(jobs):
                slot = s_i % 2

                def dma(sync, s_i=s_i, slot=slot, jb=jb):
                    if s_i >= 2:
                        sync.wait_ge(s_cast.h, cast_base + s_i - 1)
                    sync.dma_start(wstg[:, slot, :],
                                   jb["dram"][jb["row"], :]
                                   .rearrange("(p x) -> p x", p=128)).then_inc(s_ld.h, 16)
                plan.add("sync", dma)
                s_ld.inc(16)

                def cast(gp, s_i=s_i, slot=slot):
                    gp.wait_ge(s_ld.h, ld_base + 16 * (s_i + 1))
                    if s_i >= 2:
                        gp.wait_ge(s_mm.h, mm_stamp[s_i - 2])
                    gp.tensor_copy(wb[:, slot, :], wstg[:, slot, :])
                    _flush(gp, st, "vector").then_inc(s_cast.h, 1)
                plan.add("vector", cast)
                s_cast.inc(1)

                ev_base = s_ev.n

                def mms(tensor, s_i=s_i, slot=slot, jb=jb, ev_base=ev_base):
                    tensor.wait_ge(s_cast.h, cast_base + s_i + 1)
                    if s_i >= 1:
                        tensor.wait_ge(s_ev.h, ev_base)
                    for sem, cnt in jb.get("pre_mm", ()):
                        if cnt > 0:
                            tensor.wait_ge(sem.h, cnt)
                    wv_ = wb[:, slot, :].rearrange("p (k c) -> p k c", k=KC)
                    for n in range(2):
                        for k in range(KC):
                            mmi = tensor.matmul(jb["psv"][:, n, :], wv_[:, k, :],
                                                jb["rhs_fn"](k, n),
                                                start=(k == 0), stop=(k == KC - 1))
                        mmi.then_inc(s_mm.h, 1)
                plan.add("tensor", mms)
                s_mm.inc(2)
                mm_stamp[s_i] = s_mm.n

                def evac(E, jb=jb, mm_base=s_mm.n - 2):
                    for n in range(2):
                        E.wait_ge(s_mm.h, mm_base + n + 1)
                        jb["evac_fn"](E, n)
                    _flush(E, st, jb.get("evac_eng", "scalar")).then_inc(s_ev.h, 1)
                plan.add(jb.get("evac_eng", "scalar"), evac)
                s_ev.inc(1)
            plan.emit(nc)

        pq = psum_pool[:, 0:1024].rearrange("p (a b) -> p a b", a=2)

        with ExitStack() as actx:
            ea = actx.enter_context
            qT = ea(nc.sbuf_tensor("a_qT", [128, NH, T], BF16, side="right"))
            kT = ea(nc.sbuf_tensor("a_kT", [128, HKV, SEQ], BF16, side="right"))
            vtok = ea(nc.sbuf_tensor("a_vtok", [128, 16, 512], BF16, side="right"))

            with ExitStack() as qctx:
                eq = qctx.enter_context
                wstg = eq(nc.sbuf_tensor("a_wstg", [128, 2, KC * 128], F32, side="right"))
                wb = eq(nc.sbuf_tensor("a_wb", [128, 2, KC * 128], BF16, side="right"))

                for g in range(8):
                    with nc.Block() as blk:
                        @blk.sync
                        def _(sync, g=g):
                            sync.dma_start(
                                wstg[:, :, 0:T].rearrange("p s t -> p s t"),
                                xown_d[:, :].rearrange("(k p) t -> p k t", p=128)
                                [:, 2 * g:2 * g + 2, :]).then_inc(s_ld.h, 16)
                            sync.wait_ge(s_ld.h, s_ld.inc(16))
                    with nc.Block() as blk:
                        @blk.vector
                        def _(vector, g=g):
                            vector.tensor_copy(xa[:, 2 * g:2 * g + 2, :],
                                               wstg[:, :, 0:T])

                jobs = []
                for m in range(NH):
                    def q_ev(m=m):
                        def f(E, n, m=m):
                            E.activation(qT[:, m, n * 512:(n + 1) * 512],
                                         pq[:, n, :], AF.Identity)
                        return f
                    jobs.append(dict(dram=wq_d, row=m, psv=pq,
                                     rhs_fn=lambda k, n: xa[:, k, n * 512:(n + 1) * 512],
                                     evac_fn=q_ev()))
                run_strips(wstg, wb, jobs)

            pk = psum_pool[:, 1024:2048].rearrange("p (a b) -> p a b", a=2)
            pv = psum_pool[:, 2048:3072].rearrange("p (a b) -> p a b", a=2)
            with ExitStack() as kctx:
                ek = kctx.enter_context
                sstg = ek(nc.sbuf_tensor("a_sstg", [128, KC, 512], F32, side="right"))
                xsq = ek(nc.sbuf_tensor("a_xsq", [128, KC, 512], BF16, side="right"))
                wkb = ek(nc.sbuf_tensor("a_wkb", [128, HKV, KC, 128], BF16, side="right"))
                wvb = ek(nc.sbuf_tensor("a_wvb", [128, KC, 512], BF16, side="right"))

                for half in range(2):
                    with nc.Block() as blk:
                        @blk.sync
                        def _(sync, half=half):
                            sync.dma_start(
                                sstg[:, :, :].rearrange("p k c -> p (k c)")[:, 0:4096],
                                wkt_d[:, half * 4096:(half + 1) * 4096]
                                ).then_inc(s_ld.h, 16)
                            sync.wait_ge(s_ld.h, s_ld.inc(16))
                    with nc.Block() as blk:
                        @blk.vector
                        def _(vector, half=half):
                            vector.tensor_copy(
                                wkb[:, 2 * half:2 * half + 2, :, :],
                                sstg[:, :, :].rearrange("p k c -> p (k c)")[:, 0:4096]
                                .rearrange("p (h k c) -> p h k c", h=2, k=KC))
                for half in range(2):
                    with nc.Block() as blk:
                        @blk.sync
                        def _(sync, half=half):
                            sync.dma_start(
                                sstg[:, :, :].rearrange("p k c -> p (k c)")[:, 0:4096],
                                wvt_d[:, half * 4096:(half + 1) * 4096]
                                ).then_inc(s_ld.h, 16)
                            sync.wait_ge(s_ld.h, s_ld.inc(16))
                    with nc.Block() as blk:
                        @blk.vector
                        def _(vector, half=half):
                            vector.tensor_copy(
                                wvb[:, 8 * half:8 * half + 8, :],
                                sstg[:, :, :].rearrange("p k c -> p (k c)")[:, 0:4096]
                                .rearrange("p (k c) -> p k c", k=8))

                for qtr in range(4):
                    with nc.Block() as blk:
                        @blk.sync
                        def _(sync, qtr=qtr):
                            sync.dma_start(
                                sstg[:, :, :],
                                xseq_d[:, :].rearrange("(k p) t -> p k t", p=128)
                                [:, :, qtr * 512:(qtr + 1) * 512]).then_inc(s_ld.h, 16)
                            sync.wait_ge(s_ld.h, s_ld.inc(16))
                    with nc.Block() as blk:
                        @blk.vector
                        def _(vector):
                            vector.tensor_copy(xsq[:, :, :], sstg[:, :, :])
                    mm_base = s_mm.n
                    ev_base = s_ev.n
                    with nc.Block() as blk:
                        @blk.tensor
                        def _(tensor, qtr=qtr, mm_base=mm_base, ev_base=ev_base):
                            for m in range(HKV):
                                if m >= 2:
                                    tensor.wait_ge(s_ev.h, ev_base + m - 1)
                                for k in range(KC):
                                    mmi = tensor.matmul(pk[:, m % 2, :],
                                                        wkb[:, m, k, :],
                                                        xsq[:, k, :],
                                                        start=(k == 0), stop=(k == KC - 1))
                                mmi.then_inc(s_mm.h, 1)
                            for tt in range(4):
                                if tt >= 2:
                                    tensor.wait_ge(s_ev.h, ev_base + HKV + tt - 1)
                                for k in range(KC):
                                    mmi = tensor.matmul(
                                        pv[:, tt % 2, :],
                                        xsq[:, k, tt * 128:(tt + 1) * 128],
                                        wvb[:, k, :], start=(k == 0), stop=(k == KC - 1))
                                mmi.then_inc(s_mm.h, 1)

                        @blk.scalar
                        def _(scalar, qtr=qtr, mm_base=mm_base):
                            for m in range(HKV):
                                scalar.wait_ge(s_mm.h, mm_base + m + 1)
                                scalar.activation(
                                    kT[:, m, qtr * 512:(qtr + 1) * 512],
                                    pk[:, m % 2, :], AF.Identity)
                                _flush(scalar, st, "scalar").then_inc(s_ev.h, 1)
                            for tt in range(4):
                                scalar.wait_ge(s_mm.h, mm_base + HKV + tt + 1)
                                scalar.activation(vtok[:, qtr * 4 + tt, :],
                                                  pv[:, tt % 2, :], AF.Identity)
                                _flush(scalar, st, "scalar").then_inc(s_ev.h, 1)
                    s_mm.inc(HKV + 4)
                    s_ev.inc(HKV + 4)

            negC = ea(nc.sbuf_tensor("a_negC", [1, 4, T], BF16, side="right"))
            sq2 = ea(nc.sbuf_tensor("a_sq2", [128, 2048], BF16, side="right"))
            qrow = ea(nc.sbuf_tensor("a_qrow", [1, 4, T], BF16, side="right"))
            masks = ea(nc.sbuf_tensor("a_masks", [128, 9, 128], BF16, side="right"))
            pbuf = ea(nc.sbuf_tensor("a_pbuf", [128, 4, 512], BF16, side="right"))
            rlb = ea(nc.sbuf_tensor("a_rlb", [128, 512], F32, side="right"))
            rl = ea(nc.sbuf_tensor("a_rl", [1, 512], F32, side="right"))
            kmaxs = ea(nc.sbuf_tensor("a_kmaxs", [1, 8], F32, side="right"))
            negCm = ea(nc.sbuf_tensor("a_negCm", [1, 8], F32, side="right"))
            negCcol = ea(nc.sbuf_tensor("a_negCcol", [128, 8], F32, side="right"))
            with nc.Block() as blk:
                @blk.sync
                def _(sync):
                    sync.dma_start(masks[:, :, :],
                                   mask_d[:, :].rearrange("(r p) c -> p r c", p=128)
                                   ).then_inc(s_ld.h, 16)
                    sync.wait_ge(s_ld.h, s_ld.inc(16))
            def emit_cprime(kvh):
                with nc.Block() as blk:
                    @blk.scalar
                    def _(scalar, kvh=kvh):
                        scalar.activation(sq2[:, :], kT[:, kvh, :], AF.Square)
                with nc.Block() as blk:
                    @blk.tensor
                    def _(tensor):
                        for n in range(4):
                            tensor.matmul(psum_pool[0:1, n * 512:(n + 1) * 512],
                                          st["ones128_bf16"][:, 0:1],
                                          sq2[:, n * 512:(n + 1) * 512],
                                          start=True, stop=True)
                with nc.Block() as blk:
                    @blk.vector
                    def _(vector):
                        vector.tensor_reduce(kmaxs[0:1, 0:1], psum_pool[0:1, 0:2048],
                                             AX.X, ALU.max)
                with nc.Block() as blk:
                    @blk.scalar
                    def _(scalar):
                        scalar.activation(kmaxs[0:1, 1:2], kmaxs[0:1, 0:1], AF.Sqrt)
                with nc.Block() as blk:
                    @blk.scalar
                    def _(scalar, kvh=kvh):
                        for i in range(4):
                            scalar.activation(sq2[:, i * 512:(i + 1) * 512],
                                              qT[:, 4 * kvh + i, 0:512], AF.Square)
                with nc.Block() as blk:
                    @blk.tensor
                    def _(tensor):
                        for i in range(4):
                            tensor.matmul(psum_pool[0:1, i * 512:(i + 1) * 512],
                                          st["ones128_bf16"][:, 0:1],
                                          sq2[:, i * 512:(i + 1) * 512],
                                          start=True, stop=True)
                with nc.Block() as blk:
                    @blk.scalar
                    def _(scalar, kvh=kvh):
                        for i in range(4):
                            scalar.activation(sq2[:, i * 512:(i + 1) * 512],
                                              qT[:, 4 * kvh + i, 512:1024], AF.Square)
                with nc.Block() as blk:
                    @blk.tensor
                    def _(tensor):
                        for i in range(4):
                            tensor.matmul(
                                psum_pool[0:1, 2048 + i * 512:2048 + (i + 1) * 512],
                                st["ones128_bf16"][:, 0:1],
                                sq2[:, i * 512:(i + 1) * 512], start=True, stop=True)
                with nc.Block() as blk:
                    @blk.scalar
                    def _(scalar):
                        for i in range(4):
                            scalar.activation(qrow[0:1, i, 0:512],
                                              psum_pool[0:1, i * 512:(i + 1) * 512],
                                              AF.Sqrt)
                            scalar.activation(
                                qrow[0:1, i, 512:1024],
                                psum_pool[0:1, 2048 + i * 512:2048 + (i + 1) * 512],
                                AF.Sqrt)
                with nc.Block() as blk:
                    @blk.vector
                    def _(vector):
                        for i in range(4):
                            vector.tensor_scalar(negC[0:1, i, :], qrow[0:1, i, :],
                                                 kmaxs[0:1, 1:2], -1.0,
                                                 ALU.mult, ALU.mult)
                with nc.Block() as blk:
                    @blk.vector
                    def _(vector):
                        for j in range(8):
                            vector.tensor_reduce(
                                negCm[0:1, j:j + 1],
                                negC[0:1, :, j * 128:(j + 1) * 128], AX.XY, ALU.min)
                with nc.Block() as blk:
                    @blk.tensor
                    def _(tensor):
                        tensor.matmul(psum_pool[:, 3584:3592], st["ones1_f32"][0:1, :],
                                      negCm[0:1, 0:8], start=True, stop=True)
                with nc.Block() as blk:
                    @blk.scalar
                    def _(scalar):
                        scalar.activation(negCcol[:, :], psum_pool[:, 3584:3592],
                                          AF.Identity, scale=SCALE)

            psS = psum_pool[:, 0:2048].rearrange("p (a b) -> p a b", a=4)
            psL = psum_pool[0:1, 2048:2560]
            psO = psum_pool[:, 2560:3072]
            psRB = psum_pool[:, 3072:3584]
            for kvh in range(HKV):
                emit_cprime(kvh)
                with nc.Block() as blk:
                    tjobs, sjobs, vjobs = [], [], []
                    for j in range(8):
                        nch = j + 9
                        qrhs = qT[:, 4 * kvh:4 * kvh + 4, j * 128:(j + 1) * 128]
                        crhs = negC[0:1, 0:4, j * 128:(j + 1) * 128]
                        mm_base = s_mm.n
                        pe_base = s_pe.n
                        pm_base = s_pm.n
                        pv_base = s_pv.n
                        ao_base = s_ao.n
                        tl_base = s_tl.n

                        def tjob(tensor, kvh=kvh, j=j, nch=nch, qrhs=qrhs, crhs=crhs,
                                 mm_base=mm_base, pe_base=pe_base, pm_base=pm_base,
                                 ao_base=ao_base, tl_base=tl_base):
                            def lpv(c):
                                if c >= j:
                                    tensor.wait_ge(s_pm.h, pm_base + (c - j) + 1)
                                else:
                                    tensor.wait_ge(s_pe.h, pe_base + c + 1)
                                if c == 0 and ao_base > 0:
                                    tensor.wait_ge(s_ao.h, ao_base)
                                tensor.matmul(psL[0:1, :], st["ones128_bf16"][:, 0:1],
                                              pbuf[:, c % 4, :],
                                              start=(c == 0), stop=(c == nch - 1))
                                tensor.matmul(psO[:, :],
                                              vtok[:, c, kvh * 128:(kvh + 1) * 128],
                                              pbuf[:, c % 4, :],
                                              start=(c == 0), stop=(c == nch - 1)
                                              ).then_inc(s_pv.h, 1)
                            for c in range(nch):
                                if c >= 4:
                                    tensor.wait_ge(s_pe.h, pe_base + c - 3)
                                elif c == 0 and pe_base > 0:
                                    tensor.wait_ge(s_pe.h, pe_base)
                                tensor.matmul(psS[:, c % 4, :],
                                              kT[:, kvh, c * 128:(c + 1) * 128],
                                              qrhs, start=True, stop=True
                                              ).then_inc(s_mm.h, 1)
                                if c >= 1:
                                    lpv(c - 1)
                            lpv(nch - 1)
                            tensor.wait_ge(s_tl.h, tl_base + 1)
                            tensor.matmul(psRB[:, :], st["ones1_f32"][0:1, :],
                                          rl[0:1, :], start=True, stop=True
                                          ).then_inc(s_tl.h, 1)
                        tjobs.append(tjob)
                        s_mm.inc(nch)
                        s_pv.inc(nch)

                        def sjob(scalar, j=j, nch=nch, mm_base=mm_base,
                                 pv_base=pv_base, tl_base=tl_base):
                            for c in range(nch):
                                scalar.wait_ge(s_mm.h, mm_base + c + 1)
                                if c >= 4:
                                    scalar.wait_ge(s_pv.h, pv_base + c - 3)
                                scalar.activation(pbuf[:, c % 4, :], psS[:, c % 4, :],
                                                  AF.Exp, scale=SCALE,
                                                  bias=negCcol[:, j:j + 1])
                                _flush(scalar, st, "scalar").then_inc(s_pe.h, 1)
                            scalar.wait_ge(s_tl.h, tl_base + 2)
                            scalar.activation(rlb[:, :], psRB[:, :], AF.Identity)
                            _flush(scalar, st, "scalar").then_inc(s_tl.h, 1)
                        sjobs.append(sjob)
                        s_pe.inc(nch)

                        def vjob(vector, kvh=kvh, j=j, nch=nch, pe_base=pe_base,
                                 pv_base=pv_base, tl_base=tl_base):
                            for c in range(j, nch):
                                vector.wait_ge(s_pe.h, pe_base + c + 1)
                                r = c - j
                                vector.tensor_mul(
                                    pbuf[:, c % 4, :].rearrange("p (a b) -> p a b", a=4),
                                    pbuf[:, c % 4, :].rearrange("p (a b) -> p a b", a=4),
                                    masks[:, r, :].unsqueeze(1).broadcast_to([128, 4, 128]))
                                _flush(vector, st, "vector").then_inc(s_pm.h, 1)
                            vector.wait_ge(s_pv.h, pv_base + nch)
                            vector.reciprocal(rl[0:1, :], psL[0:1, :])
                            _flush(vector, st, "vector").then_inc(s_tl.h, 1)
                            vector.wait_ge(s_tl.h, tl_base + 3)
                            vector.tensor_mul(
                                xa[:, 4 * kvh:4 * kvh + 4, j * 128:(j + 1) * 128],
                                psO[:, :].rearrange("p (a b) -> p a b", a=4),
                                rlb[:, :].rearrange("p (a b) -> p a b", a=4))
                            _flush(vector, st, "vector").then_inc(s_ao.h, 1)
                        vjobs.append(vjob)
                        s_pm.inc(nch - j)
                        s_tl.inc(3)
                        s_ao.inc(1)

                    blk.tensor(lambda E, fns=tjobs: [f(E) for f in fns])
                    blk.scalar(lambda E, fns=sjobs: [f(E) for f in fns])
                    blk.vector(lambda E, fns=vjobs: [f(E) for f in fns])

        _alloc_xT(nc, st)
        with ExitStack() as wctx:
            ew = wctx.enter_context
            xT = st["xT"]
            wstg2 = ew(nc.sbuf_tensor("a_wstg2", [128, 2, KC * 128], F32, side="right"))
            wb2 = ew(nc.sbuf_tensor("a_wb2", [128, 2, KC * 128], BF16, side="right"))

            with nc.Block() as blk:
                @blk.sync
                def _(sync):
                    sync.dma_start(xT[:, :, :],
                                   xown_d[:, :].rearrange("(k p) t -> p k t", p=128)
                                   ).then_inc(s_ld.h, 16)
                    sync.wait_ge(s_ld.h, s_ld.inc(16))

            jobs = []
            for m in range(KC):
                def wo_ev(m=m):
                    def f(E, n, m=m):
                        E.scalar_tensor_tensor(xT[:, m, n * 512:(n + 1) * 512],
                                               pq[:, n, :], 1.0,
                                               xT[:, m, n * 512:(n + 1) * 512],
                                               ALU.mult, ALU.add)
                    return f
                jobs.append(dict(dram=wo_d, row=m, psv=pq,
                                 rhs_fn=lambda k, n: xa[:, k, n * 512:(n + 1) * 512],
                                 evac_fn=wo_ev(), evac_eng="vector"))
            run_strips(wstg2, wb2, jobs)

            emit_ln(nc, st, xT, xa, KC, T, post_bf16=fused)

            if not fused:
                with nc.Block() as blk:
                    @blk.sync
                    def _(sync):
                        sync.dma_start(out_d[:, :].rearrange("(k p) t -> p k t", p=128),
                                       xT[:, :, :]).then_inc(s_st.h, 16)
                        sync.wait_ge(s_st.h, s_st.inc(16))


def host_attn_inputs(x_full, wq, wk, wv, wo):
    wq_t = _tile_w(wq).reshape(KC, -1)
    wo_t = _tile_w(wo).reshape(KC, -1)
    wkt = np.ascontiguousarray(
        wk.reshape(KC, 128, HKV, 128).transpose(1, 2, 0, 3)).reshape(128, -1)
    wvt = np.ascontiguousarray(
        wv.reshape(KC, 128, 512).transpose(1, 0, 2)).reshape(128, -1)
    maps = []
    for c in range(NCORE):
        b, h = c // 2, c % 2
        xseq = _feat_major(x_full[b])
        xown = np.ascontiguousarray(xseq[:, h * T:(h + 1) * T])
        import ml_dtypes
        mask = np.zeros((9, 128, 128), np.float32)
        for r in range(9):
            if r < 8 * h:
                mask[r, :, :] = 1.0
            elif r == 8 * h:
                mask[r] = np.triu(np.ones((128, 128), np.float32))
        mask = mask.astype(ml_dtypes.bfloat16)
        maps.append(dict(xseq=xseq, xown=xown, wq=wq_t, wkt=wkt, wvt=wvt, wo=wo_t,
                         mask=mask.reshape(9 * 128, 128)))
    return maps



DIN = 4096
NHD = 64
HD = 64
DST = 128
CONVD = 4352
XBC_CH = 34
ZB = 8
NCH = 8


def build_ssd(dbg=False):
    nc = bass.Bass("TRN2", target_bir_lowering=False)
    with ExitStack() as ctx:
        st = make_base(nc, ctx)
        emit_ssd(nc, ctx, st, fused=False)
    return nc


def emit_ssd(nc, ctx, st, fused):
    if fused:
        x2_d = nc.dram_tensor("xspill", [DM, T], F32, kind="Internal")
    else:
        x2_d = nc.dram_tensor("x2t", [DM, T], F32, kind="ExternalInput")
    win_d = nc.dram_tensor("win", [67, 128 * KC * 128], F32, kind="ExternalInput")
    winz_d = nc.dram_tensor("winz", [ZB, 128 * KC * 512], F32, kind="ExternalInput")
    wout_d = nc.dram_tensor("wout", [KC, 128 * 32 * 128], F32, kind="ExternalInput")
    cw_d = nc.dram_tensor("cw", [128, XBC_CH * 4], F32, kind="ExternalInput")
    cb_d = nc.dram_tensor("cb", [128, XBC_CH], F32, kind="ExternalInput")
    dtb_d = nc.dram_tensor("dtb", [1, NHD], F32, kind="ExternalInput")
    alog_d = nc.dram_tensor("alog", [1, NHD], F32, kind="ExternalInput")
    dp_d = nc.dram_tensor("dp", [1, DIN], F32, kind="ExternalInput")
    nw_d = nc.dram_tensor("nw", [128, 32], F32, kind="ExternalInput")
    idn_d = nc.dram_tensor("idnf", [128, 128], F32, kind="ExternalInput")
    idnb_d = nc.dram_tensor("idnb", [128, 128], BF16, kind="ExternalInput")
    tri_d = nc.dram_tensor("tri", [128, 128], BF16, kind="ExternalInput")
    flags_d = nc.dram_tensor("flags", [128, 2], F32, kind="ExternalInput")
    if not fused:
        out_d = nc.dram_tensor("out", [DM, T], F32, kind="ExternalOutput")
    zscr = nc.dram_tensor("zscr", [T, DIN], BF16, kind="Internal")
    xscr = nc.dram_tensor("xscr", [DIN, T], BF16, kind="Internal")
    yscr = nc.dram_tensor("yscr", [T, DIN], BF16, kind="Internal")
    acscr = nc.dram_tensor("acscr", [NCH, 2, NHD, 128], F32, kind="Internal")
    prescr = nc.dram_tensor("prescr", [NHD, NCH], F32, kind="Internal")
    sumscr = nc.dram_tensor("sumscr", [NHD, NCH], F32, kind="Internal")
    cc_in = nc.dram_tensor("cc_in", [128, DIN + 2], F32, kind="Internal")
    cc_out = nc.dram_tensor("cc_out", [256, DIN + 2], F32, kind="Internal")
    edscr = nc.dram_tensor("edscr", [128, XBC_CH * 3], F32, kind="Internal")
    edscr2 = nc.dram_tensor("edscr2", [256, XBC_CH * 3], F32, kind="Internal")

    with ExitStack() as lctx:
        e = lctx.enter_context
        ones1_b = st["ones1_bf16"]
        xb = st["xb"]
        psum_pool = st["psum_pool"]
        rows = e(nc.sbuf_tensor("c_rows_c", [1, 4 * NHD], F32))
        nw_cols = e(nc.sbuf_tensor("c_nw_cols", [128, 32], F32))
        flags = e(nc.sbuf_tensor("c_flags_s", [128, 2], F32))
        sumrow = e(nc.sbuf_tensor("c_sumrow", [1, 512], F32))
        prerow = e(nc.sbuf_tensor("c_prerow", [1, 512], F32))
        p2g = []

        def _g(name, shape, dt_):
            g = nc.sbuf_tensor(name, shape, dt_)
            p2g.append(g)
            return g.__enter__()
        idn = _g("c_idn_s", [128, 128], F32)
        idnb = _g("c_idnb_s", [128, 128], BF16)
        tri = _g("c_tri_s", [128, 128], BF16)
        bT = _g("c_bT", [128, T], BF16)
        cT = _g("c_cT", [128, T], BF16)
        dt_tok = _g("c_dt_tok", [128, NCH, NHD], F32)
        acum_tok = _g("c_acum_tok", [128, NCH, NHD], F32)
        xfix_tok = _g("c_xfix_tok", [3, 32 * 128], BF16)

        sx = st["sctx"]
        s_ld = Sem(sx, nc, "c_ld")
        s_cast = Sem(sx, nc, "c_cast")
        s_mm = Sem(sx, nc, "c_mm")
        s_ev = Sem(sx, nc, "c_ev")
        s_cc = Sem(sx, nc, "c_cc")
        s_pm_c = Sem(sx, nc, "c_pm_c")
        s_st = Sem(sx, nc, "c_st")

        if fused:
            with nc.Block() as blk:
                @blk.sync
                def _(sync):
                    sync.dma_start(x2_d[:, :].rearrange("(k p) t -> p k t", p=128),
                                   st["xT"][:, :, :]).then_inc(s_ld.h, 16)
                    sync.wait_ge(s_ld.h, s_ld.inc(16))
            _free_xT(st)

        with nc.Block() as blk:
            @blk.sync
            def _(sync):
                for dst, src in [(idn[:, :], idn_d[:, :]), (idnb[:, :], idnb_d[:, :]),
                                 (tri[:, :], tri_d[:, :]), (nw_cols[:, :], nw_d[:, :]),
                                 (flags[:, :], flags_d[:, :]),
                                 (rows[0:1, 0:NHD], dtb_d[:, :]),
                                 (rows[0:1, NHD:2 * NHD], alog_d[:, :])]:
                    sync.dma_start(dst, src).then_inc(s_ld.h, 16)
                sync.wait_ge(s_ld.h, s_ld.inc(16 * 7))
        if not fused:
            with ExitStack() as xctx:
                xstg = xctx.enter_context(
                    nc.sbuf_tensor("c_xstg", [128, 2, T], F32, side="right"))
                for g in range(8):
                    with nc.Block() as blk:
                        @blk.sync
                        def _(sync, g=g):
                            sync.dma_start(xstg[:, :, :],
                                           x2_d[:, :].rearrange("(k p) t -> p k t",
                                                                p=128)
                                           [:, 2 * g:2 * g + 2, :]).then_inc(s_ld.h, 16)
                            sync.wait_ge(s_ld.h, s_ld.inc(16))
                    with nc.Block() as blk:
                        @blk.vector
                        def _(vector, g=g):
                            vector.tensor_copy(xb[:, 2 * g:2 * g + 2, :],
                                               xstg[:, :, :])
        with ExitStack() as zctx:
            ez = zctx.enter_context
            wstg = ez(nc.sbuf_tensor("c_wstg", [128, 2, KC, 512], F32, side="right"))
            wbz = ez(nc.sbuf_tensor("c_wbz", [128, 2, KC, 512], BF16, side="right"))
            szb = ez(nc.sbuf_tensor("c_szb", [128, 2, 512], BF16, side="right"))
            pz = psum_pool[:, 0:1024].rearrange("p (a b) -> p a b", a=2)
            plan = Plan()
            ld_base = s_ld.n
            cast_base = s_cast.n
            mm_base0 = s_mm.n
            ev_base0 = s_ev.n
            st_base = s_st.n
            mm_stamp = {}
            for zg in range(ZB):
                def dma(sync, zg=zg):
                    if zg >= 2:
                        sync.wait_ge(s_cast.h, cast_base + zg - 1)
                    sync.dma_start(wstg[:, zg % 2, :, :],
                                   winz_d[zg, :].rearrange("(p k c) -> p k c",
                                                           p=128, k=KC)
                                   ).then_inc(s_ld.h, 16)
                plan.add("sync", dma)
                s_ld.inc(16)

                def cast(gp, zg=zg):
                    gp.wait_ge(s_ld.h, ld_base + 16 * (zg + 1))
                    if zg >= 2:
                        gp.wait_ge(s_mm.h, mm_stamp[zg - 2])
                    gp.tensor_copy(wbz[:, zg % 2, :, :], wstg[:, zg % 2, :, :])
                    _flush(gp, st, "vector").then_inc(s_cast.h, 1)
                plan.add("vector", cast)
                s_cast.inc(1)

                def mms(tensor, zg=zg, ev_base=s_ev.n):
                    tensor.wait_ge(s_cast.h, cast_base + zg + 1)
                    for tt in range(8):
                        if not (zg == 0 and tt == 0):
                            tensor.wait_ge(s_ev.h, ev_base0 + (zg * 8 + tt) - 1)
                        for k in range(KC):
                            mmi = tensor.matmul(
                                pz[:, tt % 2, :],
                                xb[:, k, tt * 128:(tt + 1) * 128],
                                wbz[:, zg % 2, k, :],
                                start=(k == 0), stop=(k == KC - 1))
                        mmi.then_inc(s_mm.h, 1)
                plan.add("tensor", mms)
                s_mm.inc(8)
                mm_stamp[zg] = s_mm.n

                def evz(scalar, zg=zg):
                    for tt in range(8):
                        scalar.wait_ge(s_mm.h, mm_base0 + zg * 8 + tt + 1)
                        if zg * 8 + tt >= 2:
                            scalar.wait_ge(s_st.h, st_base + 16 * (zg * 8 + tt - 1))
                        scalar.activation(szb[:, tt % 2, :], pz[:, tt % 2, :], AF.Silu)
                        _flush(scalar, st, "scalar").then_inc(s_ev.h, 1)
                plan.add("scalar", evz)
                s_ev.inc(8)

                def stz(sync, zg=zg):
                    for tt in range(8):
                        sync.wait_ge(s_ev.h, ev_base0 + zg * 8 + tt + 1)
                        sync.dma_start(
                            zscr[tt * 128:(tt + 1) * 128, zg * 512:(zg + 1) * 512],
                            szb[:, tt % 2, :]).then_inc(s_st.h, 16)
                plan.add("sync", stz)
                s_st.inc(16 * 8)
            plan.emit(nc)

        with ExitStack() as pctx:
            ep = pctx.enter_context
            afm = ep(nc.sbuf_tensor("c_afm", [NHD, NCH, 128], F32, side="right"))
            acum_fm = ep(nc.sbuf_tensor("c_acum_fm", [NHD, NCH, 128], F32, side="right"))
            nacum_fm = ep(nc.sbuf_tensor("c_nacum_fm", [NHD, NCH, 128], F32, side="right"))
            scol = ep(nc.sbuf_tensor("c_scol", [NHD, 2, NCH], F32, side="right"))
            edgeb = ep(nc.sbuf_tensor("c_edgeb", [128, XBC_CH * 3], F32, side="right"))
            ownb = ep(nc.sbuf_tensor("c_ownb", [128, XBC_CH * 3], F32, side="right"))
            negAb = ep(nc.sbuf_tensor("c_negAb", [128, NHD], F32, side="right"))
            atok = ep(nc.sbuf_tensor("c_atok", [128, NCH, NHD], F32, side="right"))
            wstg2 = ep(nc.sbuf_tensor("c_wstg2", [128, 2, KC, 128], F32, side="right"))
            wb2 = ep(nc.sbuf_tensor("c_wb2", [128, 2, KC, 128], BF16, side="right"))
            xbct = ep(nc.sbuf_tensor("c_xbct", [128, 2, 1030], BF16, side="right"))
            cvs = ep(nc.sbuf_tensor("c_cvs", [128, 2, 1024], F32, side="right"))
            cwS = ep(nc.sbuf_tensor("c_cwS", [128, XBC_CH * 4], F32, side="right"))
            cbS = ep(nc.sbuf_tensor("c_cbS", [128, XBC_CH], F32, side="right"))
            xsil = ep(nc.sbuf_tensor("c_xsil", [128, 2, 1024], BF16, side="right"))
            sfp = ep(nc.sbuf_tensor("c_sfp", [128, 4, NCH, NHD], F32, side="right"))

            pxbc = psum_pool[:, 0:1024].rearrange("p (a b) -> p a b", a=2)
            pcv = psum_pool[:, 1024:2048].rearrange("p (a b) -> p a b", a=2)
            pdt = psum_pool[:, 2048:2560].rearrange("p (a b) -> p a b", a=NCH)
            ptr = psum_pool[:, 2560:3072]

            with nc.Block() as blk:
                @blk.sync
                def _(sync):
                    sync.dma_start(cwS[:, :], cw_d[:, :]).then_inc(s_ld.h, 16)
                    sync.dma_start(cbS[:, :], cb_d[:, :]).then_inc(s_ld.h, 16)
                    sync.wait_ge(s_ld.h, s_ld.inc(32))
                @blk.vector
                def _(vector):
                    vector.memset(xbct[:, 0, 0:3], 0.0)
                    vector.memset(xbct[:, 1, 0:3], 0.0)

            plan = Plan()
            ld_base = s_ld.n
            cast_base = s_cast.n
            mm_base = s_mm.n
            ev_base = s_ev.n
            st_base = s_st.n
            s_cv = Sem(sx, nc, "c_cv")
            s_cve = Sem(sx, nc, "c_cve")
            s_dg = Sem(sx, nc, "c_dg")
            s_ed = Sem(sx, nc, "c_ed")
            mm_stamp = {}
            for m in range(XBC_CH):
                slot = m % 2

                def dma(sync, m=m, slot=slot):
                    if m >= 2:
                        sync.wait_ge(s_cast.h, cast_base + m - 1)
                    sync.dma_start(wstg2[:, slot, :, :],
                                   win_d[32 + m, :].rearrange("(p k c) -> p k c",
                                                              p=128, k=KC)
                                   ).then_inc(s_ld.h, 16)
                plan.add("sync", dma)
                s_ld.inc(16)

                def cast(gp, m=m, slot=slot):
                    gp.wait_ge(s_ld.h, ld_base + 16 * (m + 1))
                    if m >= 2:
                        gp.wait_ge(s_mm.h, mm_stamp[m - 2])
                    gp.tensor_copy(wb2[:, slot, :, :], wstg2[:, slot, :, :])
                    _flush(gp, st, "vector").then_inc(s_cast.h, 1)
                plan.add("vector", cast)
                s_cast.inc(1)

                def mms(tensor, m=m, slot=slot):
                    tensor.wait_ge(s_cast.h, cast_base + m + 1)
                    if m >= 1:
                        tensor.wait_ge(s_ev.h, ev_base + 2 * m)
                    for n in range(2):
                        for k in range(KC):
                            mmi = tensor.matmul(pxbc[:, n, :], wb2[:, slot, k, :],
                                                xb[:, k, n * 512:(n + 1) * 512],
                                                start=(k == 0), stop=(k == KC - 1))
                        mmi.then_inc(s_mm.h, 1)
                plan.add("tensor", mms)
                s_mm.inc(2)
                mm_stamp[m] = s_mm.n

                def evx(scalar, m=m, slot=slot):
                    for n in range(2):
                        scalar.wait_ge(s_mm.h, mm_base + 2 * m + n + 1)
                        if m >= 2:
                            scalar.wait_ge(s_cv.h, m - 1)
                            scalar.wait_ge(s_ed.h, m - 1)
                        scalar.activation(xbct[:, slot, 3 + n * 512:3 + (n + 1) * 512],
                                          pxbc[:, n, :], AF.Identity)
                    _flush(scalar, st, "scalar").then_inc(s_ev.h, 2)
                plan.add("scalar", evx)
                s_ev.inc(2)

                def edge(vector, m=m, slot=slot, evt=s_ev.n, cvet=s_cve.n):
                    vector.wait_ge(s_ev.h, evt)
                    vector.tensor_copy(edgeb[:, 3 * m:3 * m + 3],
                                       xbct[:, slot, 1024:1027])
                    vector.tensor_copy(ownb[:, 3 * m:3 * m + 3],
                                       xbct[:, slot, 3:6])
                    _flush(vector, st, "vector").then_inc(s_ed.h, 1)
                    if m >= 2:
                        vector.wait_ge(s_cve.h, cvet - 1)
                    vector.tensor_scalar(cvs[:, m % 2, :], xbct[:, slot, 0:1024],
                                         cwS[:, 4 * m:4 * m + 1], None, ALU.mult)
                    for w in range(1, 4):
                        vector.scalar_tensor_tensor(
                            cvs[:, m % 2, :], xbct[:, slot, w:w + 1024],
                            cwS[:, 4 * m + w:4 * m + w + 1], cvs[:, m % 2, :],
                            ALU.mult, ALU.add)
                    _flush(vector, st, "vector").then_inc(s_cv.h, 1)
                plan.add("vector", edge)
                s_ed.inc(1)
                s_cv.inc(1)

                def evc(scalar, m=m, cvt=s_cv.n, stt=s_st.n):
                    scalar.wait_ge(s_cv.h, cvt)
                    if m < 32 and m >= 2:
                        scalar.wait_ge(s_st.h, st_base + 16 * (m - 1))
                    for n in range(2):
                        if m < 32:
                            scalar.activation(xsil[:, m % 2, n * 512:(n + 1) * 512],
                                              cvs[:, m % 2, n * 512:(n + 1) * 512],
                                              AF.Silu, bias=cbS[:, m:m + 1])
                        elif m == 32:
                            scalar.activation(bT[:, n * 512:(n + 1) * 512],
                                              cvs[:, m % 2, n * 512:(n + 1) * 512],
                                              AF.Silu, bias=cbS[:, m:m + 1])
                        else:
                            scalar.activation(cT[:, n * 512:(n + 1) * 512],
                                              cvs[:, m % 2, n * 512:(n + 1) * 512],
                                              AF.Silu, bias=cbS[:, m:m + 1])
                    _flush(scalar, st, "scalar").then_inc(s_cve.h, 1)
                plan.add("scalar", evc)
                s_cve.inc(1)

                if m < 32:
                    def stx(sync, m=m, cvet=s_cve.n):
                        sync.wait_ge(s_cve.h, cvet)
                        sync.dma_start(xscr[m * 128:(m + 1) * 128, :],
                                       xsil[:, m % 2, :]).then_inc(s_st.h, 16)
                    plan.add("sync", stx)
                    s_st.inc(16)
            plan.emit(nc)

            with nc.Block() as blk:
                @blk.sync
                def _(sync):
                    sync.dma_start(wstg2[:, 0, :, :],
                                   win_d[66, :].rearrange("(p k c) -> p k c",
                                                          p=128, k=KC)
                                   ).then_inc(s_ld.h, 16)
                    sync.wait_ge(s_ld.h, s_ld.inc(16))
            with nc.Block() as blk:
                @blk.vector
                def _(vector):
                    vector.tensor_copy(wb2[:, 0, :, :], wstg2[:, 0, :, :])
            with nc.Block() as blk:
                @blk.tensor
                def _(tensor):
                    for tt in range(NCH):
                        for k in range(KC):
                            tensor.matmul(pdt[:, tt, :],
                                          xb[:, k, tt * 128:(tt + 1) * 128],
                                          wb2[:, 0, k, 0:64],
                                          start=(k == 0), stop=False)
                        tensor.matmul(pdt[:, tt, :], st["ones1_f32"][0:1, :],
                                      rows[0:1, 0:NHD],
                                      start=False, stop=True)
            with nc.Block() as blk:
                @blk.scalar
                def _(scalar):
                    for tt in range(NCH):
                        scalar.activation(sfp[:, 0, tt, :], pdt[:, tt, :], AF.Relu)
                        scalar.activation(sfp[:, 1, tt, :], pdt[:, tt, :], AF.Abs)
            with nc.Block() as blk:
                @blk.scalar
                def _(scalar):
                    for tt in range(NCH):
                        scalar.activation(sfp[:, 2, tt, :], sfp[:, 1, tt, :],
                                          AF.Exp, scale=-1.0)
            with nc.Block() as blk:
                @blk.scalar
                def _(scalar):
                    for tt in range(NCH):
                        scalar.activation(sfp[:, 3, tt, :], sfp[:, 2, tt, :],
                                          AF.Ln, bias=1.0)
            with nc.Block() as blk:
                @blk.vector
                def _(vector):
                    for tt in range(NCH):
                        vector.tensor_add(dt_tok[:, tt, :], sfp[:, 0, tt, :],
                                          sfp[:, 3, tt, :])
            with nc.Block() as blk:
                @blk.scalar
                def _(scalar):
                    scalar.activation(rows[0:1, 2 * NHD:3 * NHD],
                                      rows[0:1, NHD:2 * NHD], AF.Exp)
            with nc.Block() as blk:
                @blk.vector
                def _(vector):
                    vector.tensor_scalar_mul(rows[0:1, 3 * NHD:4 * NHD],
                                             rows[0:1, 2 * NHD:3 * NHD], -1.0)
            with nc.Block() as blk:
                @blk.tensor
                def _(tensor):
                    tensor.matmul(ptr[:, 0:64], st["ones1_f32"][0:1, :],
                                  rows[0:1, 3 * NHD:4 * NHD], start=True, stop=True)
            with nc.Block() as blk:
                @blk.scalar
                def _(scalar):
                    scalar.activation(negAb[:, :], ptr[:, 0:64], AF.Identity)
            with nc.Block() as blk:
                @blk.vector
                def _(vector):
                    vector.tensor_mul(atok[:, :, :], dt_tok[:, :, :],
                                      negAb[:, :].unsqueeze(1)
                                      .broadcast_to([128, NCH, NHD]))
            for rnd in range(2):
                with nc.Block() as blk:
                    @blk.tensor
                    def _(tensor, rnd=rnd):
                        for i in range(4):
                            c = rnd * 4 + i
                            tensor.transpose(
                                ptr[0:64, i * 128:(i + 1) * 128]
                                if False else psum_pool[0:64, 2560 + i * 128:2560 + (i + 1) * 128],
                                atok[:, c, :], idn[:, :])
                with nc.Block() as blk:
                    @blk.scalar
                    def _(scalar, rnd=rnd):
                        for i in range(4):
                            c = rnd * 4 + i
                            scalar.activation(afm[:, c, :],
                                              psum_pool[0:64, 2560 + i * 128:2560 + (i + 1) * 128],
                                              AF.Identity)
            with nc.Block() as blk:
                @blk.vector
                def _(vector):
                    for c in range(NCH):
                        vector.tensor_tensor_scan(acum_fm[:, c, :], afm[:, c, :],
                                                  afm[:, c, :], 0.0, ALU.add, ALU.bypass)
            with nc.Block() as blk:
                @blk.vector
                def _(vector):
                    for c in range(NCH):
                        vector.tensor_copy(scol[:, 0, c:c + 1], acum_fm[:, c, 127:128])
            with nc.Block() as blk:
                @blk.sync
                def _(sync):
                    sync.dma_start(sumscr[:, :], scol[:, 0, :]).then_inc(s_ld.h, 16)
                    sync.wait_ge(s_ld.h, s_ld.inc(16))
            with nc.Block() as blk:
                @blk.vector
                def _(vector):
                    vector.tensor_tensor_scan(scol[:, 1, :], scol[:, 0, :],
                                              scol[:, 0, :], 0.0, ALU.add, ALU.bypass)
            with nc.Block() as blk:
                @blk.vector
                def _(vector):
                    vector.memset(scol[:, 0, 0:1], 0.0)
                    vector.tensor_copy(scol[:, 0, 1:NCH], scol[:, 1, 0:NCH - 1])
            for rnd in range(2):
                with nc.Block() as blk:
                    @blk.tensor
                    def _(tensor, rnd=rnd):
                        for i in range(4):
                            c = rnd * 4 + i
                            tensor.transpose(
                                psum_pool[:, 2560 + i * 64:2560 + (i + 1) * 64],
                                acum_fm[:, c, :], idn[0:64, 0:64])
                with nc.Block() as blk:
                    @blk.scalar
                    def _(scalar, rnd=rnd):
                        for i in range(4):
                            c = rnd * 4 + i
                            scalar.activation(acum_tok[:, c, :],
                                              psum_pool[:, 2560 + i * 64:2560 + (i + 1) * 64],
                                              AF.Identity)
            with nc.Block() as blk:
                @blk.sync
                def _(sync):
                    for c in range(NCH):
                        sync.dma_start(acscr[c, 0, :, :], acum_fm[:, c, :]
                                       ).then_inc(s_ld.h, 16)
                    sync.dma_start(prescr[:, :], scol[:, 0, :]).then_inc(s_ld.h, 16)
                    sync.wait_ge(s_ld.h, s_ld.inc(16 * 9))


            edr2 = ep(nc.sbuf_tensor("c_edr2", [128, 2, XBC_CH * 3], F32, side="right"))
            edge_eff = ep(nc.sbuf_tensor("c_edge_eff", [128, XBC_CH * 3], F32, side="right"))
            fxin = ep(nc.sbuf_tensor("c_fxin", [128, XBC_CH, 6], F32, side="right"))
            fxm = ep(nc.sbuf_tensor("c_fxm", [128, 12, XBC_CH], F32, side="right"))
            fxs = ep(nc.sbuf_tensor("c_fxs", [128, 6, XBC_CH], F32, side="right"))
            fxout = ep(nc.sbuf_tensor("c_fxout", [128, 3, XBC_CH], F32, side="right"))
            fxsil = ep(nc.sbuf_tensor("c_fxsil", [128, 3, XBC_CH], BF16, side="right"))
            cwV = cwS[:, :].rearrange("p (m w) -> p m w", m=XBC_CH)

            with nc.Block() as blk:
                @blk.sync
                def _(sync):
                    sync.dma_start(edscr[:, :], edgeb[:, :]).then_inc(s_ld.h, 16)
                    sync.wait_ge(s_ld.h, s_ld.inc(16))
            with nc.Block() as blk:
                @blk.gpsimd
                def _(gp):
                    gp.collective_compute(
                        "AllGather", ALU.bypass,
                        replica_groups=[[0, 1], [2, 3], [4, 5], [6, 7]],
                        ins=[edscr.ap().opt()], outs=[edscr2.ap().opt()],
                    ).then_inc(s_cc.h, 1)
                    gp.wait_ge(s_cc.h, s_cc.inc(1))
            with nc.Block() as blk:
                @blk.sync
                def _(sync):
                    sync.dma_start(edr2[:, 0, :], edscr2[0:128, :]).then_inc(s_ld.h, 16)
                    sync.dma_start(edr2[:, 1, :], edscr2[128:256, :]).then_inc(s_ld.h, 16)
                    sync.wait_ge(s_ld.h, s_ld.inc(32))
            with nc.Block() as blk:
                @blk.vector
                def _(vector):
                    vector.tensor_scalar(edge_eff[:, :], edr2[:, 0, :],
                                         flags[:, 0:1], None, ALU.mult)
                    vector.tensor_scalar(edr2[:, 1, :], edr2[:, 1, :],
                                         flags[:, 1:2], None, ALU.mult)
            with nc.Block() as blk:
                @blk.vector
                def _(vector):
                    vector.tensor_add(edge_eff[:, :], edge_eff[:, :], edr2[:, 1, :])
            with nc.Block() as blk:
                @blk.vector
                def _(vector):
                    vector.tensor_copy(fxin[:, :, 0:3],
                                       edge_eff[:, :].rearrange("p (m w) -> p m w",
                                                                m=XBC_CH))
                    vector.tensor_copy(fxin[:, :, 3:6],
                                       ownb[:, :].rearrange("p (m w) -> p m w",
                                                            m=XBC_CH))
            with nc.Block() as blk:
                @blk.vector
                def _(vector):
                    for t in range(3):
                        for w in range(4):
                            vector.tensor_mul(fxm[:, t * 4 + w, :],
                                              fxin[:, :, t + w], cwV[:, :, w])
            with nc.Block() as blk:
                @blk.vector
                def _(vector):
                    for t in range(3):
                        vector.tensor_add(fxs[:, 2 * t, :], fxm[:, t * 4, :],
                                          fxm[:, t * 4 + 1, :])
                        vector.tensor_add(fxs[:, 2 * t + 1, :], fxm[:, t * 4 + 2, :],
                                          fxm[:, t * 4 + 3, :])
            with nc.Block() as blk:
                @blk.vector
                def _(vector):
                    for t in range(3):
                        vector.tensor_add(fxout[:, t, :], fxs[:, 2 * t, :],
                                          fxs[:, 2 * t + 1, :])
            with nc.Block() as blk:
                @blk.vector
                def _(vector):
                    vector.tensor_add(fxout[:, :, :], fxout[:, :, :],
                                      cbS[:, :].unsqueeze(1)
                                      .broadcast_to([128, 3, XBC_CH]))
            with nc.Block() as blk:
                @blk.scalar
                def _(scalar):
                    scalar.activation(fxsil[:, :, :], fxout[:, :, :], AF.Silu)
            with nc.Block() as blk:
                @blk.vector
                def _(vector):
                    vector.tensor_copy(bT[:, 0:3], fxsil[:, :, 32])
                    vector.tensor_copy(cT[:, 0:3], fxsil[:, :, 33])
            for rnd in range(8):
                with nc.Block() as blk:
                    @blk.tensor
                    def _(tensor, rnd=rnd):
                        for i in range(4):
                            m = rnd * 4 + i
                            tensor.transpose(
                                psum_pool[0:3, 2560 + i * 128:2560 + (i + 1) * 128]
                                .bitcast(BF16)[0:3, 0:128],
                                fxsil[:, :, m], idnb[:, :])
                with nc.Block() as blk:
                    @blk.scalar
                    def _(scalar, rnd=rnd):
                        for i in range(4):
                            m = rnd * 4 + i
                            scalar.activation(
                                xfix_tok[0:3, m * 128:(m + 1) * 128],
                                psum_pool[0:3, 2560 + i * 128:2560 + (i + 1) * 128]
                                .bitcast(BF16)[0:3, 0:128], AF.Identity)

        with ExitStack() as s3:
            e3 = s3.enter_context
            S = e3(nc.sbuf_tensor("c_S", [128, DIN], F32, side="right"))
            Sb = e3(nc.sbuf_tensor("c_Sb", [128, DIN], BF16, side="right"))
            xtok = e3(nc.sbuf_tensor("c_xtok", [128, DIN], BF16, side="right"))
            xd = e3(nc.sbuf_tensor("c_xd", [128, DIN], BF16, side="right"))
            Yb = e3(nc.sbuf_tensor("c_Yb", [128, 2, 512], BF16, side="right"))
            mtall = e3(nc.sbuf_tensor("c_mtall", [128, 8, 8, 128], BF16, side="right"))
            cbm = e3(nc.sbuf_tensor("c_cbm", [128, 128], BF16, side="right"))
            btok = e3(nc.sbuf_tensor("c_btok", [128, 128], BF16, side="right"))
            expDs = e3(nc.sbuf_tensor("c_expDs", [128, 2, 1024], F32, side="right"))
            expDb = e3(nc.sbuf_tensor("c_expDb", [128, 2, 1024], BF16, side="right"))
            grows = e3(nc.sbuf_tensor("c_grows", [1, 2, 2, 1024], F32, side="right"))
            odde = e3(nc.sbuf_tensor("c_odde", [128, 4, NHD], F32, side="right"))
            lrow = e3(nc.sbuf_tensor("c_lrow", [1, NHD], F32, side="right"))
            lrep = e3(nc.sbuf_tensor("c_lrep", [1, DIN], BF16, side="right"))
            ttmp = e3(nc.sbuf_tensor("c_ttmp", [128, 2, 512], BF16, side="right"))
            dpB = e3(nc.sbuf_tensor("c_dpB", [128, DIN], BF16, side="right"))
            with ExitStack() as dctx:
                dpRow = dctx.enter_context(nc.sbuf_tensor("c_dpRow", [1, DIN], F32))
                with nc.Block() as blk:
                    @blk.sync
                    def _(sync):
                        sync.dma_start(dpRow[:, :], dp_d[:, :]).then_inc(s_ld.h, 16)
                        sync.wait_ge(s_ld.h, s_ld.inc(16))
                mmb = s_mm.n
                evb = s_ev.n
                with nc.Block() as blk:
                    @blk.tensor
                    def _(tensor, mmb=mmb, evb=evb):
                        for n in range(8):
                            if n >= 1:
                                tensor.wait_ge(s_ev.h, evb + n)
                            tensor.matmul(psum_pool[:, 0:512], st["ones1_f32"][0:1, :],
                                          dpRow[0:1, n * 512:(n + 1) * 512],
                                          start=True, stop=True).then_inc(s_mm.h, 1)
                    @blk.scalar
                    def _(scalar, mmb=mmb, evb=evb):
                        for n in range(8):
                            scalar.wait_ge(s_mm.h, mmb + n + 1)
                            scalar.activation(dpB[:, n * 512:(n + 1) * 512],
                                              psum_pool[:, 0:512], AF.Identity)
                            _flush(scalar, st, "scalar").then_inc(s_ev.h, 1)
                s_mm.inc(8)
                s_ev.inc(8)


            with nc.Block() as blk:
                @blk.sync
                def _(sync):
                    sync.dma_start(sumrow[0:1, :],
                                   sumscr[:, :].rearrange("a b -> (a b)").unsqueeze(0)
                                   ).then_inc(s_ld.h, 16)
                    sync.wait_ge(s_ld.h, s_ld.inc(16))
                @blk.vector
                def _(vector):
                    vector.memset(S[:, :], 0.0)
                    vector.memset(Sb[:, :], 0.0)

            pD = psum_pool[:, 0:2048].rearrange("p (a b) -> p a b", a=2)
            pYd = psum_pool[:, 2048:3072].rearrange("p (a b) -> p a b", a=2)
            pYo = psum_pool[:, 3072:4096].rearrange("p (a b) -> p a b", a=2)
            pCB = psum_pool[:, 0:128]
            pod = psum_pool[:, 128:192]
            pst = psum_pool[:, 0:1024].rearrange("p (a b) -> p a b", a=2)
            plm = psum_pool[:, 1024:2048].rearrange("p (a b) -> p a b", a=2)

            for c in range(NCH):
                with nc.Block() as blk:
                    @blk.sync
                    def _(sync, c=c):
                        sync.dma_start(xtok[:, :], xscr[:, c * 128:(c + 1) * 128],
                                       transpose=True).then_inc(s_ld.h, 16)
                        sync.wait_ge(s_ld.h, s_ld.inc(16))
                if c == 0:
                    with nc.Block() as blk:
                        @blk.vector
                        def _(vector):
                            vector.tensor_copy(xtok[0:3, :], xfix_tok[0:3, :])
                with nc.Block() as blk:
                    @blk.vector
                    def _(vector, c=c):
                        vector.tensor_mul(
                            xd[:, :].rearrange("p (h d) -> p h d", h=NHD),
                            xtok[:, :].rearrange("p (h d) -> p h d", h=NHD),
                            dt_tok[:, c, :].unsqueeze(2).broadcast_to([128, NHD, HD]))
                    @blk.tensor
                    def _(tensor, c=c):
                        tensor.matmul(pod[:, :], st["ones1_f32"][0:1, :],
                                      sumrow[0:1, :].rearrange("a (h c2) -> a h c2",
                                                               c2=NCH)[:, :, c],
                                      start=True, stop=True)
                with nc.Block() as blk:
                    @blk.scalar
                    def _(scalar, c=c):
                        scalar.activation(odde[:, 0, :], acum_tok[:, c, :], AF.Exp)
                        scalar.activation(lrow[0:1, :],
                                          sumrow[0:1, :].rearrange("a (h c2) -> a h c2",
                                                                   c2=NCH)[:, :, c],
                                          AF.Exp)
                    @blk.vector
                    def _(vector, c=c):
                        vector.tensor_sub(odde[:, 1, :], pod[:, :], acum_tok[:, c, :])
                with nc.Block() as blk:
                    @blk.scalar
                    def _(scalar):
                        scalar.activation(odde[:, 2, :], odde[:, 1, :], AF.Exp)
                    @blk.vector
                    def _(vector):
                        vector.tensor_copy(lrep[0:1, :],
                                           lrow[0:1, :].unsqueeze(2)
                                           .broadcast_to([1, NHD, HD]))
                with nc.Block() as blk:
                    @blk.tensor
                    def _(tensor, c=c):
                        tensor.matmul(pCB[:, :], bT[:, c * 128:(c + 1) * 128],
                                      cT[:, c * 128:(c + 1) * 128],
                                      start=True, stop=True)
                with nc.Block() as blk:
                    @blk.vector
                    def _(vector):
                        vector.tensor_mul(cbm[:, :], pCB[:, :], tri[:, :])
                with nc.Block() as blk:
                    @blk.tensor
                    def _(tensor, c=c):
                        tensor.transpose(
                            psum_pool[:, 192:256].bitcast(BF16)[:, 0:128],
                            bT[:, c * 128:(c + 1) * 128], idnb[:, :])
                with nc.Block() as blk:
                    @blk.scalar
                    def _(scalar):
                        scalar.activation(btok[:, :],
                                          psum_pool[:, 192:256].bitcast(BF16)[:, 0:128],
                                          AF.Identity)
                ldb = s_ld.n
                mmb = s_mm.n
                evb = s_ev.n
                with nc.Block() as blk:
                    @blk.sync
                    def _(sync, c=c, ldb=ldb, mmb=mmb):
                        for g in range(8):
                            if g >= 2:
                                sync.wait_ge(s_mm.h, mmb + g - 1)
                            sync.dma_start(
                                grows[0:1, g % 2, 0, :],
                                acscr[c, 0, :, :].rearrange("a b -> (a b)")
                                [8 * g * 128:(8 * g + 8) * 128].unsqueeze(0)
                                ).then_inc(s_ld.h, 16)
                    s_ld.inc(16 * 8)

                    @blk.tensor
                    def _(tensor, c=c, ldb=ldb, mmb=mmb, evb=evb):
                        for g in range(8):
                            tensor.wait_ge(s_ld.h, ldb + 16 * (g + 1))
                            if g >= 2:
                                tensor.wait_ge(s_ev.h, evb + g - 1)
                            for n in range(2):
                                mmi = tensor.matmul(
                                    pD[:, g % 2, n * 512:(n + 1) * 512],
                                    st["ones1_f32"][0:1, :],
                                    grows[0:1, g % 2, 0, n * 512:(n + 1) * 512],
                                    start=True, stop=True)
                            mmi.then_inc(s_mm.h, 1)
                    s_mm.inc(8)

                    @blk.scalar
                    def _(scalar, c=c, mmb=mmb, evb=evb):
                        for g in range(8):
                            scalar.wait_ge(s_mm.h, mmb + g + 1)
                            for hl in range(8):
                                scalar.activation(
                                    expDs[:, g % 2, hl * 128:(hl + 1) * 128],
                                    pD[:, g % 2, hl * 128:(hl + 1) * 128],
                                    AF.Relu, scale=-1.0,
                                    bias=acum_tok[:, c, 8 * g + hl:8 * g + hl + 1])
                            scalar.activation(expDb[:, g % 2, :], expDs[:, g % 2, :],
                                              AF.Exp, scale=-1.0)
                            _flush(scalar, st, "scalar").then_inc(s_ev.h, 1)
                    s_ev.inc(8)

                    @blk.vector
                    def _(vector, evb=evb):
                        for g in range(8):
                            vector.wait_ge(s_ev.h, evb + g + 1)
                            vector.tensor_mul(
                                mtall[:, g, :, :],
                                expDb[:, g % 2, :].rearrange("p (h l) -> p h l", h=8),
                                cbm[:, :].unsqueeze(1).broadcast_to([128, 8, 128]))
                            _flush(vector, st, "vector").then_inc(s_pm_c.h, 1)
                    s_pm_c.inc(8)

                mmb = s_mm.n
                evb = s_ev.n
                stb = s_st.n
                with nc.Block() as blk:
                    @blk.tensor
                    def _(tensor, c=c, mmb=mmb, evb=evb):
                        for g in range(8):
                            if g >= 2:
                                tensor.wait_ge(s_ev.h, evb + 2 * (g - 1))
                            for hl in range(8):
                                tensor.matmul(pYd[:, g % 2, hl * 64:(hl + 1) * 64],
                                              mtall[:, g, hl, :],
                                              xd[:, (8 * g + hl) * 64:(8 * g + hl + 1) * 64],
                                              start=True, stop=(hl >= 0))
                            mmi = tensor.matmul(pYo[:, g % 2, :],
                                                cT[:, c * 128:(c + 1) * 128],
                                                Sb[:, g * 512:(g + 1) * 512],
                                                start=True, stop=True)
                            mmi.then_inc(s_mm.h, 1)
                    s_mm.inc(8)

                    @blk.vector
                    def _(vector, c=c, mmb=mmb, evb=evb, stb=stb):
                        for g in range(8):
                            vector.wait_ge(s_mm.h, mmb + g + 1)
                            if g >= 2:
                                vector.wait_ge(s_st.h, stb + 16 * (g - 1))
                            odr = odde[:, 0, 8 * g:8 * g + 8]
                            vector.tensor_mul(
                                ttmp[:, g % 2, :].rearrange("p (h d) -> p h d", h=8),
                                pYo[:, g % 2, :].rearrange("p (h d) -> p h d", h=8),
                                odr.unsqueeze(2).broadcast_to([128, 8, 64]))
                            vector.scalar_tensor_tensor(
                                ttmp[:, g % 2, :], pYd[:, g % 2, :], 1.0,
                                ttmp[:, g % 2, :], ALU.mult, ALU.add)
                            vector.tensor_mul(Yb[:, g % 2, :],
                                              dpB[:, g * 512:(g + 1) * 512],
                                              xtok[:, g * 512:(g + 1) * 512])
                            vector.tensor_add(Yb[:, g % 2, :], Yb[:, g % 2, :],
                                              ttmp[:, g % 2, :])
                            _flush(vector, st, "vector").then_inc(s_ev.h, 2)
                    s_ev.inc(16)

                    @blk.sync
                    def _(sync, c=c, evb=evb):
                        for g in range(8):
                            sync.wait_ge(s_ev.h, evb + 2 * (g + 1))
                            sync.dma_start(
                                yscr[c * 128:(c + 1) * 128, g * 512:(g + 1) * 512],
                                Yb[:, g % 2, :]).then_inc(s_st.h, 16)
                    s_st.inc(16 * 8)

                with nc.Block() as blk:
                    @blk.vector
                    def _(vector, c=c):
                        vector.tensor_mul(
                            xd[:, :].rearrange("p (h d) -> p h d", h=NHD),
                            xd[:, :].rearrange("p (h d) -> p h d", h=NHD),
                            odde[:, 2, :].unsqueeze(2).broadcast_to([128, NHD, HD]))
                mmb = s_mm.n
                evb = s_ev.n
                with nc.Block() as blk:
                    @blk.tensor
                    def _(tensor, c=c, mmb=mmb, evb=evb):
                        for g in range(8):
                            if g >= 2:
                                tensor.wait_ge(s_ev.h, evb + 2 * (g - 1))
                            tensor.matmul(pst[:, g % 2, :], btok[:, :],
                                          xd[:, g * 512:(g + 1) * 512],
                                          start=True, stop=True)
                            tensor.matmul(plm[:, g % 2, :], ones1_b[0:1, :],
                                          lrep[0:1, g * 512:(g + 1) * 512],
                                          start=True, stop=True).then_inc(s_mm.h, 1)
                    s_mm.inc(8)

                    @blk.vector
                    def _(vector, mmb=mmb):
                        for g in range(8):
                            vector.wait_ge(s_mm.h, mmb + g + 1)
                            vector.tensor_mul(S[:, g * 512:(g + 1) * 512],
                                              S[:, g * 512:(g + 1) * 512],
                                              plm[:, g % 2, :])
                            vector.tensor_add(S[:, g * 512:(g + 1) * 512],
                                              S[:, g * 512:(g + 1) * 512],
                                              pst[:, g % 2, :])
                            _flush(vector, st, "vector").then_inc(s_ev.h, 2)
                    s_ev.inc(16)
                with nc.Block() as blk:
                    @blk.vector
                    def _(vector):
                        vector.tensor_copy(Sb[:, :], S[:, :])

            with nc.Block() as blk:
                @blk.sync
                def _(sync):
                    sync.dma_start(cc_in[:, 0:DIN], S[:, :]).then_inc(s_ld.h, 16)
                    sync.wait_ge(s_ld.h, s_ld.inc(16))
            with nc.Block() as blk:
                @blk.gpsimd
                def _(gp):
                    gp.collective_compute(
                        "AllGather", ALU.bypass,
                        replica_groups=[[0, 1], [2, 3], [4, 5], [6, 7]],
                        ins=[cc_in.ap().opt()], outs=[cc_out.ap().opt()],
                    ).then_inc(s_cc.h, 1)
                    gp.wait_ge(s_cc.h, s_cc.inc(1))
            with nc.Block() as blk:
                @blk.sync
                def _(sync):
                    sync.dma_start(S[:, :], cc_out[0:128, 0:DIN]).then_inc(s_ld.h, 16)
                    sync.wait_ge(s_ld.h, s_ld.inc(16))
            with nc.Block() as blk:
                @blk.vector
                def _(vector):
                    vector.tensor_scalar(S[:, :], S[:, :], flags[:, 0:1], None, ALU.mult)
            exf = expDs[:, :, :].rearrange("p a b -> p (a b)")
            for half in range(2):
                with nc.Block() as blk:
                    @blk.sync
                    def _(sync, half=half):
                        sync.dma_start(exf[:, :],
                                       cc_out[128:256, half * 2048:(half + 1) * 2048]
                                       ).then_inc(s_ld.h, 16)
                        sync.wait_ge(s_ld.h, s_ld.inc(16))
                with nc.Block() as blk:
                    @blk.vector
                    def _(vector, half=half):
                        vector.scalar_tensor_tensor(
                            S[:, half * 2048:(half + 1) * 2048], exf[:, :],
                            flags[:, 1:2], S[:, half * 2048:(half + 1) * 2048],
                            ALU.mult, ALU.add)
            with nc.Block() as blk:
                @blk.vector
                def _(vector):
                    vector.tensor_copy(Sb[:, :], S[:, :])

            with nc.Block() as blk:
                @blk.sync
                def _(sync):
                    sync.dma_start(prerow[0:1, :],
                                   prescr[:, :].rearrange("a b -> (a b)").unsqueeze(0)
                                   ).then_inc(s_ld.h, 16)
                    sync.wait_ge(s_ld.h, s_ld.inc(16))
            for c in range(NCH):
                with nc.Block() as blk:
                    @blk.sync
                    def _(sync, c=c):
                        sync.dma_start(xtok[:, :],
                                       yscr[c * 128:(c + 1) * 128, :]).then_inc(s_ld.h, 16)
                        sync.dma_start(xd[:, :],
                                       zscr[c * 128:(c + 1) * 128, :]).then_inc(s_ld.h, 16)
                        sync.wait_ge(s_ld.h, s_ld.inc(32))
                    @blk.tensor
                    def _(tensor, c=c):
                        tensor.matmul(pod[:, :], st["ones1_f32"][0:1, :],
                                      prerow[0:1, :].rearrange("a (h c2) -> a h c2",
                                                               c2=NCH)[:, :, c],
                                      start=True, stop=True)
                with nc.Block() as blk:
                    @blk.vector
                    def _(vector, c=c):
                        vector.tensor_add(odde[:, 1, :], pod[:, :], acum_tok[:, c, :])
                with nc.Block() as blk:
                    @blk.scalar
                    def _(scalar):
                        scalar.activation(odde[:, 0, :], odde[:, 1, :], AF.Exp)
                mmb = s_mm.n
                with nc.Block() as blk:
                    @blk.tensor
                    def _(tensor, c=c, mmb=mmb):
                        for g in range(8):
                            if g >= 2:
                                tensor.wait_ge(s_ev.h, s_ev.n + g - 1)
                            tensor.matmul(pYo[:, g % 2, :],
                                          cT[:, c * 128:(c + 1) * 128],
                                          Sb[:, g * 512:(g + 1) * 512],
                                          start=True, stop=True).then_inc(s_mm.h, 1)

                    @blk.vector
                    def _(vector, c=c, mmb=mmb):
                        for g in range(8):
                            vector.wait_ge(s_mm.h, mmb + g + 1)
                            odr = odde[:, 0, 8 * g:8 * g + 8]
                            vector.tensor_mul(
                                ttmp[:, g % 2, :].rearrange("p (h d) -> p h d", h=8),
                                pYo[:, g % 2, :].rearrange("p (h d) -> p h d", h=8),
                                odr.unsqueeze(2).broadcast_to([128, 8, 64]))
                            vector.tensor_add(xtok[:, g * 512:(g + 1) * 512],
                                              xtok[:, g * 512:(g + 1) * 512],
                                              ttmp[:, g % 2, :])
                            _flush(vector, st, "vector").then_inc(s_ev.h, 1)
                    s_mm.inc(8)
                    s_ev.inc(8)
                with nc.Block() as blk:
                    @blk.vector
                    def _(vector):
                        vector.tensor_mul(xtok[:, :], xtok[:, :], xd[:, :])
                with nc.Block() as blk:
                    @blk.scalar
                    def _(scalar):
                        scalar.activation(S[:, :], xtok[:, :], AF.Square,
                                          accum_out=odde[:, 3, 0:1])
                with nc.Block() as blk:
                    @blk.vector
                    def _(vector):
                        vector.tensor_scalar(odde[:, 3, 1:2], odde[:, 3, 0:1],
                                             1.0 / DIN, 1e-5, ALU.mult, ALU.add)
                with nc.Block() as blk:
                    @blk.scalar
                    def _(scalar):
                        scalar.activation(odde[:, 3, 1:2], odde[:, 3, 1:2], AF.Sqrt)
                with nc.Block() as blk:
                    @blk.vector
                    def _(vector):
                        vector.reciprocal(odde[:, 3, 2:3], odde[:, 3, 1:2])
                with nc.Block() as blk:
                    @blk.vector
                    def _(vector):
                        vector.tensor_scalar(xtok[:, :], xtok[:, :], odde[:, 3, 2:3],
                                             None, ALU.mult)
                with nc.Block() as blk:
                    @blk.sync
                    def _(sync, c=c):
                        sync.dma_start(yscr[c * 128:(c + 1) * 128, :], xtok[:, :]
                                       ).then_inc(s_ld.h, 16)
                        sync.wait_ge(s_ld.h, s_ld.inc(16))

        for g_ in reversed(p2g):
            g_.__exit__(None, None, None)
        with ExitStack() as s5:
            e5 = s5.enter_context
            xT = _alloc_xT(nc, st)
            wstg3 = e5(nc.sbuf_tensor("c_wstg3", [128, 1, 32, 128], F32, side="right"))
            wb3 = e5(nc.sbuf_tensor("c_wb3", [128, 1, 32, 128], BF16, side="right"))
            rhsb = e5(nc.sbuf_tensor("c_rhsb", [128, 32, 512], BF16, side="right"))
            with nc.Block() as blk:
                @blk.sync
                def _(sync):
                    sync.dma_start(xT[:, :, :],
                                   x2_d[:, :].rearrange("(k p) t -> p k t", p=128)
                                   ).then_inc(s_ld.h, 16)
                    sync.wait_ge(s_ld.h, s_ld.inc(16))
            nwc = nw_cols
            pwo = psum_pool[:, 0:512]

            for ps in range(2):
                ldb = s_ld.n
                with nc.Block() as blk:
                    @blk.sync
                    def _(sync, ps=ps, ldb=ldb):
                        for k in range(32):
                            sync.dma_start(rhsb[:, k, :],
                                           yscr[ps * 512:(ps + 1) * 512,
                                                k * 128:(k + 1) * 128],
                                           transpose=True).then_inc(s_ld.h, 16)
                        sync.wait_ge(s_ld.h, s_ld.inc(16 * 32))
                with nc.Block() as blk:
                    @blk.vector
                    def _(vector):
                        for k in range(32):
                            vector.tensor_scalar(rhsb[:, k, :], rhsb[:, k, :],
                                                 nwc[:, k:k + 1], None, ALU.mult)
                ldb = s_ld.n
                cb_ = s_cast.n
                mmb = s_mm.n
                evb = s_ev.n
                plan = Plan()
                for m in range(KC):
                    def dma(sync, m=m, ps=ps, ldb=ldb, cb_=cb_, mmb=mmb):
                        if m >= 1:
                            sync.wait_ge(s_cast.h, cb_ + m)
                        sync.dma_start(wstg3[:, 0, :, :],
                                       wout_d[m, :].rearrange("(p k c) -> p k c",
                                                              p=128, k=32)
                                       ).then_inc(s_ld.h, 16)
                    plan.add("sync", dma)
                    s_ld.inc(16)

                    def cast(gp, m=m, ldb=ldb, cb_=cb_, mmb=mmb):
                        gp.wait_ge(s_ld.h, ldb + 16 * (m + 1))
                        if m >= 1:
                            gp.wait_ge(s_mm.h, mmb + m)
                        gp.tensor_copy(wb3[:, 0, :, :], wstg3[:, 0, :, :])
                        _flush(gp, st, "vector").then_inc(s_cast.h, 1)
                    plan.add("vector", cast)
                    s_cast.inc(1)

                    def mms(tensor, m=m, cb_=cb_, evb=evb):
                        tensor.wait_ge(s_cast.h, cb_ + m + 1)
                        if m >= 1:
                            tensor.wait_ge(s_ev.h, evb + m)
                        for k in range(32):
                            mmi = tensor.matmul(pwo[:, :], wb3[:, 0, k, :],
                                                rhsb[:, k, :],
                                                start=(k == 0), stop=(k == 31))
                        mmi.then_inc(s_mm.h, 1)
                    plan.add("tensor", mms)
                    s_mm.inc(1)

                    def evac(vector, m=m, ps=ps, mmb=mmb):
                        vector.wait_ge(s_mm.h, mmb + m + 1)
                        vector.scalar_tensor_tensor(
                            xT[:, m, ps * 512:(ps + 1) * 512], pwo[:, :], 1.0,
                            xT[:, m, ps * 512:(ps + 1) * 512], ALU.mult, ALU.add)
                        _flush(vector, st, "vector").then_inc(s_ev.h, 1)
                    plan.add("vector", evac)
                    s_ev.inc(1)
                plan.emit(nc)

            emit_ln(nc, st, xT, xb, KC, T, post_bf16=fused)
            if not fused:
                with nc.Block() as blk:
                    @blk.sync
                    def _(sync):
                        sync.dma_start(out_d[:, :].rearrange("(k p) t -> p k t",
                                                             p=128),
                                       xT[:, :, :]).then_inc(s_st.h, 16)
                        sync.wait_ge(s_st.h, s_st.inc(16))


def host_ssd_inputs(x2_full, W_in, conv_w, conv_b, dt_bias, A_log, D_param,
                    norm_w, W_out):
    import ml_dtypes
    winp = np.zeros((DM, 8576), np.float32)
    winp[:, :8512] = W_in
    win_t = _tile_w(winp).reshape(67, -1)
    winz = np.ascontiguousarray(
        W_in[:, :4096].reshape(KC, 128, ZB, 512).transpose(2, 1, 0, 3)
    ).reshape(ZB, -1)
    wout_t = _tile_w(W_out).reshape(KC, -1)
    cw = np.ascontiguousarray(conv_w.reshape(XBC_CH, 128, 4).transpose(1, 0, 2)
                              ).reshape(128, -1)
    cb = np.ascontiguousarray(conv_b.reshape(XBC_CH, 128).T)
    nw = np.ascontiguousarray(norm_w.reshape(32, 128).T)
    dp_rep = np.repeat(D_param, HD)[None, :]
    idnf = np.eye(128, dtype=np.float32)
    idnb = np.eye(128, dtype=np.float32).astype(ml_dtypes.bfloat16)
    tri = np.triu(np.ones((128, 128), np.float32)).astype(ml_dtypes.bfloat16)
    maps = []
    for c in range(NCORE):
        b, h = c // 2, c % 2
        x2t = _feat_major(x2_full[b, h * T:(h + 1) * T, :])
        fl = np.zeros((128, 2), np.float32)
        if h == 1:
            fl[:, 0] = 1.0
        maps.append(dict(
            x2t=x2t, win=win_t, winz=winz, wout=wout_t, cw=cw, cb=cb,
            dtb=dt_bias[None, :].astype(np.float32),
            alog=A_log[None, :].astype(np.float32),
            dp=dp_rep.astype(np.float32), nw=nw,
            idnf=idnf, idnb=idnb, tri=tri, flags=fl))
    return maps



def build_fused(upto=4):
    nc = bass.Bass("TRN2", target_bir_lowering=False)
    with ExitStack() as ctx:
        st = make_base(nc, ctx)
        emit_attn(nc, ctx, st, fused=True)
        if upto >= 2:
            emit_moe(nc, ctx, st, fused=True)
        if upto >= 3:
            emit_ssd(nc, ctx, st, fused=True)
        if upto >= 4:
            emit_tail(nc, ctx, st, fused=True)
        else:
            out_d = nc.dram_tensor("out", [DM, T], F32, kind="ExternalOutput")
            s_fin = Sem(st["sctx"], nc, "fin")
            with nc.Block() as blk:
                @blk.sync
                def _(sync):
                    sync.dma_start(out_d[:, :].rearrange("(k p) t -> p k t", p=128),
                                   st["xT"][:, :, :]).then_inc(s_fin.h, 16)
                    sync.wait_ge(s_fin.h, s_fin.inc(16))
    return nc


def host_fused_inputs(ins, upto=4):
    dummy = np.zeros((4, 2048, DM), np.float32)
    per_core = host_attn_inputs(ins["L0_attn_x"], ins["L0_attn_wq"],
                                ins["L0_attn_wk"], ins["L0_attn_wv"],
                                ins["L0_attn_wo"])
    if upto >= 2:
        mb = host_moe_inputs(dummy, ins["L0_ffn_gate_weight"], ins["L0_ffn_gate_proj"],
                             ins["L0_ffn_up_proj"], ins["L0_ffn_down_proj"],
                             ins["L0_ffn_shared_gate"], ins["L0_ffn_shared_up"],
                             ins["L0_ffn_shared_down"])
        for c in range(NCORE):
            m = dict(mb[c]); m.pop("x1t")
            per_core[c].update(m)
    if upto >= 3:
        mc = host_ssd_inputs(dummy, ins["L1_attn_W_in"], ins["L1_attn_conv_w"],
                             ins["L1_attn_conv_b"], ins["L1_attn_dt_bias"],
                             ins["L1_attn_A_log"], ins["L1_attn_D_param"],
                             ins["L1_attn_norm_w"], ins["L1_attn_W_out"])
        for c in range(NCORE):
            m = dict(mc[c]); m.pop("x2t")
            per_core[c].update(m)
    if upto >= 4:
        md = host_tail_inputs(dummy, ins["L1_ffn_factors"], ins["L1_ffn_bias"])
        for c in range(NCORE):
            m = dict(md[c]); m.pop("x3t")
            per_core[c].update(m)
    return per_core


def kernel(**ins):
    nc = build_fused(upto=4)
    maps = host_fused_inputs(ins, upto=4)
    res = run_bass_kernel_spmd(nc, maps, core_ids=list(range(NCORE)))
    return gather_tail(res.results)


def kernel_multi(**ins):
    nc = build_attn()
    maps = host_attn_inputs(ins["L0_attn_x"], ins["L0_attn_wq"], ins["L0_attn_wk"],
                            ins["L0_attn_wv"], ins["L0_attn_wo"])
    res = run_bass_kernel_spmd(nc, maps, core_ids=list(range(NCORE)))
    x1 = gather_std(res.results)
    nc = build_moe()
    maps = host_moe_inputs(x1, ins["L0_ffn_gate_weight"], ins["L0_ffn_gate_proj"],
                           ins["L0_ffn_up_proj"], ins["L0_ffn_down_proj"],
                           ins["L0_ffn_shared_gate"], ins["L0_ffn_shared_up"],
                           ins["L0_ffn_shared_down"])
    res = run_bass_kernel_spmd(nc, maps, core_ids=list(range(NCORE)))
    x2 = gather_std(res.results)
    nc = build_ssd()
    maps = host_ssd_inputs(x2, ins["L1_attn_W_in"], ins["L1_attn_conv_w"],
                           ins["L1_attn_conv_b"], ins["L1_attn_dt_bias"],
                           ins["L1_attn_A_log"], ins["L1_attn_D_param"],
                           ins["L1_attn_norm_w"], ins["L1_attn_W_out"])
    res = run_bass_kernel_spmd(nc, maps, core_ids=list(range(NCORE)))
    x3 = gather_std(res.results)
    nc = build_tail()
    maps = host_tail_inputs(x3, ins["L1_ffn_factors"], ins["L1_ffn_bias"])
    res = run_bass_kernel_spmd(nc, maps, core_ids=list(range(NCORE)))
    return gather_tail(res.results)
